# version 05
# baseline (speedup 1.0000x reference)
import numpy as np
import ml_dtypes
from contextlib import ExitStack

import concourse.bass as bass
import concourse.mybir as mybir
from concourse.bass_utils import run_bass_kernel_spmd

F32 = mybir.dt.float32
BF16 = mybir.dt.bfloat16
AF = mybir.ActivationFunctionType
ALU = mybir.AluOpType
AX = mybir.AxisListType

NCORES = 8
BPC = 4
T = 2048
D = 1024
NT = 8
TC = 512
NTC = T // TC
H = 64
CH = 128
NCH = T // CH
DI = 2048
NDI = 16
DTR = 64
NS = 16
SEG = 256
NSEG = T // SEG
NE0 = 8
H0 = 256
R0 = 64
NE1 = 32
I1 = 512
SHI = 1024

ENGINES = ("tensor", "vector", "scalar", "gpsimd", "sync")
DEBUG_TAPS = ()
PHASES = {"l0a": True, "l0f": True, "ssm": True, "moe": True}
S2_dump = [None]
S2_mktap = [None]


class Ctx:
    def __init__(self, nc, es):
        self.nc = nc
        self.q = {e: [] for e in ENGINES}
        self.sem = {e: es.enter_context(nc.semaphore("sem_" + e)) for e in ENGINES}
        self.debug_taps = ()
        self.auto_drain = False
        self.cnt = {e: 0 for e in ENGINES}

    _uid = [0]

    def uid(self):
        Ctx._uid[0] += 1
        return Ctx._uid[0]

    def op(self, eng, fn):
        self.q[eng].append(fn)

    def sig(self, eng, fn, dma=False):
        inc = 16 if dma else 1
        self.cnt[eng] += inc
        assert self.cnt[eng] < 2 ** 31
        sem = self.sem[eng]

        def g(e, fn=fn, sem=sem, inc=inc):
            fn(e).then_inc(sem, inc)

        self.q[eng].append(g)
        if self.auto_drain and eng in ("vector", "scalar") and not dma:
            self.q[eng].append(lambda e: e.drain())
        return (sem, self.cnt[eng])

    def wait(self, eng, *toks):
        for tk in toks:
            if tk is None:
                continue
            sem, val = tk
            if val > 0:
                self.q[eng].append(lambda e, s=sem, v=val: e.wait_ge(s, v))

    def now(self, eng):
        return (self.sem[eng], self.cnt[eng])

    def barrier(self):
        toks = {e: self.now(e) for e in ENGINES}
        for e in ENGINES:
            self.wait(e, *[toks[o] for o in ENGINES if o != e])
        return toks


def _r(ap, spec, **kw):
    return ap.rearrange(spec, **kw)


def emit_layernorm(ctx, nc, S, pre_toks):
    s_x, s_xn, s_onesf, s_kone, pb = S["x"], S["xn"], S["onesf"], S["kone"], S["pb"]
    with nc.sbuf_tensor("ln_xsq_%d" % ctx.uid(), [128, TC], F32) as s_xsq, \
         nc.sbuf_tensor("ln_st_%d" % ctx.uid(), [1, 4 * TC], F32) as s_st:
        ctx.wait("scalar", *pre_toks)
        ctx.wait("tensor", *pre_toks)
        for c in range(NTC):
            for i in range(NT):
                sl = (i * T + c * TC, i * T + (c + 1) * TC)
                tok_sq = ctx.sig("scalar", lambda s, sl=sl: s.activation(
                    s_xsq[:, :], s_x[:, sl[0]:sl[1]], AF.Square))
                ctx.sig("tensor", lambda t, sl=sl, i=i: t.matmul(
                    pb[0][0:1, 0:TC], s_onesf[:, 0:1], s_x[:, sl[0]:sl[1]],
                    start=(i == 0), stop=(i == NT - 1)))
                ctx.wait("tensor", tok_sq)
                tok_mm2 = ctx.sig("tensor", lambda t, i=i: t.matmul(
                    pb[1][0:1, 0:TC], s_onesf[:, 0:1], s_xsq[:, :],
                    start=(i == 0), stop=(i == NT - 1)))
                ctx.wait("scalar", tok_mm2)
            tok_sums = ctx.now("tensor")
            ctx.wait("vector", tok_sums)
            ctx.sig("vector", lambda v: v.tensor_scalar(
                s_st[0:1, 0:TC], pb[0][0:1, 0:TC], 1.0 / D, None, ALU.mult))
            ctx.sig("vector", lambda v: v.tensor_tensor(
                s_st[0:1, 2 * TC:3 * TC], s_st[0:1, 0:TC], s_st[0:1, 0:TC], ALU.mult))
            tok_m2 = ctx.sig("vector", lambda v: v.scalar_tensor_tensor(
                s_st[0:1, 3 * TC:4 * TC], pb[1][0:1, 0:TC], 1.0 / D,
                s_st[0:1, 2 * TC:3 * TC], ALU.mult, ALU.subtract))
            ctx.wait("scalar", tok_m2)
            ctx.sig("scalar", lambda s: s.activation(
                s_st[0:1, 3 * TC:4 * TC], s_st[0:1, 3 * TC:4 * TC],
                AF.Ln, bias=s_kone[0:1, 1:2]))
            tok_rstd = ctx.sig("scalar", lambda s: s.activation(
                s_st[0:1, TC:2 * TC], s_st[0:1, 3 * TC:4 * TC], AF.Exp, scale=-0.5))
            ctx.wait("tensor", tok_rstd)
            tok_bm = ctx.sig("tensor", lambda t: t.matmul(
                pb[2][:, 0:TC], s_onesf[0:1, :], s_st[0:1, 0:TC], start=True, stop=True))
            tok_br = ctx.sig("tensor", lambda t: t.matmul(
                pb[3][:, 0:TC], s_onesf[0:1, :], s_st[0:1, TC:2 * TC], start=True, stop=True))
            ctx.wait("vector", tok_bm, tok_br)
            for i in range(NT):
                sl = (i * T + c * TC, i * T + (c + 1) * TC)
                ctx.sig("vector", lambda v, sl=sl: v.tensor_tensor(
                    s_xsq[:, :], s_x[:, sl[0]:sl[1]], pb[2][:, 0:TC], ALU.subtract))
                ctx.sig("vector", lambda v, sl=sl: v.tensor_tensor(
                    s_xn[:, sl[0]:sl[1]], s_xsq[:, :], pb[3][:, 0:TC], ALU.mult))
            tok_done = ctx.now("vector")
            ctx.wait("tensor", tok_done)
            ctx.wait("scalar", tok_done)


def emit_l0_attn(ctx, nc, S):
    s_xn, s_oall, s_triu, s_onesb, pb = S["xn"], S["oall"], S["triu"], S["onesb"], S["pb"]
    s_wq, s_wk, s_wv = S["wq"], S["wk"], S["wv"]
    with nc.sbuf_tensor("at_q_%d" % ctx.uid(), [H, CH], BF16) as s_q, \
         nc.sbuf_tensor("at_kc_%d" % ctx.uid(), [CH, H], BF16) as s_kc, \
         nc.sbuf_tensor("at_vca_%d" % ctx.uid(), [CH, H + 1], BF16) as s_vca, \
         nc.sbuf_tensor("at_vt_%d" % ctx.uid(), [H, CH], BF16) as s_vt, \
         nc.sbuf_tensor("at_am_%d" % ctx.uid(), [CH, CH], BF16) as s_am, \
         nc.sbuf_tensor("at_st_%d" % ctx.uid(), [H + 1, H], BF16) as s_stz, \
         nc.sbuf_tensor("at_qz_%d" % ctx.uid(), [H, CH], BF16) as s_qz, \
         nc.sbuf_tensor("at_sc_%d" % ctx.uid(), [128, CH], F32) as s_sc, \
         nc.sbuf_tensor("at_dn_%d" % ctx.uid(), [1, 2 * CH], F32) as s_dn, \
         nc.sbuf_tensor("at_dnb_%d" % ctx.uid(), [1, CH], BF16) as s_dnb, \
         nc.sbuf_tensor("at_of_%d" % ctx.uid(), [H, CH], F32) as s_of:

        def _init(v):
            v.memset(s_stz[:, :], 0.0)
            v.memset(s_vca[:, H:H + 1], 1.0)
            return v.drain()
        ctx.op("vector", _init)

        for c in range(NCH):
            c0 = c * CH

            def phi_emit(src_pb, dst, n):
                tk1 = ctx.sig("vector", lambda v, s=src_pb, n=n: v.tensor_scalar(
                    s_sc[0:n[0], 0:n[1]], s, 0.0, None, ALU.min))
                ctx.wait("scalar", tk1)
                tk2 = ctx.sig("scalar", lambda s, n=n: s.activation(
                    s_sc[0:n[0], 0:n[1]], s_sc[0:n[0], 0:n[1]], AF.Exp))
                ctx.wait("vector", tk2)
                return ctx.sig("vector", lambda v, s=src_pb, d=dst, n=n: v.scalar_tensor_tensor(
                    d, s, 0.0, s_sc[0:n[0], 0:n[1]], ALU.max, ALU.add))

            for i in range(NT):
                xsl = s_xn[:, i * T + c0:i * T + c0 + CH]
                ctx.sig("tensor", lambda t, i=i, xs=xsl: t.matmul(
                    pb[0][0:H, 0:CH], s_wq[:, i * H:(i + 1) * H], xs,
                    start=(i == 0), stop=(i == NT - 1)))
                ctx.sig("tensor", lambda t, i=i, xs=xsl: t.matmul(
                    pb[1][0:H, 0:CH], s_wv[:, i * H:(i + 1) * H], xs,
                    start=(i == 0), stop=(i == NT - 1)))
                ctx.sig("tensor", lambda t, i=i, xs=xsl: t.matmul(
                    pb[2][0:CH, 0:H], xs, s_wk[:, i * H:(i + 1) * H],
                    start=(i == 0), stop=(i == NT - 1)))
                ctx.sig("tensor", lambda t, i=i, xs=xsl: t.matmul(
                    pb[3][0:CH, 0:H], xs, s_wv[:, i * H:(i + 1) * H],
                    start=(i == 0), stop=(i == NT - 1)))
            ctx.wait("vector", ctx.now("tensor"))
            tok_q = phi_emit(pb[0][0:H, 0:CH], s_q[:, :], (H, CH))
            tok_k = phi_emit(pb[2][0:CH, 0:H], s_kc[:, :], (CH, H))
            tok_vt = ctx.sig("vector", lambda v: v.tensor_copy(s_vt[:, :], pb[1][0:H, 0:CH]))
            tok_vc = ctx.sig("vector", lambda v: v.tensor_copy(
                s_vca[:, 0:H], pb[3][0:CH, 0:H]))

            ctx.wait("tensor", tok_q, tok_vt)
            tok_A = ctx.sig("tensor", lambda t: t.matmul(
                pb[4][0:CH, 0:CH], s_vt[:, :], s_q[:, :], start=True, stop=True))
            ctx.wait("vector", tok_A)
            tok_am = ctx.sig("vector", lambda v: v.tensor_tensor(
                s_am[:, :], pb[4][0:CH, 0:CH], s_triu[:, :], ALU.mult))

            ctx.wait("tensor", tok_am, tok_k)
            ctx.sig("tensor", lambda t: t.matmul(
                pb[4][0:H, CH:2 * CH], s_kc[:, :], s_am[:, :], start=True, stop=False))
            tok_O = ctx.sig("tensor", lambda t: t.matmul(
                pb[4][0:H, CH:2 * CH], s_stz[0:H, 0:H], s_q[:, :], start=False, stop=True))

            ctx.sig("tensor", lambda t: t.matmul(
                pb[5][0:H, 0:CH], s_kc[:, :], s_triu[:, :], start=True, stop=False))
            tok_Z = ctx.sig("tensor", lambda t: t.matmul(
                pb[5][0:H, 0:CH], s_stz[H:H + 1, 0:H], s_onesb[H:H + 1, 0:CH],
                start=False, stop=True))

            ctx.wait("tensor", tok_vc)
            tok_su = ctx.sig("tensor", lambda t, c=c: t.matmul(
                pb[6][0:H + 1, 0:H], s_vca[:, :], s_kc[:, :],
                start=(c == 0), stop=(c == NCH - 1)))

            ctx.wait("vector", tok_Z)
            tok_qz = ctx.sig("vector", lambda v: v.tensor_tensor(
                s_qz[:, :], s_q[:, :], pb[5][0:H, 0:CH], ALU.mult))
            ctx.wait("tensor", tok_qz)
            tok_dmm = ctx.sig("tensor", lambda t: t.matmul(
                pb[5][0:1, CH:2 * CH], s_onesb[0:H, 0:1], s_qz[:, :], start=True, stop=True))
            ctx.wait("vector", tok_dmm)
            tok_dn = ctx.sig("vector", lambda v: v.tensor_scalar(
                s_dn[0:1, 0:CH], pb[5][0:1, CH:2 * CH], 1e-6, None, ALU.max))
            ctx.wait("scalar", tok_dn)
            ctx.sig("scalar", lambda s: s.activation(
                s_dn[0:1, CH:2 * CH], s_dn[0:1, 0:CH], AF.Ln))
            tok_rcp = ctx.sig("scalar", lambda s: s.activation(
                s_dn[0:1, 0:CH], s_dn[0:1, CH:2 * CH], AF.Exp, scale=-1.0))
            ctx.wait("vector", tok_rcp)
            tok_dnb = ctx.sig("vector", lambda v: v.tensor_copy(s_dnb[0:1, :], s_dn[0:1, 0:CH]))
            ctx.wait("tensor", tok_dnb)
            tok_rb = ctx.sig("tensor", lambda t: t.matmul(
                pb[5][0:H, CH:2 * CH], s_onesb[0:1, 0:H], s_dnb[0:1, :], start=True, stop=True))
            ctx.wait("vector", tok_O, tok_rb)
            ctx.sig("vector", lambda v: v.tensor_copy(s_of[:, :], pb[4][0:H, CH:2 * CH]))
            ctx.op("vector", lambda v: v.drain())
            tok_of = ctx.sig("vector", lambda v, c0=c0: v.tensor_tensor(
                s_oall[:, c0:c0 + CH], s_of[:, :], pb[5][0:H, CH:2 * CH], ALU.mult))

            ctx.wait("vector", tok_su)
            tok_snap = ctx.sig("vector", lambda v: v.tensor_copy(
                s_stz[:, :], pb[6][0:H + 1, 0:H]))
            ctx.wait("tensor", tok_of, tok_snap, tok_vc, tok_k)


def emit_l0_ffn(ctx, nc, S, i_wd1, i_wd2, i_wad, i_wau, i_wbase):
    s_xn, s_x, s_oall, s_wo, pb = S["xn"], S["x"], S["oall"], S["wo"], S["pb"]
    s_onesb, s_onesf = S["onesb"], S["onesf"]
    s_bd1, s_bd2, s_bad, s_bau, s_bbase = S["bd1"], S["bd2"], S["bad"], S["bau"], S["bbase"]
    with nc.sbuf_tensor("f0_wd1_%d" % ctx.uid(), [128, 2 * NT * H0], BF16) as s_wd1, \
         nc.sbuf_tensor("f0_wd2_%d" % ctx.uid(), [128, 2 * 2 * D], BF16) as s_wd2, \
         nc.sbuf_tensor("f0_h_%d" % ctx.uid(), [128, 2 * T], BF16) as s_h, \
         nc.sbuf_tensor("f0_diff_%d" % ctx.uid(), [128, 3 * TC], F32) as s_diff, \
         nc.sbuf_tensor("f0_loss_%d" % ctx.uid(), [128, NE0 * 33], F32) as s_loss, \
         nc.sbuf_tensor("f0_l8_%d" % ctx.uid(), [128, NE0], F32) as s_l8, \
         nc.sbuf_tensor("f0_lrow_%d" % ctx.uid(), [1, 3 * NE0], F32) as s_lrow, \
         nc.sbuf_tensor("f0_lrb_%d" % ctx.uid(), [1, NE0], BF16) as s_lrb, \
         nc.sbuf_tensor("f0_ind_%d" % ctx.uid(), [128, 2 * NE0 + 1], F32) as s_ind, \
         nc.sbuf_tensor("f0_wad_%d" % ctx.uid(), [128, NE0 * NT * R0], BF16) as s_wad, \
         nc.sbuf_tensor("f0_wau_%d" % ctx.uid(), [R0, NE0 * D], BF16) as s_wau, \
         nc.sbuf_tensor("f0_wadsel_%d" % ctx.uid(), [128, NT * R0], BF16) as s_wadsel, \
         nc.sbuf_tensor("f0_wausel_%d" % ctx.uid(), [R0, D], BF16) as s_wausel, \
         nc.sbuf_tensor("f0_badsel_%d" % ctx.uid(), [R0, 2], F32) as s_badsel, \
         nc.sbuf_tensor("f0_l0b_%d" % ctx.uid(), [128, NT + NE0], F32) as s_l0b, \
         nc.sbuf_tensor("f0_abf_%d" % ctx.uid(), [R0, T], BF16) as s_abf, \
         nc.sbuf_tensor("f0_wbase_%d" % ctx.uid(), [128, NT * D], BF16) as s_wbase:

        ctx.wait("sync", ctx.now("vector"), ctx.now("tensor"), ctx.now("scalar"))
        tok_wd = []
        for e in range(2):
            tok_wd.append(ctx.sig("sync", lambda y, e=e: y.dma_start(
                out=_r(s_wd1[:, e * NT * H0:(e + 1) * NT * H0], "p (c m) -> p c m", c=NT),
                in_=_r(i_wd1[e], "(c p) m -> p c m", p=128)), dma=True))
            tok_wd.append(ctx.sig("sync", lambda y, e=e: y.dma_start(
                out=_r(s_wd2[:, e * 2 * D:(e + 1) * 2 * D], "p (c m) -> p c m", c=2),
                in_=_r(i_wd2[e], "(c p) m -> p c m", p=128)), dma=True))
        tok_wad = ctx.sig("sync", lambda y: y.dma_start(
            out=_r(s_wad[:, :], "p (e c m) -> p e c m", e=NE0, c=NT),
            in_=_r(i_wad, "e (c p) m -> p e c m", p=128)), dma=True)
        tok_wau = ctx.sig("sync", lambda y: y.dma_start(
            out=_r(s_wau[:, :], "p (e m) -> p e m", e=NE0),
            in_=_r(i_wau, "e p m -> p e m")), dma=True)
        tok_wbase = ctx.sig("sync", lambda y: y.dma_start(
            out=_r(s_wbase[:, :], "p (c m) -> p c m", c=NT),
            in_=_r(i_wbase, "(c p) m -> p c m", p=128)), dma=True)

        diff_par = [None, None]
        for e in range(NE0):
            sl = e % 2
            base1 = sl * NT * H0
            base2 = sl * 2 * D
            ctx.wait("tensor", tok_wd[2 * e], tok_wd[2 * e + 1])
            for it in range(2):
                for c in range(NTC):
                    for k in range(NT):
                        ctx.sig("tensor", lambda t, k=k, it=it, c=c, b1=base1: t.matmul(
                            pb[0][:, 0:TC],
                            s_wd1[:, b1 + k * H0 + it * 128: b1 + k * H0 + (it + 1) * 128],
                            s_xn[:, k * T + c * TC:k * T + (c + 1) * TC],
                            start=(k == 0), stop=(k == NT - 1)))
                    tok_h = ctx.now("tensor")
                    ctx.wait("scalar", tok_h)
                    tok_hr = ctx.sig("scalar", lambda s, it=it, c=c, e=e: s.activation(
                        s_h[:, it * T + c * TC:it * T + (c + 1) * TC], pb[0][:, 0:TC],
                        AF.Relu, bias=s_bd1[:, e * 2 + it:e * 2 + it + 1]))
                    ctx.wait("tensor", tok_hr)
            tok_hdone = ctx.now("scalar")
            for dt in range(NT):
                for c in range(NTC):
                    ctx.wait("tensor", tok_hdone)
                    for k2 in range(2):
                        ctx.sig("tensor", lambda t, k2=k2, dt=dt, c=c, b2=base2: t.matmul(
                            pb[1][:, 0:TC],
                            s_wd2[:, b2 + k2 * D + dt * 128: b2 + k2 * D + (dt + 1) * 128],
                            s_h[:, k2 * T + c * TC:k2 * T + (c + 1) * TC],
                            start=(k2 == 0), stop=(k2 == 1)))
                    tok_rc = ctx.now("tensor")
                    slot = (dt * NTC + c) % 2
                    ctx.wait("vector", tok_rc, diff_par[slot])
                    tok_df = ctx.sig("vector", lambda v, dt=dt, c=c, e=e, slot=slot:
                                     v.scalar_tensor_tensor(
                        s_diff[:, slot * TC:(slot + 1) * TC], pb[1][:, 0:TC],
                        s_bd2[:, e * NT + dt:e * NT + dt + 1],
                        s_xn[:, dt * T + c * TC:dt * T + (c + 1) * TC],
                        ALU.add, ALU.subtract))
                    ctx.wait("tensor", tok_df)
                    ctx.wait("scalar", tok_df)
                    diff_par[slot] = ctx.sig("scalar", lambda s, e=e, dt=dt, c=c, slot=slot:
                                             s.activation(
                        s_diff[:, 2 * TC:3 * TC], s_diff[:, slot * TC:(slot + 1) * TC],
                        AF.Square, accum_out=s_loss[:, e * 33 + dt * NTC + c:
                                                    e * 33 + dt * NTC + c + 1]))
            if e + 2 < NE0:
                ctx.wait("sync", ctx.now("tensor"))
                tok_wd.append(ctx.sig("sync", lambda y, e=e, sl=sl: y.dma_start(
                    out=_r(s_wd1[:, sl * NT * H0:(sl + 1) * NT * H0], "p (c m) -> p c m", c=NT),
                    in_=_r(i_wd1[e + 2], "(c p) m -> p c m", p=128)), dma=True))
                tok_wd.append(ctx.sig("sync", lambda y, e=e, sl=sl: y.dma_start(
                    out=_r(s_wd2[:, sl * 2 * D:(sl + 1) * 2 * D], "p (c m) -> p c m", c=2),
                    in_=_r(i_wd2[e + 2], "(c p) m -> p c m", p=128)), dma=True))
            ctx.wait("vector", diff_par[0], diff_par[1])
            ctx.op("vector", lambda v: v.drain())
            ctx.sig("vector", lambda v, e=e: v.tensor_reduce(
                s_l8[:, e:e + 1], s_loss[:, e * 33:e * 33 + 32], AX.X, ALU.add))
        ctx.auto_drain = True
        tok_l8 = ctx.now("vector")
        ctx.wait("tensor", tok_l8)
        tok_lmm = ctx.sig("tensor", lambda t: t.matmul(
            pb[2][0:1, 0:NE0], s_onesf[:, 0:1], s_l8[:, :], start=True, stop=True))
        ctx.wait("vector", tok_lmm)
        tok_lrow = ctx.sig("vector", lambda v: v.tensor_copy(
            s_lrow[0:1, 0:NE0], pb[2][0:1, 0:NE0]))
        ctx.wait("tensor", tok_lrow)
        tok_lbc = ctx.sig("tensor", lambda t: t.matmul(
            pb[3][:, 0:NE0], s_onesf[0:1, :], s_lrow[0:1, 0:NE0], start=True, stop=True))
        ctx.wait("vector", tok_lbc, tok_wad, tok_wau)
        ctx.sig("vector", lambda v: v.tensor_copy(
            s_ind[:, NE0:2 * NE0], pb[3][:, 0:NE0]))
        ctx.op("vector", lambda v: v.drain())
        ctx.sig("vector", lambda v: v.tensor_reduce(
            s_ind[:, 2 * NE0:2 * NE0 + 1], s_ind[:, NE0:2 * NE0], AX.X, ALU.min))
        ctx.op("vector", lambda v: v.drain())
        tok_indbc = ctx.sig("vector", lambda v: v.tensor_scalar(
            s_ind[:, 0:NE0], s_ind[:, NE0:2 * NE0],
            s_ind[:, 2 * NE0:2 * NE0 + 1], None, ALU.is_equal))

        ctx.op("vector", lambda v: v.drain())
        for kc in range(NT):
            ctx.sig("vector", lambda v, kc=kc: v.tensor_scalar(
                s_wadsel[:, kc * R0:(kc + 1) * R0],
                s_wad[:, (0 * NT + kc) * R0:(0 * NT + kc + 1) * R0],
                s_ind[:, 0:1], None, ALU.mult))
            for e in range(1, NE0):
                ctx.op("vector", lambda v: v.drain())
                ctx.sig("vector", lambda v, kc=kc, e=e: v.scalar_tensor_tensor(
                    s_wadsel[:, kc * R0:(kc + 1) * R0],
                    s_wad[:, (e * NT + kc) * R0:(e * NT + kc + 1) * R0],
                    s_ind[:, e:e + 1], s_wadsel[:, kc * R0:(kc + 1) * R0],
                    ALU.mult, ALU.add))
        ctx.sig("vector", lambda v: v.tensor_scalar(
            s_wausel[:, :], s_wau[:, 0:D], s_ind[0:R0, 0:1], None, ALU.mult))
        for e in range(1, NE0):
            ctx.op("vector", lambda v: v.drain())
            ctx.sig("vector", lambda v, e=e: v.scalar_tensor_tensor(
                s_wausel[:, :], s_wau[:, e * D:(e + 1) * D],
                s_ind[0:R0, e:e + 1], s_wausel[:, :], ALU.mult, ALU.add))
        ctx.sig("vector", lambda v: v.tensor_scalar(
            s_badsel[:, 0:1], s_bad[:, 0:1], s_ind[0:R0, 0:1], None, ALU.mult))
        for e in range(1, NE0):
            ctx.op("vector", lambda v: v.drain())
            ctx.sig("vector", lambda v, e=e: v.scalar_tensor_tensor(
                s_badsel[:, 0:1], s_bad[:, e:e + 1], s_ind[0:R0, e:e + 1],
                s_badsel[:, 0:1], ALU.mult, ALU.add))
        for dt in range(NT):
            ctx.sig("vector", lambda v, dt=dt: v.tensor_tensor(
                s_l0b[:, NT:NT + NE0],
                _r(s_bau[:, :], "p (e c) -> p e c", e=NE0)[:, :, dt],
                s_ind[:, 0:NE0], ALU.mult))
            ctx.op("vector", lambda v: v.drain())
            ctx.sig("vector", lambda v, dt=dt: v.tensor_reduce(
                s_l0b[:, dt:dt + 1], s_l0b[:, NT:NT + NE0], AX.X, ALU.add))
            ctx.op("vector", lambda v: v.drain())
            ctx.sig("vector", lambda v, dt=dt: v.tensor_tensor(
                s_l0b[:, dt:dt + 1], s_l0b[:, dt:dt + 1],
                s_bbase[:, dt:dt + 1], ALU.add))
        tok_sel = ctx.now("vector")

        S2_mktap[0]("ffn_dbg", (128, 96))
        S2_mktap[0]("ffn_h7", (128, 2 * T), BF16)
        if "ffn_dbg" in ctx.debug_taps and getattr(ctx, "cur_b", 0) == 0:
            with nc.sbuf_tensor("f0_dbg_%d" % ctx.uid(), [128, 96], F32) as s_dbg_:
                ctx.sig("vector", lambda v: v.tensor_copy(s_dbg_[:, 0:33], s_loss[:, 0 * 33:1 * 33]))
                ctx.sig("vector", lambda v: v.tensor_copy(s_dbg_[:, 33:41], s_l8[:, :]))
                ctx.sig("vector", lambda v: v.tensor_copy(s_dbg_[:, 41:49], s_ind[:, 0:NE0]))
                ctx.sig("vector", lambda v: v.tensor_copy(s_dbg_[:, 49:57], s_l0b[:, 0:8]))
                ctx.sig("vector", lambda v: v.tensor_copy(s_dbg_[0:1, 57:57 + 24], s_lrow[0:1, :]))
                ctx.sig("vector", lambda v: v.tensor_copy(s_dbg_[:, 81:90], s_ind[:, NE0:2 * NE0 + 1]))
                S2_dump[0]("ffn_dbg", lambda: s_dbg_[:, :])
                S2_dump[0]("ffn_h7", lambda: s_h[:, :])
        ctx.wait("tensor", tok_sel)
        for c in range(NTC):
            for k in range(NT):
                ctx.sig("tensor", lambda t, k=k, c=c: t.matmul(
                    pb[3][0:R0, 0:TC], s_wadsel[:, k * R0:(k + 1) * R0],
                    s_xn[:, k * T + c * TC:k * T + (c + 1) * TC],
                    start=(k == 0), stop=(k == NT - 1)))
            tok_a = ctx.now("tensor")
            ctx.wait("scalar", tok_a)
            tok_ar = ctx.sig("scalar", lambda s, c=c: s.activation(
                s_abf[:, c * TC:(c + 1) * TC], pb[3][0:R0, 0:TC],
                AF.Relu, bias=s_badsel[:, 0:1]))
            ctx.wait("tensor", tok_ar)

        ctx.wait("tensor", tok_wbase)
        evac = [None, None]
        for dt in range(NT):
            for c in range(NTC):
                bnk = (dt * NTC + c) % 2
                p = pb[4 + bnk]
                ctx.wait("tensor", evac[bnk])
                ctx.sig("tensor", lambda t, dt=dt, c=c, p=p: t.matmul(
                    p[:, 0:TC], s_wo[:, dt * 128:(dt + 1) * 128],
                    s_oall[:, c * TC:(c + 1) * TC], start=True, stop=False))
                for k in range(NT):
                    ctx.sig("tensor", lambda t, k=k, dt=dt, c=c, p=p: t.matmul(
                        p[:, 0:TC], s_wbase[:, k * D + dt * 128:k * D + (dt + 1) * 128],
                        s_xn[:, k * T + c * TC:k * T + (c + 1) * TC],
                        start=False, stop=False))
                tok_mm = ctx.sig("tensor", lambda t, dt=dt, c=c, p=p: t.matmul(
                    p[:, 0:TC], s_wausel[:, dt * 128:(dt + 1) * 128],
                    s_abf[:, c * TC:(c + 1) * TC], start=False, stop=True))
                ctx.wait("vector", tok_mm)
                evac[bnk] = ctx.sig("vector", lambda v, dt=dt, c=c, p=p:
                                    v.scalar_tensor_tensor(
                    s_x[:, dt * T + c * TC:dt * T + (c + 1) * TC], p[:, 0:TC],
                    s_l0b[:, dt:dt + 1],
                    s_x[:, dt * T + c * TC:dt * T + (c + 1) * TC], ALU.add, ALU.add))
        ctx.wait("tensor", evac[0], evac[1])


def emit_l1_ssm(ctx, nc, S, i_winT, i_wxT, i_wdtT, i_woutT):
    s_xn, s_x, pb = S["xn"], S["x"], S["pb"]
    s_binc, s_bxc, s_bdtc, s_A, s_dc, s_boutc = (S["binc"], S["bxc"], S["bdtc"],
                                                 S["A"], S["dc"], S["boutc"])
    s_hstate, s_kone, s_onesb = S["hstate"], S["kone"], S["onesb"]
    es2 = ExitStack()
    s_win = es2.enter_context(nc.sbuf_tensor("sm_win_%d" % ctx.uid(), [128, NT * 128], BF16))
    s_wxT = es2.enter_context(nc.sbuf_tensor("sm_wxT_%d" % ctx.uid(), [128, NDI * 128], BF16))
    s_wdtT = es2.enter_context(nc.sbuf_tensor("sm_wdtT_%d" % ctx.uid(), [DTR, DI], BF16))
    s_wout = es2.enter_context(nc.sbuf_tensor("sm_wout_%d" % ctx.uid(), [128, NDI * 128], BF16))
    s_xs = es2.enter_context(nc.sbuf_tensor("sm_xs_%d" % ctx.uid(), [128, NDI * SEG], BF16))
    s_z = es2.enter_context(nc.sbuf_tensor("sm_z_%d" % ctx.uid(), [128, NDI * SEG], BF16))
    s_dl = es2.enter_context(nc.sbuf_tensor("sm_dl_%d" % ctx.uid(), [128, NDI * SEG], BF16))
    s_dx = es2.enter_context(nc.sbuf_tensor("sm_dx_%d" % ctx.uid(), [128, NDI * SEG], BF16))
    s_y = es2.enter_context(nc.sbuf_tensor("sm_y_%d" % ctx.uid(), [128, NDI * SEG], F32))
    s_draw = es2.enter_context(nc.sbuf_tensor("sm_draw_%d" % ctx.uid(), [DTR, SEG], BF16))
    s_bc = es2.enter_context(nc.sbuf_tensor("sm_bc_%d" % ctx.uid(), [R0, SEG], F32))
    s_bcb = es2.enter_context(nc.sbuf_tensor("sm_bcb_%d" % ctx.uid(), [R0, SEG], BF16))
    s_flat = es2.enter_context(nc.sbuf_tensor("sm_flat_%d" % ctx.uid(), [65, NS * SEG], BF16))
    s_ab = es2.enter_context(nc.sbuf_tensor("sm_ab_%d" % ctx.uid(), [128, 2 * SEG], F32))
    s_abig = es2.enter_context(nc.sbuf_tensor("sm_abig_%d" % ctx.uid(), [128, 8 * SEG], F32))
    s_bbig = es2.enter_context(nc.sbuf_tensor("sm_bbig_%d" % ctx.uid(), [128, 8 * SEG], BF16))
    s_hbig = es2.enter_context(nc.sbuf_tensor("sm_hbig_%d" % ctx.uid(), [128, 8 * SEG], F32))
    s_g1 = es2.enter_context(nc.sbuf_tensor("sm_g1_%d" % ctx.uid(), [128, SEG], F32))
    s_Bs = es2.enter_context(nc.sbuf_tensor("sm_Bs_%d" % ctx.uid(), [128, SEG], BF16))
    s_Cs = es2.enter_context(nc.sbuf_tensor("sm_Cs_%d" % ctx.uid(), [128, SEG], BF16))
    if True:
        s_yg = s_abig[:, :].bitcast(BF16)[:, 0:NDI * SEG]
        s_fix = s_g1

        ctx.wait("sync", ctx.now("vector"), ctx.now("tensor"), ctx.now("scalar"))
        tok_wx = ctx.sig("sync", lambda y: y.dma_start(
            out=_r(s_wxT[:, :], "p (c m) -> p c m", c=NDI),
            in_=_r(i_wxT, "(c p) m -> p c m", p=128)), dma=True)
        tok_wdt = ctx.sig("sync", lambda y: y.dma_start(
            out=s_wdtT[:, :], in_=i_wdtT), dma=True)
        def _h0(v):
            v.memset(s_hstate[:, :], 0.0)
            return v.drain()
        ctx.op("vector", _h0)

        win_tok = [None, None]
        win_used = [None, None]
        wout_tok = [None, None]
        wout_used = [None, None]
        a_toks = [None, None]
        a_free = [None, None]
        gp_hc = [None, None]
        bc_free = None

        for sg in range(NSEG):
            t0 = sg * SEG
            for m in range(32):
                ctx.wait("sync", win_used[0])
                win_tok[0] = ctx.sig("sync", lambda y, m=m: y.dma_start(
                    out=_r(s_win[:, :], "p (c m) -> p c m", c=NT),
                    in_=_r(i_winT[:, m * 128:(m + 1) * 128], "(c p) m -> p c m", p=128)),
                    dma=True)
                ctx.wait("tensor", win_tok[0])
                for k in range(NT):
                    ctx.sig("tensor", lambda t, k=k, t0=t0: t.matmul(
                        pb[0][:, 0:SEG],
                        s_win[:, k * 128:(k + 1) * 128],
                        s_xn[:, k * T + t0:k * T + t0 + SEG],
                        start=(k == 0), stop=(k == NT - 1)))
                win_used[0] = ctx.now("tensor")
                ctx.wait("vector", win_used[0])
                if m < NDI:
                    ctx.sig("vector", lambda v, m=m: v.tensor_scalar(
                        s_xs[:, m * SEG:(m + 1) * SEG], pb[0][:, 0:SEG],
                        s_binc[:, m:m + 1], None, ALU.add))
                    tok_ev = ctx.sig("vector", lambda v, m=m: v.tensor_scalar(
                        s_y[:, m * SEG:(m + 1) * SEG], pb[0][:, 0:SEG],
                        s_binc[:, m:m + 1], s_dc[:, m:m + 1], ALU.add, ALU.mult))
                else:
                    tok_ev = ctx.sig("vector", lambda v, m=m: v.tensor_scalar(
                        s_z[:, (m - NDI) * SEG:(m - NDI + 1) * SEG], pb[0][:, 0:SEG],
                        s_binc[:, m:m + 1], None, ALU.add))
                ctx.wait("tensor", tok_ev)
            ctx.wait("tensor", tok_wx)
            for k in range(NDI):
                ctx.sig("tensor", lambda t, k=k: t.matmul(
                    pb[1][:, 0:SEG], s_wxT[:, k * 128:(k + 1) * 128],
                    s_xs[:, k * SEG:(k + 1) * SEG], start=(k == 0), stop=(k == NDI - 1)))
            tok_dbc = ctx.now("tensor")
            ctx.wait("vector", tok_dbc)
            ctx.sig("vector", lambda v: v.tensor_scalar(
                s_draw[:, :], pb[1][0:DTR, 0:SEG], s_bxc[0:DTR, 0:1], None, ALU.add))
            ctx.sig("vector", lambda v: v.tensor_scalar(
                s_bc[:, :], pb[1][DTR:128, 0:SEG], s_bxc[DTR:128, 0:1], None, ALU.add))
            ctx.op("vector", lambda v: v.drain())
            tok_bc = ctx.sig("vector", lambda v: v.tensor_copy(s_bcb[:, :], s_bc[:, :]))
            ctx.wait("tensor", tok_bc)
            ctx.wait("sync", tok_bc)
            tok_fb = ctx.sig("sync", lambda y: y.dma_start(
                out=_r(s_flat[0:1, 0:NS * SEG], "a (n t) -> a n t", n=NS),
                in_=s_bcb[0:NS, :]), dma=True)
            tok_fc = ctx.sig("sync", lambda y: y.dma_start(
                out=_r(s_flat[64:65, 0:NS * SEG], "a (n t) -> a n t", n=NS),
                in_=s_bcb[32:32 + NS, :]), dma=True)
            ctx.wait("tensor", tok_wdt)
            dl_done = None
            for dt in range(NDI):
                ctx.wait("tensor", dl_done)
                tok_dm = ctx.sig("tensor", lambda t, dt=dt: t.matmul(
                    pb[2][:, 0:SEG], s_wdtT[:, dt * 128:(dt + 1) * 128],
                    s_draw[:, :], start=True, stop=True))
                ctx.wait("scalar", tok_dm)
                ctx.sig("scalar", lambda s, dt=dt: s.activation(
                    s_ab[:, 0:SEG], pb[2][:, 0:SEG], AF.Abs,
                    bias=s_bdtc[:, dt:dt + 1]))
                ctx.sig("scalar", lambda s: s.activation(
                    s_ab[:, 0:SEG], s_ab[:, 0:SEG], AF.Exp, scale=-1.0))
                ctx.sig("scalar", lambda s: s.activation(
                    s_ab[:, 0:SEG], s_ab[:, 0:SEG], AF.Ln, bias=s_kone[:, 0:1]))
                tok_sp = ctx.sig("scalar", lambda s, dt=dt: s.activation(
                    s_ab[:, SEG:2 * SEG], pb[2][:, 0:SEG], AF.Relu,
                    bias=s_bdtc[:, dt:dt + 1]))
                ctx.wait("vector", tok_sp)
                dl_done = ctx.sig("vector", lambda v, dt=dt: v.tensor_tensor(
                    s_dl[:, dt * SEG:(dt + 1) * SEG], s_ab[:, 0:SEG],
                    s_ab[:, SEG:2 * SEG], ALU.add))
                ctx.op("vector", lambda v: v.drain())
                ctx.sig("vector", lambda v, dt=dt: v.tensor_tensor(
                    s_dx[:, dt * SEG:(dt + 1) * SEG],
                    s_dl[:, dt * SEG:(dt + 1) * SEG],
                    s_xs[:, dt * SEG:(dt + 1) * SEG], ALU.mult))
                ctx.wait("scalar", dl_done)
            tok_dl = ctx.now("vector")

            ctx.wait("tensor", tok_fb, tok_fc, tok_dl)
            ctx.wait("gpsimd", tok_dl)
            for n in range(NS):
                ctx.wait("tensor", bc_free)
                tok_Bb = ctx.sig("tensor", lambda t, n=n: t.matmul(
                    pb[3][:, 0:SEG], s_onesb[0:1, :],
                    s_flat[0:1, n * SEG:(n + 1) * SEG], start=True, stop=True))
                tok_Cb = ctx.sig("tensor", lambda t, n=n: t.matmul(
                    pb[4][:, 0:SEG], s_onesb[64:65, :],
                    s_flat[64:65, n * SEG:(n + 1) * SEG],
                    start=True, stop=True))
                ctx.wait("vector", tok_Bb, tok_Cb)
                ctx.sig("vector", lambda v: v.tensor_copy(s_Bs[:, :], pb[3][:, 0:SEG]))
                tok_bc_ev = ctx.sig("vector", lambda v: v.tensor_copy(s_Cs[:, :], pb[4][:, 0:SEG]))
                bc_free = tok_bc_ev
                for hf in range(2):
                    ctx.wait("scalar", a_free[0])
                    for j in range(8):
                        dt = hf * 8 + j
                        a_toks[0] = ctx.sig("scalar", lambda s, dt=dt, n=n, j=j: s.activation(
                            s_abig[:, j * SEG:(j + 1) * SEG],
                            s_dl[:, dt * SEG:(dt + 1) * SEG],
                            AF.Exp, scale=s_A[:, dt * NS + n:dt * NS + n + 1]))
                    ctx.wait("gpsimd", tok_bc_ev, a_free[0])
                    tok_gb = ctx.sig("gpsimd", lambda g, hf=hf: g.tensor_tensor(
                        _r(s_bbig[:, :], "p (c t) -> p c t", c=8),
                        _r(s_dx[:, hf * 8 * SEG:(hf + 1) * 8 * SEG], "p (c t) -> p c t", c=8),
                        s_Bs[:, None, 0:SEG].broadcast_to([128, 8, SEG]), ALU.mult))
                    a_first = _r(s_abig[:, :], "p (c t) -> p c t", c=8)[:, 1:8, 0:1]
                    b_first = _r(s_bbig[:, :], "p (c t) -> p c t", c=8)[:, 1:8, 0:1]
                    hs_v = _r(s_hstate[:, :], "p (dt n) -> p dt n", dt=NDI)[:, hf * 8 + 1:hf * 8 + 8, n:n + 1]
                    ctx.wait("vector", a_toks[0], tok_gb, gp_hc[hf])
                    ctx.sig("vector", lambda v, a=a_first, h=hs_v: v.tensor_tensor(
                        s_fix[:, 0:7, None], a, h, ALU.mult))
                    ctx.op("vector", lambda v: v.drain())
                    ctx.sig("vector", lambda v, b=b_first: v.tensor_tensor(
                        b, b, s_fix[:, 0:7, None], ALU.add))
                    def _za(v, a=a_first):
                        v.memset(a, 0.0)
                        return v.drain()
                    ctx.op("vector", _za)
                    tok_scan = ctx.sig("vector", lambda v, hf=hf, n=n: v.tensor_tensor_scan(
                        s_hbig[:, :], s_abig[:, :], s_bbig[:, :],
                        s_hstate[:, (hf * 8) * NS + n:(hf * 8) * NS + n + 1],
                        ALU.mult, ALU.add))
                    a_free[0] = tok_scan
                    ctx.op("vector", lambda v: v.drain())
                    ctx.sig("vector", lambda v, hf=hf, n=n: v.tensor_copy(
                        _r(s_hstate[:, :], "p (dt n) -> p dt n", dt=NDI)[:, hf * 8:hf * 8 + 8, n:n + 1],
                        _r(s_hbig[:, :], "p (c t) -> p c t", c=8)[:, :, SEG - 1:SEG]))
                    tok_st = ctx.now("vector")
                    ctx.wait("gpsimd", tok_scan)
                    ctx.sig("gpsimd", lambda g, hf=hf: g.tensor_tensor(
                        _r(s_xs[:, hf * 8 * SEG:(hf + 1) * 8 * SEG], "p (c t) -> p c t", c=8),
                        _r(s_hbig[:, :], "p (c t) -> p c t", c=8),
                        s_Cs[:, None, 0:SEG].broadcast_to([128, 8, SEG]), ALU.mult))
                    gp_hc[hf] = ctx.sig("gpsimd", lambda g, hf=hf: g.tensor_tensor(
                        s_y[:, hf * 8 * SEG:(hf + 1) * 8 * SEG],
                        s_y[:, hf * 8 * SEG:(hf + 1) * 8 * SEG],
                        s_xs[:, hf * 8 * SEG:(hf + 1) * 8 * SEG], ALU.add))
                    ctx.wait("vector", gp_hc[hf])
            ctx.wait("vector", gp_hc[0], gp_hc[1])
            ctx.wait("scalar", gp_hc[0], gp_hc[1])
            ctx.wait("scalar", ctx.now("vector"))
            for dt in range(NDI):
                tok_tz = ctx.sig("scalar", lambda s, dt=dt: s.activation(
                    s_g1[:, 0:SEG], s_z[:, dt * SEG:(dt + 1) * SEG], AF.Tanh, scale=0.5))
                ctx.wait("vector", tok_tz)
                ctx.sig("vector", lambda v, dt=dt: v.scalar_tensor_tensor(
                    s_g1[:, 0:SEG], s_g1[:, 0:SEG], 1.0,
                    s_z[:, dt * SEG:(dt + 1) * SEG], ALU.add, ALU.mult))
                ctx.op("vector", lambda v: v.drain())
                tok_yg = ctx.sig("vector", lambda v, dt=dt: v.scalar_tensor_tensor(
                    s_yg[:, dt * SEG:(dt + 1) * SEG], s_y[:, dt * SEG:(dt + 1) * SEG],
                    0.5, s_g1[:, 0:SEG], ALU.mult, ALU.mult))
                ctx.wait("scalar", tok_yg)
            tok_yg_all = ctx.now("vector")
            if getattr(ctx, "cur_b", 0) == 0 and sg == 0:
                S2_mktap[0]("ssm_yg", (128, NDI * SEG), BF16)
                S2_dump[0]("ssm_yg", lambda: s_yg[:, :])
            ctx.wait("tensor", tok_yg_all)
            for dp in range(NT):
                ctx.wait("sync", wout_used[0])
                wout_tok[0] = ctx.sig("sync", lambda y, dp=dp: y.dma_start(
                    out=_r(s_wout[:, :], "p (c m) -> p c m", c=NDI),
                    in_=_r(i_woutT[:, dp * 128:(dp + 1) * 128], "(c p) m -> p c m", p=128)),
                    dma=True)
                ctx.wait("tensor", wout_tok[0])
                for k in range(NDI):
                    ctx.sig("tensor", lambda t, k=k: t.matmul(
                        pb[5][:, 0:SEG],
                        s_wout[:, k * 128:(k + 1) * 128],
                        s_yg[:, k * SEG:(k + 1) * SEG], start=(k == 0), stop=(k == NDI - 1)))
                wout_used[0] = ctx.now("tensor")
                ctx.wait("vector", wout_used[0])
                tok_xev = ctx.sig("vector", lambda v, dp=dp, t0=t0: v.scalar_tensor_tensor(
                    s_x[:, dp * T + t0:dp * T + t0 + SEG], pb[5][:, 0:SEG],
                    s_boutc[:, dp:dp + 1],
                    s_x[:, dp * T + t0:dp * T + t0 + SEG], ALU.add, ALU.add))
                ctx.wait("tensor", tok_xev)
            tok_seg = ctx.now("vector")
            ctx.wait("tensor", tok_seg)
            ctx.wait("scalar", tok_seg)
            ctx.wait("scalar", wout_used[0])
            ctx.wait("vector", wout_used[0])


    es2.close()


def emit_l1_moe(ctx, nc, S, i_gwT, i_uwT, i_dwT, i_sgwT, i_suwT, i_sdwT):
    s_xn, s_x, pb, pbh = S["xn"], S["x"], S["pb"], S["pbh"]
    s_rwT, s_rbb, s_onesb, s_onesf, s_ident, s_kone = (S["rwT"], S["rbb"], S["onesb"],
                                                       S["onesf"], S["ident"], S["kone"])
    es_moe = ExitStack()
    s_wTb = es_moe.enter_context(
        nc.sbuf_tensor("mo_wTb_%d" % ctx.uid(), [NE1, T], BF16))
    s_wrow = es_moe.enter_context(
        nc.sbuf_tensor("mo_wrow_%d" % ctx.uid(), [1, T], BF16))
    with nc.sbuf_tensor("mo_ro_%d" % ctx.uid(), [128, 16 * NE1], F32) as s_ro, \
         nc.sbuf_tensor("mo_wdb_%d" % ctx.uid(), [128, NE1], BF16) as s_wdb:
        C0 = 0
        def cols(j):
            return slice(j * NE1, (j + 1) * NE1)
        ctx.auto_drain = True
        for tile in range(16):
            tt = tile * 128
            for k in range(NT):
                ctx.sig("tensor", lambda t, k=k, tt=tt: t.matmul(
                    pb[0][:, 0:NE1], s_xn[:, k * T + tt:k * T + tt + 128],
                    s_rwT[:, k * NE1:(k + 1) * NE1], start=(k == 0), stop=(k == NT - 1)))
            tok_lg = ctx.now("tensor")
            ctx.wait("scalar", tok_lg)
            tok_th = ctx.sig("scalar", lambda s: s.activation(
                s_ro[:, cols(0)], pb[0][:, 0:NE1], AF.Tanh, scale=0.5))
            ctx.wait("vector", tok_th)
            ctx.sig("vector", lambda v: v.tensor_scalar(
                s_ro[:, cols(1)], s_ro[:, cols(0)], 1.0, 0.5, ALU.add, ALU.mult))
            tok_sfc = ctx.sig("vector", lambda v: v.tensor_tensor(
                s_ro[:, cols(2)], s_ro[:, cols(1)], s_rbb[:, :], ALU.add))
            ctx.wait("tensor", tok_sfc)
            mcol = 10 * NE1
            for g in range(4):
                gsl = slice(2 * NE1 + 8 * g, 2 * NE1 + 8 * (g + 1))
                ctx.op("vector", lambda v: v.drain())
                ctx.sig("vector", lambda v, gsl=gsl, mcol=mcol: v.tensor_reduce(
                    s_ro[:, mcol:mcol + 1], s_ro[:, gsl], AX.X, ALU.max))
                ctx.op("vector", lambda v: v.drain())
                ctx.sig("vector", lambda v, gsl=gsl, mcol=mcol: v.tensor_scalar(
                    s_ro[:, cols(6)][:, 0:8], s_ro[:, gsl], s_ro[:, mcol:mcol + 1],
                    None, ALU.is_equal))
                ctx.op("vector", lambda v: v.drain())
                ctx.sig("vector", lambda v, gsl=gsl: v.scalar_tensor_tensor(
                    s_ro[:, cols(6)][:, 8:16], s_ro[:, cols(6)][:, 0:8], -1e30,
                    s_ro[:, gsl], ALU.mult, ALU.add))
                ctx.op("vector", lambda v: v.drain())
                ctx.sig("vector", lambda v, mcol=mcol: v.tensor_reduce(
                    s_ro[:, mcol + 1:mcol + 2], s_ro[:, cols(6)][:, 8:16], AX.X, ALU.max))
                ctx.op("vector", lambda v: v.drain())
                ctx.sig("vector", lambda v, g=g, mcol=mcol: v.tensor_tensor(
                    s_ro[:, 3 * NE1 + g:3 * NE1 + g + 1], s_ro[:, mcol:mcol + 1],
                    s_ro[:, mcol + 1:mcol + 2], ALU.add))
            gsl4 = slice(3 * NE1, 3 * NE1 + 4)
            ctx.op("vector", lambda v: v.drain())
            ctx.sig("vector", lambda v: v.tensor_reduce(
                s_ro[:, mcol:mcol + 1], s_ro[:, gsl4], AX.X, ALU.max))
            ctx.op("vector", lambda v: v.drain())
            ctx.sig("vector", lambda v: v.tensor_scalar(
                s_ro[:, cols(6)][:, 0:4], s_ro[:, gsl4], s_ro[:, mcol:mcol + 1],
                None, ALU.is_equal))
            ctx.op("vector", lambda v: v.drain())
            ctx.sig("vector", lambda v: v.scalar_tensor_tensor(
                s_ro[:, cols(6)][:, 8:12], s_ro[:, cols(6)][:, 0:4], -1e30,
                s_ro[:, gsl4], ALU.mult, ALU.add))
            ctx.op("vector", lambda v: v.drain())
            ctx.sig("vector", lambda v: v.tensor_reduce(
                s_ro[:, mcol + 1:mcol + 2], s_ro[:, cols(6)][:, 8:12], AX.X, ALU.max))
            ctx.op("vector", lambda v: v.drain())
            ctx.sig("vector", lambda v: v.tensor_scalar(
                s_ro[:, 3 * NE1 + 8:3 * NE1 + 12], s_ro[:, gsl4],
                s_ro[:, mcol + 1:mcol + 2], None, ALU.is_ge))
            ctx.sig("vector", lambda v: v.tensor_tensor(
                _r(s_ro[:, cols(4)], "p (g j) -> p g j", g=4),
                _r(s_ro[:, cols(2)], "p (g j) -> p g j", g=4),
                s_ro[:, 3 * NE1 + 8:3 * NE1 + 12][:, :, None].broadcast_to([128, 4, 8]),
                ALU.mult))
            ctx.sig("vector", lambda v: v.tensor_copy(s_ro[:, cols(5)], s_ro[:, cols(4)]))
            ctx.op("vector", lambda v: v.drain())
            def _z(v):
                v.memset(s_ro[:, cols(8)], 0.0)
                return v.drain()
            ctx.op("vector", _z)
            for i4 in range(4):
                ctx.op("vector", lambda v: v.drain())
                ctx.sig("vector", lambda v, mcol=mcol: v.tensor_reduce(
                    s_ro[:, mcol + 2:mcol + 3], s_ro[:, cols(5)], AX.X, ALU.max))
                ctx.op("vector", lambda v: v.drain())
                ctx.sig("vector", lambda v, mcol=mcol: v.tensor_scalar(
                    s_ro[:, cols(6)], s_ro[:, cols(5)], s_ro[:, mcol + 2:mcol + 3],
                    None, ALU.is_equal))
                ctx.op("vector", lambda v: v.drain())
                ctx.sig("vector", lambda v: v.tensor_tensor_scan(
                    s_ro[:, cols(7)], s_onesf[:, 0:NE1], s_ro[:, cols(6)],
                    0.0, ALU.mult, ALU.add))
                ctx.op("vector", lambda v: v.drain())
                ctx.sig("vector", lambda v: v.tensor_scalar(
                    s_ro[:, cols(7)], s_ro[:, cols(7)], 1.0, None, ALU.is_equal))
                ctx.op("vector", lambda v: v.drain())
                ctx.sig("vector", lambda v: v.tensor_tensor(
                    s_ro[:, cols(6)], s_ro[:, cols(6)], s_ro[:, cols(7)], ALU.mult))
                ctx.op("vector", lambda v: v.drain())
                ctx.sig("vector", lambda v: v.tensor_tensor(
                    s_ro[:, cols(8)], s_ro[:, cols(8)], s_ro[:, cols(6)], ALU.add))
                ctx.sig("vector", lambda v: v.scalar_tensor_tensor(
                    s_ro[:, cols(5)], s_ro[:, cols(6)], -1e30, s_ro[:, cols(5)],
                    ALU.mult, ALU.add))
                ctx.op("vector", lambda v: v.drain())
            ctx.sig("vector", lambda v: v.tensor_tensor(
                s_ro[:, cols(9)], s_ro[:, cols(8)], s_ro[:, cols(1)], ALU.mult))
            ctx.op("vector", lambda v: v.drain())
            ctx.sig("vector", lambda v, mcol=mcol: v.tensor_reduce(
                s_ro[:, mcol + 3:mcol + 4], s_ro[:, cols(9)], AX.X, ALU.add))
            ctx.op("vector", lambda v: v.drain())
            tok_den = ctx.sig("vector", lambda v, mcol=mcol: v.tensor_scalar(
                s_ro[:, mcol + 3:mcol + 4], s_ro[:, mcol + 3:mcol + 4], 1e-20,
                None, ALU.add))
            ctx.wait("scalar", tok_den)
            ctx.sig("scalar", lambda s, mcol=mcol: s.activation(
                s_ro[:, mcol + 4:mcol + 5], s_ro[:, mcol + 3:mcol + 4], AF.Ln))
            tok_wi = ctx.sig("scalar", lambda s, mcol=mcol: s.activation(
                s_ro[:, mcol + 4:mcol + 5], s_ro[:, mcol + 4:mcol + 5], AF.Exp, scale=-1.0))
            ctx.wait("vector", tok_wi)
            tok_wd = ctx.sig("vector", lambda v, mcol=mcol: v.tensor_scalar(
                s_wdb[:, :], s_ro[:, cols(9)], s_ro[:, mcol + 4:mcol + 5], 2.5,
                ALU.mult, ALU.mult))
            ctx.wait("tensor", tok_wd)
            tok_tr = ctx.sig("tensor", lambda t: t.transpose(
                pbh[0:NE1, 0:128], s_wdb[:, :], s_ident[:, :]))
            ctx.wait("vector", tok_tr)
            tok_wT = ctx.sig("vector", lambda v, tt=tt: v.tensor_copy(
                s_wTb[:, tt:tt + 128], pbh[0:NE1, 0:128]))
            ctx.wait("tensor", tok_wT)
        tok_wTb = ctx.now("vector")
        ctx.auto_drain = False
        if getattr(ctx, "cur_b", 0) == 0:
            S2_mktap[0]("moe_ro", (128, 16 * NE1))
            S2_mktap[0]("rbb", (128, NE1))
            S2_dump[0]("rbb", lambda: s_rbb[:, :])
            S2_mktap[0]("wTb", (NE1, T), BF16)
            S2_dump[0]("moe_ro", lambda: s_ro[:, :])
            S2_dump[0]("wTb", lambda: s_wTb[:, :])

    with nc.sbuf_tensor("sh_gw_%d" % ctx.uid(), [128, NT * SHI], BF16) as s_sgw, \
         nc.sbuf_tensor("sh_uw_%d" % ctx.uid(), [128, NT * SHI], BF16) as s_suw, \
         nc.sbuf_tensor("sh_dw_%d" % ctx.uid(), [128, NT * D], BF16) as s_sdw, \
         nc.sbuf_tensor("sh_hid_%d" % ctx.uid(), [128, NT * TC], BF16) as s_shid, \
         nc.sbuf_tensor("sh_sg_%d" % ctx.uid(), [128, TC], F32) as s_ssg:
        ctx.wait("sync", ctx.now("vector"), ctx.now("tensor"), ctx.now("scalar"))
        tok_g = ctx.sig("sync", lambda y: y.dma_start(
            out=_r(s_sgw[:, :], "p (c m) -> p c m", c=NT),
            in_=_r(i_sgwT, "(c p) m -> p c m", p=128)), dma=True)
        tok_u = ctx.sig("sync", lambda y: y.dma_start(
            out=_r(s_suw[:, :], "p (c m) -> p c m", c=NT),
            in_=_r(i_suwT, "(c p) m -> p c m", p=128)), dma=True)
        tok_d = ctx.sig("sync", lambda y: y.dma_start(
            out=_r(s_sdw[:, :], "p (c m) -> p c m", c=NT),
            in_=_r(i_sdwT, "(c p) m -> p c m", p=128)), dma=True)
        ctx.wait("tensor", tok_g, tok_u, tok_d)
        shev_tok = [None, None]
        for c in range(NTC):
            for it in range(NT):
                for k in range(NT):
                    ctx.sig("tensor", lambda t, k=k, it=it, c=c: t.matmul(
                        pb[0][:, 0:TC], s_sgw[:, k * SHI + it * 128:k * SHI + (it + 1) * 128],
                        s_xn[:, k * T + c * TC:k * T + (c + 1) * TC],
                        start=(k == 0), stop=(k == NT - 1)))
                for k in range(NT):
                    ctx.sig("tensor", lambda t, k=k, it=it, c=c: t.matmul(
                        pb[1][:, 0:TC], s_suw[:, k * SHI + it * 128:k * SHI + (it + 1) * 128],
                        s_xn[:, k * T + c * TC:k * T + (c + 1) * TC],
                        start=(k == 0), stop=(k == NT - 1)))
                tok_gu = ctx.now("tensor")
                ctx.wait("scalar", tok_gu)
                tok_sg = ctx.sig("scalar", lambda s: s.activation(
                    s_ssg[:, :], pb[0][:, 0:TC], AF.Silu))
                ctx.wait("vector", tok_sg)
                tok_hid = ctx.sig("vector", lambda v, it=it: v.tensor_tensor(
                    s_shid[:, it * TC:(it + 1) * TC], s_ssg[:, :], pb[1][:, 0:TC],
                    ALU.mult))
                ctx.wait("tensor", tok_hid)
            for dp in range(NT):
                bnk = 2 + dp % 2
                ctx.wait("tensor", shev_tok[dp % 2])
                for k in range(NT):
                    ctx.sig("tensor", lambda t, k=k, dp=dp, bnk=bnk: t.matmul(
                        pb[bnk][:, 0:TC], s_sdw[:, k * D + dp * 128:k * D + (dp + 1) * 128],
                        s_shid[:, k * TC:(k + 1) * TC], start=(k == 0), stop=(k == NT - 1)))
                tok_o = ctx.now("tensor")
                ctx.wait("vector", tok_o)
                shev_tok[dp % 2] = ctx.sig("vector", lambda v, dp=dp, c=c, bnk=bnk:
                                           v.scalar_tensor_tensor(
                    s_x[:, dp * T + c * TC:dp * T + (c + 1) * TC], pb[bnk][:, 0:TC], 1.0,
                    s_x[:, dp * T + c * TC:dp * T + (c + 1) * TC], ALU.mult, ALU.add))
            ctx.wait("tensor", shev_tok[0], shev_tok[1])

    with nc.sbuf_tensor("mo_gw_%d" % ctx.uid(), [128, 2 * NT * I1], BF16) as s_gw, \
         nc.sbuf_tensor("mo_uw_%d" % ctx.uid(), [128, 2 * NT * I1], BF16) as s_uw, \
         nc.sbuf_tensor("mo_dw_%d" % ctx.uid(), [128, 2 * 4 * D], BF16) as s_dw, \
         nc.sbuf_tensor("mo_hid_%d" % ctx.uid(), [128, 4 * T], BF16) as s_hid, \
         nc.sbuf_tensor("mo_sg_%d" % ctx.uid(), [128, 2 * TC], F32) as s_sg, \
         nc.sbuf_tensor("mo_t1_%d" % ctx.uid(), [128, 2 * TC], F32) as s_t1:
        ld_tok = [None, None]
        used_tok = [None, None]
        hid_done = [None, None]
        sg_free = [None, None]
        tok_wb_last = None
        for e in range(NE1):
            sl = e % 2
            if e == 0:
                ctx.wait("sync", ctx.now("vector"), ctx.now("tensor"), tok_wTb)
            ctx.wait("sync", used_tok[sl])
            ctx.wait("sync", tok_wb_last)
            tok_row = ctx.sig("sync", lambda y, e=e: y.dma_start(
                out=s_wrow[0:1, :], in_=s_wTb[e:e + 1, :]), dma=True)
            ctx.sig("sync", lambda y, e=e, sl=sl: y.dma_start(
                out=_r(s_gw[:, sl * NT * I1:(sl + 1) * NT * I1], "p (c m) -> p c m", c=NT),
                in_=_r(i_gwT[e], "(c p) m -> p c m", p=128)), dma=True)
            ctx.sig("sync", lambda y, e=e, sl=sl: y.dma_start(
                out=_r(s_uw[:, sl * NT * I1:(sl + 1) * NT * I1], "p (c m) -> p c m", c=NT),
                in_=_r(i_uwT[e], "(c p) m -> p c m", p=128)), dma=True)
            ld_tok[sl] = ctx.sig("sync", lambda y, e=e, sl=sl: y.dma_start(
                out=_r(s_dw[:, sl * 4 * D:(sl + 1) * 4 * D], "p (c m) -> p c m", c=4),
                in_=_r(i_dwT[e], "(c p) m -> p c m", p=128)), dma=True)
            ctx.wait("tensor", ld_tok[sl])
            g0 = sl * NT * I1
            d0 = sl * 4 * D
            for c in range(NTC):
                wpar = c % 2
                ctx.wait("tensor", tok_row, hid_done[0], hid_done[1])
                tok_wb = ctx.sig("tensor", lambda t, c=c: t.matmul(
                    pb[2][:, 0:TC], s_onesb[0:1, :],
                    s_wrow[0:1, c * TC:(c + 1) * TC],
                    start=True, stop=True))
                tok_wb_last = tok_wb
                for it in range(4):
                    par = it % 2
                    pg, pu = pb[0 + 5 * par], pb[1 + 5 * par]
                    ctx.wait("tensor", hid_done[par])
                    for k in range(NT):
                        ctx.sig("tensor", lambda t, k=k, it=it, c=c, g0=g0, pg=pg: t.matmul(
                            pg[:, 0:TC],
                            s_gw[:, g0 + k * I1 + it * 128:g0 + k * I1 + (it + 1) * 128],
                            s_xn[:, k * T + c * TC:k * T + (c + 1) * TC],
                            start=(k == 0), stop=(k == NT - 1)))
                    for k in range(NT):
                        ctx.sig("tensor", lambda t, k=k, it=it, c=c, g0=g0, pu=pu: t.matmul(
                            pu[:, 0:TC],
                            s_uw[:, g0 + k * I1 + it * 128:g0 + k * I1 + (it + 1) * 128],
                            s_xn[:, k * T + c * TC:k * T + (c + 1) * TC],
                            start=(k == 0), stop=(k == NT - 1)))
                    tok_gu = ctx.now("tensor")
                    ctx.wait("scalar", tok_gu, sg_free[par])
                    tok_sg = ctx.sig("scalar", lambda s, par=par, pg=pg: s.activation(
                        s_sg[:, par * TC:(par + 1) * TC], pg[:, 0:TC], AF.Silu))
                    ctx.wait("vector", tok_sg, tok_wb)
                    ctx.sig("vector", lambda v, par=par, pu=pu: v.tensor_tensor(
                        s_t1[:, par * TC:(par + 1) * TC],
                        s_sg[:, par * TC:(par + 1) * TC], pu[:, 0:TC], ALU.mult))
                    ctx.op("vector", lambda v: v.drain())
                    tok_hid = ctx.sig("vector", lambda v, it=it, c=c, par=par: v.tensor_tensor(
                        s_hid[:, it * T + c * TC:it * T + (c + 1) * TC],
                        s_t1[:, par * TC:(par + 1) * TC],
                        pb[2][:, 0:TC], ALU.mult))
                    hid_done[par] = tok_hid
                    sg_free[par] = tok_hid
            ctx.wait("tensor", hid_done[0], hid_done[1])
            dev_tok = [None, None]
            for c in range(NTC):
                for dp in range(NT):
                    i2 = c * NT + dp
                    bnk = 3 + (i2 % 2)
                    ctx.wait("tensor", dev_tok[i2 % 2])
                    for k in range(4):
                        ctx.sig("tensor", lambda t, k=k, dp=dp, c=c, d0=d0, bnk=bnk: t.matmul(
                            pb[bnk][:, 0:TC],
                            s_dw[:, d0 + k * D + dp * 128:d0 + k * D + (dp + 1) * 128],
                            s_hid[:, k * T + c * TC:k * T + (c + 1) * TC],
                            start=(k == 0), stop=(k == 3)))
                    tok_o = ctx.now("tensor")
                    ctx.wait("vector", tok_o)
                    dev_tok[i2 % 2] = ctx.sig("vector", lambda v, dp=dp, c=c, bnk=bnk:
                                              v.scalar_tensor_tensor(
                        s_x[:, dp * T + c * TC:dp * T + (c + 1) * TC],
                        pb[bnk][:, 0:TC], 1.0,
                        s_x[:, dp * T + c * TC:dp * T + (c + 1) * TC],
                        ALU.mult, ALU.add))
            ctx.wait("tensor", dev_tok[0], dev_tok[1])
            used_tok[sl] = ctx.now("tensor")
    es_moe.close()


def build_program(debug_taps=()):
    nc = bass.Bass("TRN2", target_bir_lowering=False)
    es = ExitStack()

    def din(name, shape, dt=F32):
        return nc.dram_tensor(name, list(shape), dt, kind="ExternalInput").ap()

    d_x = din("x_in", (BPC, D, T))
    d_wq = din("w_q", (D, H)); d_wk = din("w_k", (D, H)); d_wv = din("w_v", (D, H))
    d_wo = din("w_o", (H, D))
    d_wbase = din("w_base", (D, D))
    d_wd1 = din("wd1", (NE0, D, H0)); d_wd2 = din("wd2", (NE0, H0, D))
    d_wad = din("wa_down", (NE0, D, R0)); d_wau = din("wa_up", (NE0, R0, D))
    d_bbase = din("bb_base", (128, NT))
    d_bd1 = din("bd1_c", (128, NE0, 2))
    d_bd2 = din("bd2_c", (128, NE0, NT))
    d_bad = din("bad_c", (R0, NE0))
    d_bau = din("bau_c", (128, NE0, NT))
    d_winT = din("w_inT", (D, 2 * DI))
    d_wxT = din("w_xT", (DI, 128))
    d_wdtT = din("w_dtT", (DTR, DI))
    d_woutT = din("w_outT", (DI, D))
    d_binc = din("b_in_c", (128, 32))
    d_bxc = din("b_x_c", (128, 1))
    d_bdtc = din("b_dt_c", (128, NDI))
    d_alogc = din("a_log_c", (128, NDI * NS))
    d_dc = din("d_c", (128, NDI))
    d_boutc = din("b_out_c", (128, NT))
    d_rwT = din("rw_T", (D, NE1))
    d_rb = din("r_bias", (1, NE1))
    d_gwT = din("gw_T", (NE1, D, I1))
    d_uwT = din("uw_T", (NE1, D, I1))
    d_dwT = din("dw_T", (NE1, I1, D))
    d_sgwT = din("sgw_T", (D, SHI))
    d_suwT = din("suw_T", (D, SHI))
    d_sdwT = din("sdw_T", (SHI, D))
    d_ident = din("c_ident", (128, 128), BF16)
    d_triu = din("c_triu", (128, 128), BF16)
    d_onesb = din("c_ones_b", (128, 128), BF16)
    d_onesf = din("c_ones_f", (128, 128), F32)

    d_out = nc.dram_tensor("x_out", [BPC, D, T], F32, kind="ExternalOutput").ap()

    def dint(name, shape):
        return nc.dram_tensor(name, list(shape), BF16, kind="Internal").ap()

    i_wq = dint("i_wq", (D, H)); i_wk = dint("i_wk", (D, H)); i_wv = dint("i_wv", (D, H))
    i_wo = dint("i_wo", (H, D))
    i_wbase = dint("i_wbase", (D, D))
    i_wd1 = dint("i_wd1", (NE0, D, H0)); i_wd2 = dint("i_wd2", (NE0, H0, D))
    i_wad = dint("i_wad", (NE0, D, R0)); i_wau = dint("i_wau", (NE0, R0, D))
    i_winT = dint("i_winT", (D, 2 * DI))
    i_wxT = dint("i_wxT", (DI, 128))
    i_wdtT = dint("i_wdtT", (DTR, DI))
    i_woutT = dint("i_woutT", (DI, D))
    i_rwT = dint("i_rwT", (D, NE1))
    i_gwT = dint("i_gwT", (NE1, D, I1))
    i_uwT = dint("i_uwT", (NE1, D, I1))
    i_dwT = dint("i_dwT", (NE1, I1, D))
    i_sgwT = dint("i_sgwT", (D, SHI))
    i_suwT = dint("i_suwT", (D, SHI))
    i_sdwT = dint("i_sdwT", (SHI, D))

    ctx = Ctx(nc, es)
    ctx.debug_taps = tuple(debug_taps)

    taps = {}
    def mktap(name, shape, dt=F32):
        if name in debug_taps and name not in taps:
            taps[name] = nc.dram_tensor("tap_" + name, list(shape), dt,
                                        kind="ExternalOutput").ap()

    for b in range(BPC):
        mktap("xn0_b%d" % b, (128, NT * T), BF16)
        mktap("x_l0_b%d" % b, (128, NT * T))
        mktap("oall_b%d" % b, (H, T), BF16)
        mktap("xn1_b%d" % b, (128, NT * T), BF16)
        mktap("x_ssm_b%d" % b, (128, NT * T))

    def dump(name, ap_fn):
        t = taps.get(name)
        if t is None:
            return
        bar = ctx.barrier()
        ctx.sig("gpsimd", lambda g, t=t: g.dma_start(out=t, in_=ap_fn()), dma=True)
        ctx.barrier()
    S2_dump[0] = dump
    S2_mktap[0] = mktap

    def sbr(name, shape, dt):
        return es.enter_context(nc.sbuf_tensor(name, list(shape), dt))

    S = {}
    S["x"] = sbr("s_x", (128, NT * T), F32)
    S["xn"] = sbr("s_xn", (128, NT * T), BF16)
    S["ident"] = sbr("s_ident", (128, 128), BF16)
    S["triu"] = sbr("s_triu", (128, 128), BF16)
    S["onesb"] = sbr("s_onesb", (128, 128), BF16)
    S["onesf"] = sbr("s_onesf", (128, 128), F32)
    S["wq"] = sbr("s_wq", (128, NT * H), BF16)
    S["wk"] = sbr("s_wk", (128, NT * H), BF16)
    S["wv"] = sbr("s_wv", (128, NT * H), BF16)
    S["wo"] = sbr("s_wo", (H, D), BF16)
    S["rwT"] = sbr("s_rwT", (128, NT * NE1), BF16)
    S["bbase"] = sbr("s_bbase", (128, NT), F32)
    S["bd1"] = sbr("s_bd1", (128, NE0 * 2), F32)
    S["bd2"] = sbr("s_bd2", (128, NE0 * NT), F32)
    S["bad"] = sbr("s_bad", (R0, NE0), F32)
    S["bau"] = sbr("s_bau", (128, NE0 * NT), F32)
    S["binc"] = sbr("s_binc", (128, 32), F32)
    S["bxc"] = sbr("s_bxc", (128, 1), F32)
    S["bdtc"] = sbr("s_bdtc", (128, NDI), F32)
    S["A"] = sbr("s_A", (128, NDI * NS), F32)
    S["dc"] = sbr("s_dc", (128, NDI), F32)
    S["boutc"] = sbr("s_boutc", (128, NT), F32)
    S["rbb"] = sbr("s_rbb", (128, NE1), F32)
    S["kone"] = sbr("s_kone", (128, 2), F32)
    S["hstate"] = sbr("s_hstate", (128, NDI * NS), F32)

    pb = [es.enter_context(nc.psum_tensor("pb%d" % i, [128, 512], F32))
          for i in range(7)]
    pbh = es.enter_context(nc.psum_tensor("pbh", [128, 1024], BF16))
    S["pb"] = pb
    S["pbh"] = pbh

    cast_toks = []
    for (dst, src) in [(i_wq, d_wq), (i_wk, d_wk), (i_wv, d_wv), (i_wo, d_wo),
                       (i_wbase, d_wbase), (i_wd1, d_wd1), (i_wd2, d_wd2),
                       (i_wad, d_wad), (i_wau, d_wau)]:
        cast_toks.append(ctx.sig("gpsimd", lambda g, a=dst, b=src: g.dma_start(out=a, in_=b), dma=True))
    tok_cast_l0 = cast_toks[-1]
    for (dst, src) in [(i_winT, d_winT), (i_wxT, d_wxT), (i_wdtT, d_wdtT),
                       (i_woutT, d_woutT), (i_rwT, d_rwT),
                       (i_sgwT, d_sgwT), (i_suwT, d_suwT), (i_sdwT, d_sdwT)]:
        cast_toks.append(ctx.sig("gpsimd", lambda g, a=dst, b=src: g.dma_start(out=a, in_=b), dma=True))
    tok_cast_l1 = cast_toks[-1]
    for (dst, src) in [(i_gwT, d_gwT), (i_uwT, d_uwT), (i_dwT, d_dwT)]:
        cast_toks.append(ctx.sig("gpsimd", lambda g, a=dst, b=src: g.dma_start(out=a, in_=b), dma=True))
    tok_cast_moe = cast_toks[-1]

    for (dst, src) in [(S["ident"], d_ident), (S["triu"], d_triu), (S["onesb"], d_onesb),
                       (S["onesf"], d_onesf), (S["bbase"], d_bbase),
                       (S["bad"], d_bad), (S["binc"], d_binc), (S["bxc"], d_bxc),
                       (S["bdtc"], d_bdtc), (S["dc"], d_dc), (S["boutc"], d_boutc)]:
        ctx.sig("sync", lambda y, a=dst, b=src: y.dma_start(out=a[:, :], in_=b), dma=True)
    ctx.sig("sync", lambda y: y.dma_start(out=S["bd1"][:, :], in_=_r(d_bd1, "p e c -> p (e c)")), dma=True)
    ctx.sig("sync", lambda y: y.dma_start(out=S["bd2"][:, :], in_=_r(d_bd2, "p e c -> p (e c)")), dma=True)
    ctx.sig("sync", lambda y: y.dma_start(out=S["bau"][:, :], in_=_r(d_bau, "p e c -> p (e c)")), dma=True)
    ctx.sig("sync", lambda y: y.dma_start(out=S["A"][:, :], in_=d_alogc), dma=True)
    ctx.sig("sync", lambda y: y.dma_start(out=S["rbb"][0:1, 0:NE1], in_=d_rb), dma=True)
    ctx.wait("sync", tok_cast_l1)
    for (dst, src) in [(S["wq"], i_wq), (S["wk"], i_wk), (S["wv"], i_wv)]:
        ctx.sig("sync", lambda y, a=dst, b=src: y.dma_start(
            out=_r(a[:, :], "p (c m) -> p c m", m=H),
            in_=_r(b, "(c p) m -> p c m", p=128)), dma=True)
    ctx.sig("sync", lambda y: y.dma_start(out=S["wo"][:, :], in_=i_wo), dma=True)
    ctx.sig("sync", lambda y: y.dma_start(
        out=_r(S["rwT"][:, :], "p (c m) -> p c m", m=NE1),
        in_=_r(i_rwT, "(c p) m -> p c m", p=128)), dma=True)
    tok_params = ctx.now("sync")

    ctx.wait("tensor", tok_params)
    tok_rbmm = ctx.sig("tensor", lambda t: t.matmul(
        pb[0][:, 0:NE1], S["onesf"][0:1, :], S["rbb"][0:1, 0:NE1], start=True, stop=True))
    ctx.wait("vector", tok_rbmm)
    ctx.sig("vector", lambda v: v.tensor_copy(S["rbb"][:, :], pb[0][:, 0:NE1]))
    ctx.wait("scalar", tok_params)
    tok_Ae = ctx.sig("scalar", lambda s: s.activation(S["A"][:, :], S["A"][:, :], AF.Exp))
    ctx.wait("vector", tok_Ae)
    ctx.sig("vector", lambda v: v.tensor_scalar(S["A"][:, :], S["A"][:, :], -1.0, None, ALU.mult))
    def _kone(v):
        v.memset(S["kone"][:, 0:1], 1.0)
        v.memset(S["kone"][:, 1:2], 1e-5)
        return v.drain()
    ctx.op("vector", _kone)
    ctx.barrier()

    for b in range(BPC):
        ctx.cur_b = b
        tok_x = ctx.sig("sync", lambda y, b=b: y.dma_start(
            out=_r(S["x"][:, :], "p (c t) -> p c t", c=NT),
            in_=_r(d_x[b], "(c p) t -> p c t", p=128)), dma=True)
        emit_layernorm(ctx, nc, S, [tok_x])
        ctx.barrier()
        dump("xn0_b%d" % b, lambda: S["xn"][:, :])
        es_l0 = ExitStack()
        S["oall"] = es_l0.enter_context(
            nc.sbuf_tensor("s_oall_%d" % ctx.uid(), [H, T], BF16))
        if PHASES["l0a"]:
            emit_l0_attn(ctx, nc, S)
        dump("oall_b%d" % b, lambda: S["oall"][:, :])
        if PHASES["l0f"]:
            emit_l0_ffn(ctx, nc, S, i_wd1, i_wd2, i_wad, i_wau, i_wbase)
        es_l0.close()
        ctx.barrier()
        dump("x_l0_b%d" % b, lambda: S["x"][:, :])
        emit_layernorm(ctx, nc, S, [])
        ctx.barrier()
        dump("xn1_b%d" % b, lambda: S["xn"][:, :])
        if PHASES["ssm"]:
            emit_l1_ssm(ctx, nc, S, i_winT, i_wxT, i_wdtT, i_woutT)
        ctx.barrier()
        dump("x_ssm_b%d" % b, lambda: S["x"][:, :])
        if PHASES["moe"]:
            ctx.wait("sync", tok_cast_moe)
            emit_l1_moe(ctx, nc, S, i_gwT, i_uwT, i_dwT, i_sgwT, i_suwT, i_sdwT)
        ctx.barrier()
        tok_store = ctx.sig("gpsimd", lambda g, b=b: g.dma_start(
            out=_r(d_out[b], "(c p) t -> p c t", p=128),
            in_=_r(S["x"][:, :], "p (c t) -> p c t", c=NT)), dma=True)
        ctx.wait("sync", tok_store)
        ctx.wait("gpsimd", tok_store)

    ctx.barrier()

    blk = es.enter_context(nc.Block())

    @blk.tensor
    def _(e):
        for f in ctx.q["tensor"]:
            f(e)

    @blk.vector
    def _(e):
        for f in ctx.q["vector"]:
            f(e)

    @blk.scalar
    def _(e):
        for f in ctx.q["scalar"]:
            f(e)

    @blk.gpsimd
    def _(e):
        for f in ctx.q["gpsimd"]:
            f(e)

    @blk.sync
    def _(e):
        for f in ctx.q["sync"]:
            f(e)

    es.close()
    return nc


def host_inputs(inp, core):
    f = np.ascontiguousarray

    def t(a):
        return f(np.asarray(a).transpose())

    x = np.asarray(inp["L0_attn_x"])
    xb = x[core * BPC:(core + 1) * BPC]
    m = {
        "x_in": f(xb.transpose(0, 2, 1)),
        "w_q": f(inp["L0_attn_W_q"]), "w_k": f(inp["L0_attn_W_k"]),
        "w_v": f(inp["L0_attn_W_v"]), "w_o": f(inp["L0_attn_W_o"]),
        "w_base": f(inp["L0_ffn_W_base"]),
        "wd1": f(inp["L0_ffn_Wd1"]), "wd2": f(inp["L0_ffn_Wd2"]),
        "wa_down": f(inp["L0_ffn_Wa_down"]), "wa_up": f(inp["L0_ffn_Wa_up"]),
        "bb_base": f(np.asarray(inp["L0_ffn_b_base"]).reshape(NT, 128).T),
        "bd1_c": f(np.asarray(inp["L0_ffn_bd1"]).reshape(NE0, 2, 128).transpose(2, 0, 1)),
        "bd2_c": f(np.asarray(inp["L0_ffn_bd2"]).reshape(NE0, NT, 128).transpose(2, 0, 1)),
        "bad_c": f(np.asarray(inp["L0_ffn_ba_down"]).T),
        "bau_c": f(np.asarray(inp["L0_ffn_ba_up"]).reshape(NE0, NT, 128).transpose(2, 0, 1)),
        "w_inT": t(inp["L1_attn_W_in"]),
        "w_dtT": t(inp["L1_attn_W_dt"]),
        "w_outT": t(inp["L1_attn_W_out"]),
        "b_in_c": f(np.asarray(inp["L1_attn_b_in"]).reshape(32, 128).T),
        "b_dt_c": f(np.asarray(inp["L1_attn_b_dt"]).reshape(NDI, 128).T),
        "a_log_c": f(np.asarray(inp["L1_attn_A_log"]).reshape(NDI, 128, NS)
                     .transpose(1, 0, 2).reshape(128, NDI * NS)),
        "d_c": f(np.asarray(inp["L1_attn_D"]).reshape(NDI, 128).T),
        "b_out_c": f(np.asarray(inp["L1_attn_b_out"]).reshape(NT, 128).T),
        "rw_T": t(inp["L1_ffn_router_weight"]),
        "r_bias": f(np.asarray(inp["L1_ffn_router_bias"]).reshape(1, NE1)),
        "gw_T": f(np.asarray(inp["L1_ffn_gate_w"]).transpose(0, 2, 1)),
        "uw_T": f(np.asarray(inp["L1_ffn_up_w"]).transpose(0, 2, 1)),
        "dw_T": f(np.asarray(inp["L1_ffn_down_w"]).transpose(0, 2, 1)),
        "sgw_T": t(inp["L1_ffn_shared_gate_w"]),
        "suw_T": t(inp["L1_ffn_shared_up_w"]),
        "sdw_T": t(inp["L1_ffn_shared_down_w"]),
    }
    wx = np.asarray(inp["L1_attn_W_x"])
    wxT = np.zeros((DI, 128), dtype=np.float32)
    wxT[:, 0:64] = wx[0:64].T
    wxT[:, 64:80] = wx[64:80].T
    wxT[:, 96:112] = wx[80:96].T
    m["w_xT"] = f(wxT)
    bx = np.asarray(inp["L1_attn_b_x"])
    bxc = np.zeros((128, 1), dtype=np.float32)
    bxc[0:64, 0] = bx[0:64]
    bxc[64:80, 0] = bx[64:80]
    bxc[96:112, 0] = bx[80:96]
    m["b_x_c"] = bxc
    m["c_ident"] = np.eye(128, dtype=ml_dtypes.bfloat16)
    m["c_triu"] = np.triu(np.ones((128, 128), dtype=np.float32)).astype(ml_dtypes.bfloat16)
    m["c_ones_b"] = np.ones((128, 128), dtype=ml_dtypes.bfloat16)
    m["c_ones_f"] = np.ones((128, 128), dtype=np.float32)
    return m


def kernel(**inp):
    nc = build_program(DEBUG_TAPS)
    in_maps = [host_inputs(inp, core) for core in range(NCORES)]
    res = run_bass_kernel_spmd(nc, in_maps, core_ids=list(range(NCORES)))
    outs = []
    for core in range(NCORES):
        o = res.results[core]["x_out"]
        outs.append(np.ascontiguousarray(o.transpose(0, 2, 1)))
    return np.concatenate(outs, axis=0).astype(np.float32)


if __name__ == "__main__":
    print("building program...")
    import time
    t0 = time.time()
    nc = build_program()
    print("built in %.1fs" % (time.time() - t0))


# version 06
# speedup vs baseline: 1.0057x; 1.0057x over previous
import numpy as np
import ml_dtypes
from contextlib import ExitStack

import concourse.bass as bass
import concourse.mybir as mybir
from concourse.bass_utils import run_bass_kernel_spmd

F32 = mybir.dt.float32
BF16 = mybir.dt.bfloat16
AF = mybir.ActivationFunctionType
ALU = mybir.AluOpType
AX = mybir.AxisListType

NCORES = 8
BPC = 4
T = 2048
D = 1024
NT = 8
TC = 512
NTC = T // TC
H = 64
CH = 128
NCH = T // CH
DI = 2048
NDI = 16
DTR = 64
NS = 16
SEG = 256
NSEG = T // SEG
NE0 = 8
H0 = 256
R0 = 64
NE1 = 32
I1 = 512
SHI = 1024

ENGINES = ("tensor", "vector", "scalar", "gpsimd", "sync")
DEBUG_TAPS = ()
PHASES = {"l0a": True, "l0f": True, "ssm": True, "moe": True}
S2_dump = [None]
S2_mktap = [None]


class Ctx:
    def __init__(self, nc, es):
        self.nc = nc
        self.q = {e: [] for e in ENGINES}
        self.sem = {e: es.enter_context(nc.semaphore("sem_" + e)) for e in ENGINES}
        self.debug_taps = ()
        self.auto_drain = False
        self.cnt = {e: 0 for e in ENGINES}

    _uid = [0]

    def uid(self):
        Ctx._uid[0] += 1
        return Ctx._uid[0]

    def op(self, eng, fn):
        self.q[eng].append(fn)

    def sig(self, eng, fn, dma=False):
        inc = 16 if dma else 1
        self.cnt[eng] += inc
        assert self.cnt[eng] < 2 ** 31
        sem = self.sem[eng]

        def g(e, fn=fn, sem=sem, inc=inc):
            fn(e).then_inc(sem, inc)

        self.q[eng].append(g)
        if self.auto_drain and eng in ("vector", "scalar") and not dma:
            self.q[eng].append(lambda e: e.drain())
        return (sem, self.cnt[eng])

    def wait(self, eng, *toks):
        for tk in toks:
            if tk is None:
                continue
            sem, val = tk
            if val > 0:
                self.q[eng].append(lambda e, s=sem, v=val: e.wait_ge(s, v))

    def now(self, eng):
        return (self.sem[eng], self.cnt[eng])

    def barrier(self):
        toks = {e: self.now(e) for e in ENGINES}
        for e in ENGINES:
            self.wait(e, *[toks[o] for o in ENGINES if o != e])
        return toks


def _r(ap, spec, **kw):
    return ap.rearrange(spec, **kw)


def emit_layernorm(ctx, nc, S, pre_toks):
    s_x, s_xn, s_onesf, s_kone, pb = S["x"], S["xn"], S["onesf"], S["kone"], S["pb"]
    with nc.sbuf_tensor("ln_xsq_%d" % ctx.uid(), [128, TC], F32) as s_xsq, \
         nc.sbuf_tensor("ln_st_%d" % ctx.uid(), [1, 4 * TC], F32) as s_st:
        ctx.wait("scalar", *pre_toks)
        ctx.wait("tensor", *pre_toks)
        for c in range(NTC):
            for i in range(NT):
                sl = (i * T + c * TC, i * T + (c + 1) * TC)
                tok_sq = ctx.sig("scalar", lambda s, sl=sl: s.activation(
                    s_xsq[:, :], s_x[:, sl[0]:sl[1]], AF.Square))
                ctx.sig("tensor", lambda t, sl=sl, i=i: t.matmul(
                    pb[0][0:1, 0:TC], s_onesf[:, 0:1], s_x[:, sl[0]:sl[1]],
                    start=(i == 0), stop=(i == NT - 1)))
                ctx.wait("tensor", tok_sq)
                tok_mm2 = ctx.sig("tensor", lambda t, i=i: t.matmul(
                    pb[1][0:1, 0:TC], s_onesf[:, 0:1], s_xsq[:, :],
                    start=(i == 0), stop=(i == NT - 1)))
                ctx.wait("scalar", tok_mm2)
            tok_sums = ctx.now("tensor")
            ctx.wait("vector", tok_sums)
            ctx.sig("vector", lambda v: v.tensor_scalar(
                s_st[0:1, 0:TC], pb[0][0:1, 0:TC], 1.0 / D, None, ALU.mult))
            ctx.sig("vector", lambda v: v.tensor_tensor(
                s_st[0:1, 2 * TC:3 * TC], s_st[0:1, 0:TC], s_st[0:1, 0:TC], ALU.mult))
            tok_m2 = ctx.sig("vector", lambda v: v.scalar_tensor_tensor(
                s_st[0:1, 3 * TC:4 * TC], pb[1][0:1, 0:TC], 1.0 / D,
                s_st[0:1, 2 * TC:3 * TC], ALU.mult, ALU.subtract))
            ctx.wait("scalar", tok_m2)
            ctx.sig("scalar", lambda s: s.activation(
                s_st[0:1, 3 * TC:4 * TC], s_st[0:1, 3 * TC:4 * TC],
                AF.Ln, bias=s_kone[0:1, 1:2]))
            tok_rstd = ctx.sig("scalar", lambda s: s.activation(
                s_st[0:1, TC:2 * TC], s_st[0:1, 3 * TC:4 * TC], AF.Exp, scale=-0.5))
            ctx.wait("tensor", tok_rstd)
            tok_bm = ctx.sig("tensor", lambda t: t.matmul(
                pb[2][:, 0:TC], s_onesf[0:1, :], s_st[0:1, 0:TC], start=True, stop=True))
            tok_br = ctx.sig("tensor", lambda t: t.matmul(
                pb[3][:, 0:TC], s_onesf[0:1, :], s_st[0:1, TC:2 * TC], start=True, stop=True))
            ctx.wait("vector", tok_bm, tok_br)
            for i in range(NT):
                sl = (i * T + c * TC, i * T + (c + 1) * TC)
                ctx.sig("vector", lambda v, sl=sl: v.tensor_tensor(
                    s_xsq[:, :], s_x[:, sl[0]:sl[1]], pb[2][:, 0:TC], ALU.subtract))
                ctx.sig("vector", lambda v, sl=sl: v.tensor_tensor(
                    s_xn[:, sl[0]:sl[1]], s_xsq[:, :], pb[3][:, 0:TC], ALU.mult))
            tok_done = ctx.now("vector")
            ctx.wait("tensor", tok_done)
            ctx.wait("scalar", tok_done)


def emit_l0_attn(ctx, nc, S):
    s_xn, s_oall, s_triu, s_onesb, pb = S["xn"], S["oall"], S["triu"], S["onesb"], S["pb"]
    s_wq, s_wk, s_wv = S["wq"], S["wk"], S["wv"]
    with nc.sbuf_tensor("at_q_%d" % ctx.uid(), [H, CH], BF16) as s_q, \
         nc.sbuf_tensor("at_kc_%d" % ctx.uid(), [CH, H], BF16) as s_kc, \
         nc.sbuf_tensor("at_vca_%d" % ctx.uid(), [CH, H + 1], BF16) as s_vca, \
         nc.sbuf_tensor("at_vt_%d" % ctx.uid(), [H, CH], BF16) as s_vt, \
         nc.sbuf_tensor("at_am_%d" % ctx.uid(), [CH, CH], BF16) as s_am, \
         nc.sbuf_tensor("at_st_%d" % ctx.uid(), [H + 1, H], BF16) as s_stz, \
         nc.sbuf_tensor("at_qz_%d" % ctx.uid(), [H, CH], BF16) as s_qz, \
         nc.sbuf_tensor("at_sc_%d" % ctx.uid(), [128, CH], F32) as s_sc, \
         nc.sbuf_tensor("at_dn_%d" % ctx.uid(), [1, 2 * CH], F32) as s_dn, \
         nc.sbuf_tensor("at_dnb_%d" % ctx.uid(), [1, CH], BF16) as s_dnb, \
         nc.sbuf_tensor("at_of_%d" % ctx.uid(), [H, CH], F32) as s_of:

        def _init(v):
            v.memset(s_stz[:, :], 0.0)
            v.memset(s_vca[:, H:H + 1], 1.0)
            return v.drain()
        ctx.op("vector", _init)

        for c in range(NCH):
            c0 = c * CH

            def phi_emit(src_pb, dst, n):
                tk1 = ctx.sig("vector", lambda v, s=src_pb, n=n: v.tensor_scalar(
                    s_sc[0:n[0], 0:n[1]], s, 0.0, None, ALU.min))
                ctx.wait("scalar", tk1)
                tk2 = ctx.sig("scalar", lambda s, n=n: s.activation(
                    s_sc[0:n[0], 0:n[1]], s_sc[0:n[0], 0:n[1]], AF.Exp))
                ctx.wait("vector", tk2)
                return ctx.sig("vector", lambda v, s=src_pb, d=dst, n=n: v.scalar_tensor_tensor(
                    d, s, 0.0, s_sc[0:n[0], 0:n[1]], ALU.max, ALU.add))

            for i in range(NT):
                xsl = s_xn[:, i * T + c0:i * T + c0 + CH]
                ctx.sig("tensor", lambda t, i=i, xs=xsl: t.matmul(
                    pb[0][0:H, 0:CH], s_wq[:, i * H:(i + 1) * H], xs,
                    start=(i == 0), stop=(i == NT - 1)))
                ctx.sig("tensor", lambda t, i=i, xs=xsl: t.matmul(
                    pb[1][0:H, 0:CH], s_wv[:, i * H:(i + 1) * H], xs,
                    start=(i == 0), stop=(i == NT - 1)))
                ctx.sig("tensor", lambda t, i=i, xs=xsl: t.matmul(
                    pb[2][0:CH, 0:H], xs, s_wk[:, i * H:(i + 1) * H],
                    start=(i == 0), stop=(i == NT - 1)))
                ctx.sig("tensor", lambda t, i=i, xs=xsl: t.matmul(
                    pb[3][0:CH, 0:H], xs, s_wv[:, i * H:(i + 1) * H],
                    start=(i == 0), stop=(i == NT - 1)))
            ctx.wait("vector", ctx.now("tensor"))
            tok_q = phi_emit(pb[0][0:H, 0:CH], s_q[:, :], (H, CH))
            tok_k = phi_emit(pb[2][0:CH, 0:H], s_kc[:, :], (CH, H))
            tok_vt = ctx.sig("vector", lambda v: v.tensor_copy(s_vt[:, :], pb[1][0:H, 0:CH]))
            tok_vc = ctx.sig("vector", lambda v: v.tensor_copy(
                s_vca[:, 0:H], pb[3][0:CH, 0:H]))

            ctx.wait("tensor", tok_q, tok_vt)
            tok_A = ctx.sig("tensor", lambda t: t.matmul(
                pb[4][0:CH, 0:CH], s_vt[:, :], s_q[:, :], start=True, stop=True))
            ctx.wait("vector", tok_A)
            tok_am = ctx.sig("vector", lambda v: v.tensor_tensor(
                s_am[:, :], pb[4][0:CH, 0:CH], s_triu[:, :], ALU.mult))

            ctx.wait("tensor", tok_am, tok_k)
            ctx.sig("tensor", lambda t: t.matmul(
                pb[4][0:H, CH:2 * CH], s_kc[:, :], s_am[:, :], start=True, stop=False))
            tok_O = ctx.sig("tensor", lambda t: t.matmul(
                pb[4][0:H, CH:2 * CH], s_stz[0:H, 0:H], s_q[:, :], start=False, stop=True))

            ctx.sig("tensor", lambda t: t.matmul(
                pb[5][0:H, 0:CH], s_kc[:, :], s_triu[:, :], start=True, stop=False))
            tok_Z = ctx.sig("tensor", lambda t: t.matmul(
                pb[5][0:H, 0:CH], s_stz[H:H + 1, 0:H], s_onesb[H:H + 1, 0:CH],
                start=False, stop=True))

            ctx.wait("tensor", tok_vc)
            tok_su = ctx.sig("tensor", lambda t, c=c: t.matmul(
                pb[6][0:H + 1, 0:H], s_vca[:, :], s_kc[:, :],
                start=(c == 0), stop=(c == NCH - 1)))

            ctx.wait("vector", tok_Z)
            tok_qz = ctx.sig("vector", lambda v: v.tensor_tensor(
                s_qz[:, :], s_q[:, :], pb[5][0:H, 0:CH], ALU.mult))
            ctx.wait("tensor", tok_qz)
            tok_dmm = ctx.sig("tensor", lambda t: t.matmul(
                pb[5][0:1, CH:2 * CH], s_onesb[0:H, 0:1], s_qz[:, :], start=True, stop=True))
            ctx.wait("vector", tok_dmm)
            tok_dn = ctx.sig("vector", lambda v: v.tensor_scalar(
                s_dn[0:1, 0:CH], pb[5][0:1, CH:2 * CH], 1e-6, None, ALU.max))
            ctx.wait("scalar", tok_dn)
            ctx.sig("scalar", lambda s: s.activation(
                s_dn[0:1, CH:2 * CH], s_dn[0:1, 0:CH], AF.Ln))
            tok_rcp = ctx.sig("scalar", lambda s: s.activation(
                s_dn[0:1, 0:CH], s_dn[0:1, CH:2 * CH], AF.Exp, scale=-1.0))
            ctx.wait("vector", tok_rcp)
            tok_dnb = ctx.sig("vector", lambda v: v.tensor_copy(s_dnb[0:1, :], s_dn[0:1, 0:CH]))
            ctx.wait("tensor", tok_dnb)
            tok_rb = ctx.sig("tensor", lambda t: t.matmul(
                pb[5][0:H, CH:2 * CH], s_onesb[0:1, 0:H], s_dnb[0:1, :], start=True, stop=True))
            ctx.wait("vector", tok_O, tok_rb)
            ctx.sig("vector", lambda v: v.tensor_copy(s_of[:, :], pb[4][0:H, CH:2 * CH]))
            ctx.op("vector", lambda v: v.drain())
            tok_of = ctx.sig("vector", lambda v, c0=c0: v.tensor_tensor(
                s_oall[:, c0:c0 + CH], s_of[:, :], pb[5][0:H, CH:2 * CH], ALU.mult))

            ctx.wait("vector", tok_su)
            tok_snap = ctx.sig("vector", lambda v: v.tensor_copy(
                s_stz[:, :], pb[6][0:H + 1, 0:H]))
            ctx.wait("tensor", tok_of, tok_snap, tok_vc, tok_k)


def emit_l0_ffn(ctx, nc, S, i_wd1, i_wd2, i_wad, i_wau, i_wbase):
    s_xn, s_x, s_oall, s_wo, pb = S["xn"], S["x"], S["oall"], S["wo"], S["pb"]
    s_onesb, s_onesf = S["onesb"], S["onesf"]
    s_bd1, s_bd2, s_bad, s_bau, s_bbase = S["bd1"], S["bd2"], S["bad"], S["bau"], S["bbase"]
    with nc.sbuf_tensor("f0_wd1_%d" % ctx.uid(), [128, 2 * NT * H0], BF16) as s_wd1, \
         nc.sbuf_tensor("f0_wd2_%d" % ctx.uid(), [128, 2 * 2 * D], BF16) as s_wd2, \
         nc.sbuf_tensor("f0_h_%d" % ctx.uid(), [128, 2 * T], BF16) as s_h, \
         nc.sbuf_tensor("f0_diff_%d" % ctx.uid(), [128, 3 * TC], F32) as s_diff, \
         nc.sbuf_tensor("f0_loss_%d" % ctx.uid(), [128, NE0 * 33], F32) as s_loss, \
         nc.sbuf_tensor("f0_l8_%d" % ctx.uid(), [128, NE0], F32) as s_l8, \
         nc.sbuf_tensor("f0_lrow_%d" % ctx.uid(), [1, 3 * NE0], F32) as s_lrow, \
         nc.sbuf_tensor("f0_lrb_%d" % ctx.uid(), [1, NE0], BF16) as s_lrb, \
         nc.sbuf_tensor("f0_ind_%d" % ctx.uid(), [128, 2 * NE0 + 1], F32) as s_ind, \
         nc.sbuf_tensor("f0_wad_%d" % ctx.uid(), [128, NE0 * NT * R0], BF16) as s_wad, \
         nc.sbuf_tensor("f0_wau_%d" % ctx.uid(), [R0, NE0 * D], BF16) as s_wau, \
         nc.sbuf_tensor("f0_wadsel_%d" % ctx.uid(), [128, NT * R0], BF16) as s_wadsel, \
         nc.sbuf_tensor("f0_wausel_%d" % ctx.uid(), [R0, D], BF16) as s_wausel, \
         nc.sbuf_tensor("f0_badsel_%d" % ctx.uid(), [R0, 2], F32) as s_badsel, \
         nc.sbuf_tensor("f0_l0b_%d" % ctx.uid(), [128, NT + NE0], F32) as s_l0b, \
         nc.sbuf_tensor("f0_abf_%d" % ctx.uid(), [R0, T], BF16) as s_abf, \
         nc.sbuf_tensor("f0_wbase_%d" % ctx.uid(), [128, NT * D], BF16) as s_wbase:

        ctx.wait("sync", ctx.now("vector"), ctx.now("tensor"), ctx.now("scalar"))
        tok_wd = []
        for e in range(2):
            tok_wd.append(ctx.sig("sync", lambda y, e=e: y.dma_start(
                out=_r(s_wd1[:, e * NT * H0:(e + 1) * NT * H0], "p (c m) -> p c m", c=NT),
                in_=_r(i_wd1[e], "(c p) m -> p c m", p=128)), dma=True))
            tok_wd.append(ctx.sig("sync", lambda y, e=e: y.dma_start(
                out=_r(s_wd2[:, e * 2 * D:(e + 1) * 2 * D], "p (c m) -> p c m", c=2),
                in_=_r(i_wd2[e], "(c p) m -> p c m", p=128)), dma=True))
        tok_wad = ctx.sig("sync", lambda y: y.dma_start(
            out=_r(s_wad[:, :], "p (e c m) -> p e c m", e=NE0, c=NT),
            in_=_r(i_wad, "e (c p) m -> p e c m", p=128)), dma=True)
        tok_wau = ctx.sig("sync", lambda y: y.dma_start(
            out=_r(s_wau[:, :], "p (e m) -> p e m", e=NE0),
            in_=_r(i_wau, "e p m -> p e m")), dma=True)
        tok_wbase = ctx.sig("sync", lambda y: y.dma_start(
            out=_r(s_wbase[:, :], "p (c m) -> p c m", c=NT),
            in_=_r(i_wbase, "(c p) m -> p c m", p=128)), dma=True)

        diff_par = [None, None]
        for e in range(NE0):
            sl = e % 2
            base1 = sl * NT * H0
            base2 = sl * 2 * D
            ctx.wait("tensor", tok_wd[2 * e], tok_wd[2 * e + 1])
            for it in range(2):
                for c in range(NTC):
                    for k in range(NT):
                        ctx.sig("tensor", lambda t, k=k, it=it, c=c, b1=base1: t.matmul(
                            pb[0][:, 0:TC],
                            s_wd1[:, b1 + k * H0 + it * 128: b1 + k * H0 + (it + 1) * 128],
                            s_xn[:, k * T + c * TC:k * T + (c + 1) * TC],
                            start=(k == 0), stop=(k == NT - 1)))
                    tok_h = ctx.now("tensor")
                    ctx.wait("scalar", tok_h)
                    tok_hr = ctx.sig("scalar", lambda s, it=it, c=c, e=e: s.activation(
                        s_h[:, it * T + c * TC:it * T + (c + 1) * TC], pb[0][:, 0:TC],
                        AF.Relu, bias=s_bd1[:, e * 2 + it:e * 2 + it + 1]))
                    ctx.wait("tensor", tok_hr)
            tok_hdone = ctx.now("scalar")
            for dt in range(NT):
                for c in range(NTC):
                    ctx.wait("tensor", tok_hdone)
                    for k2 in range(2):
                        ctx.sig("tensor", lambda t, k2=k2, dt=dt, c=c, b2=base2: t.matmul(
                            pb[1][:, 0:TC],
                            s_wd2[:, b2 + k2 * D + dt * 128: b2 + k2 * D + (dt + 1) * 128],
                            s_h[:, k2 * T + c * TC:k2 * T + (c + 1) * TC],
                            start=(k2 == 0), stop=(k2 == 1)))
                    tok_rc = ctx.now("tensor")
                    slot = (dt * NTC + c) % 2
                    ctx.wait("vector", tok_rc, diff_par[slot])
                    tok_df = ctx.sig("vector", lambda v, dt=dt, c=c, e=e, slot=slot:
                                     v.scalar_tensor_tensor(
                        s_diff[:, slot * TC:(slot + 1) * TC], pb[1][:, 0:TC],
                        s_bd2[:, e * NT + dt:e * NT + dt + 1],
                        s_xn[:, dt * T + c * TC:dt * T + (c + 1) * TC],
                        ALU.add, ALU.subtract))
                    ctx.wait("tensor", tok_df)
                    ctx.wait("scalar", tok_df)
                    diff_par[slot] = ctx.sig("scalar", lambda s, e=e, dt=dt, c=c, slot=slot:
                                             s.activation(
                        s_diff[:, 2 * TC:3 * TC], s_diff[:, slot * TC:(slot + 1) * TC],
                        AF.Square, accum_out=s_loss[:, e * 33 + dt * NTC + c:
                                                    e * 33 + dt * NTC + c + 1]))
            if e + 2 < NE0:
                ctx.wait("sync", ctx.now("tensor"))
                tok_wd.append(ctx.sig("sync", lambda y, e=e, sl=sl: y.dma_start(
                    out=_r(s_wd1[:, sl * NT * H0:(sl + 1) * NT * H0], "p (c m) -> p c m", c=NT),
                    in_=_r(i_wd1[e + 2], "(c p) m -> p c m", p=128)), dma=True))
                tok_wd.append(ctx.sig("sync", lambda y, e=e, sl=sl: y.dma_start(
                    out=_r(s_wd2[:, sl * 2 * D:(sl + 1) * 2 * D], "p (c m) -> p c m", c=2),
                    in_=_r(i_wd2[e + 2], "(c p) m -> p c m", p=128)), dma=True))
            ctx.wait("vector", diff_par[0], diff_par[1])
            ctx.op("vector", lambda v: v.drain())
            ctx.sig("vector", lambda v, e=e: v.tensor_reduce(
                s_l8[:, e:e + 1], s_loss[:, e * 33:e * 33 + 32], AX.X, ALU.add))
        ctx.auto_drain = True
        tok_l8 = ctx.now("vector")
        ctx.wait("tensor", tok_l8)
        tok_lmm = ctx.sig("tensor", lambda t: t.matmul(
            pb[2][0:1, 0:NE0], s_onesf[:, 0:1], s_l8[:, :], start=True, stop=True))
        ctx.wait("vector", tok_lmm)
        tok_lrow = ctx.sig("vector", lambda v: v.tensor_copy(
            s_lrow[0:1, 0:NE0], pb[2][0:1, 0:NE0]))
        ctx.wait("tensor", tok_lrow)
        tok_lbc = ctx.sig("tensor", lambda t: t.matmul(
            pb[3][:, 0:NE0], s_onesf[0:1, :], s_lrow[0:1, 0:NE0], start=True, stop=True))
        ctx.wait("vector", tok_lbc, tok_wad, tok_wau)
        ctx.sig("vector", lambda v: v.tensor_copy(
            s_ind[:, NE0:2 * NE0], pb[3][:, 0:NE0]))
        ctx.op("vector", lambda v: v.drain())
        ctx.sig("vector", lambda v: v.tensor_reduce(
            s_ind[:, 2 * NE0:2 * NE0 + 1], s_ind[:, NE0:2 * NE0], AX.X, ALU.min))
        ctx.op("vector", lambda v: v.drain())
        tok_indbc = ctx.sig("vector", lambda v: v.tensor_scalar(
            s_ind[:, 0:NE0], s_ind[:, NE0:2 * NE0],
            s_ind[:, 2 * NE0:2 * NE0 + 1], None, ALU.is_equal))

        ctx.op("vector", lambda v: v.drain())
        for kc in range(NT):
            ctx.sig("vector", lambda v, kc=kc: v.tensor_scalar(
                s_wadsel[:, kc * R0:(kc + 1) * R0],
                s_wad[:, (0 * NT + kc) * R0:(0 * NT + kc + 1) * R0],
                s_ind[:, 0:1], None, ALU.mult))
            for e in range(1, NE0):
                ctx.op("vector", lambda v: v.drain())
                ctx.sig("vector", lambda v, kc=kc, e=e: v.scalar_tensor_tensor(
                    s_wadsel[:, kc * R0:(kc + 1) * R0],
                    s_wad[:, (e * NT + kc) * R0:(e * NT + kc + 1) * R0],
                    s_ind[:, e:e + 1], s_wadsel[:, kc * R0:(kc + 1) * R0],
                    ALU.mult, ALU.add))
        ctx.sig("vector", lambda v: v.tensor_scalar(
            s_wausel[:, :], s_wau[:, 0:D], s_ind[0:R0, 0:1], None, ALU.mult))
        for e in range(1, NE0):
            ctx.op("vector", lambda v: v.drain())
            ctx.sig("vector", lambda v, e=e: v.scalar_tensor_tensor(
                s_wausel[:, :], s_wau[:, e * D:(e + 1) * D],
                s_ind[0:R0, e:e + 1], s_wausel[:, :], ALU.mult, ALU.add))
        ctx.sig("vector", lambda v: v.tensor_scalar(
            s_badsel[:, 0:1], s_bad[:, 0:1], s_ind[0:R0, 0:1], None, ALU.mult))
        for e in range(1, NE0):
            ctx.op("vector", lambda v: v.drain())
            ctx.sig("vector", lambda v, e=e: v.scalar_tensor_tensor(
                s_badsel[:, 0:1], s_bad[:, e:e + 1], s_ind[0:R0, e:e + 1],
                s_badsel[:, 0:1], ALU.mult, ALU.add))
        for dt in range(NT):
            ctx.sig("vector", lambda v, dt=dt: v.tensor_tensor(
                s_l0b[:, NT:NT + NE0],
                _r(s_bau[:, :], "p (e c) -> p e c", e=NE0)[:, :, dt],
                s_ind[:, 0:NE0], ALU.mult))
            ctx.op("vector", lambda v: v.drain())
            ctx.sig("vector", lambda v, dt=dt: v.tensor_reduce(
                s_l0b[:, dt:dt + 1], s_l0b[:, NT:NT + NE0], AX.X, ALU.add))
            ctx.op("vector", lambda v: v.drain())
            ctx.sig("vector", lambda v, dt=dt: v.tensor_tensor(
                s_l0b[:, dt:dt + 1], s_l0b[:, dt:dt + 1],
                s_bbase[:, dt:dt + 1], ALU.add))
        tok_sel = ctx.now("vector")

        S2_mktap[0]("ffn_dbg", (128, 96))
        S2_mktap[0]("ffn_h7", (128, 2 * T), BF16)
        if "ffn_dbg" in ctx.debug_taps and getattr(ctx, "cur_b", 0) == 0:
            with nc.sbuf_tensor("f0_dbg_%d" % ctx.uid(), [128, 96], F32) as s_dbg_:
                ctx.sig("vector", lambda v: v.tensor_copy(s_dbg_[:, 0:33], s_loss[:, 0 * 33:1 * 33]))
                ctx.sig("vector", lambda v: v.tensor_copy(s_dbg_[:, 33:41], s_l8[:, :]))
                ctx.sig("vector", lambda v: v.tensor_copy(s_dbg_[:, 41:49], s_ind[:, 0:NE0]))
                ctx.sig("vector", lambda v: v.tensor_copy(s_dbg_[:, 49:57], s_l0b[:, 0:8]))
                ctx.sig("vector", lambda v: v.tensor_copy(s_dbg_[0:1, 57:57 + 24], s_lrow[0:1, :]))
                ctx.sig("vector", lambda v: v.tensor_copy(s_dbg_[:, 81:90], s_ind[:, NE0:2 * NE0 + 1]))
                S2_dump[0]("ffn_dbg", lambda: s_dbg_[:, :])
                S2_dump[0]("ffn_h7", lambda: s_h[:, :])
        ctx.wait("tensor", tok_sel)
        for c in range(NTC):
            for k in range(NT):
                ctx.sig("tensor", lambda t, k=k, c=c: t.matmul(
                    pb[3][0:R0, 0:TC], s_wadsel[:, k * R0:(k + 1) * R0],
                    s_xn[:, k * T + c * TC:k * T + (c + 1) * TC],
                    start=(k == 0), stop=(k == NT - 1)))
            tok_a = ctx.now("tensor")
            ctx.wait("scalar", tok_a)
            tok_ar = ctx.sig("scalar", lambda s, c=c: s.activation(
                s_abf[:, c * TC:(c + 1) * TC], pb[3][0:R0, 0:TC],
                AF.Relu, bias=s_badsel[:, 0:1]))
            ctx.wait("tensor", tok_ar)

        ctx.wait("tensor", tok_wbase)
        evac = [None, None]
        for dt in range(NT):
            for c in range(NTC):
                bnk = (dt * NTC + c) % 2
                p = pb[4 + bnk]
                ctx.wait("tensor", evac[bnk])
                ctx.sig("tensor", lambda t, dt=dt, c=c, p=p: t.matmul(
                    p[:, 0:TC], s_wo[:, dt * 128:(dt + 1) * 128],
                    s_oall[:, c * TC:(c + 1) * TC], start=True, stop=False))
                for k in range(NT):
                    ctx.sig("tensor", lambda t, k=k, dt=dt, c=c, p=p: t.matmul(
                        p[:, 0:TC], s_wbase[:, k * D + dt * 128:k * D + (dt + 1) * 128],
                        s_xn[:, k * T + c * TC:k * T + (c + 1) * TC],
                        start=False, stop=False))
                tok_mm = ctx.sig("tensor", lambda t, dt=dt, c=c, p=p: t.matmul(
                    p[:, 0:TC], s_wausel[:, dt * 128:(dt + 1) * 128],
                    s_abf[:, c * TC:(c + 1) * TC], start=False, stop=True))
                ctx.wait("vector", tok_mm)
                evac[bnk] = ctx.sig("vector", lambda v, dt=dt, c=c, p=p:
                                    v.scalar_tensor_tensor(
                    s_x[:, dt * T + c * TC:dt * T + (c + 1) * TC], p[:, 0:TC],
                    s_l0b[:, dt:dt + 1],
                    s_x[:, dt * T + c * TC:dt * T + (c + 1) * TC], ALU.add, ALU.add))
        ctx.wait("tensor", evac[0], evac[1])


def emit_l1_ssm(ctx, nc, S, i_winT, i_wxT, i_wdtT, i_woutT):
    s_xn, s_x, pb = S["xn"], S["x"], S["pb"]
    s_binc, s_bxc, s_bdtc, s_A, s_dc, s_boutc = (S["binc"], S["bxc"], S["bdtc"],
                                                 S["A"], S["dc"], S["boutc"])
    s_hstate, s_kone, s_onesb = S["hstate"], S["kone"], S["onesb"]
    es2 = ExitStack()
    s_win = es2.enter_context(nc.sbuf_tensor("sm_win_%d" % ctx.uid(), [128, NT * 128], BF16))
    s_wxT = es2.enter_context(nc.sbuf_tensor("sm_wxT_%d" % ctx.uid(), [128, NDI * 128], BF16))
    s_wdtT = es2.enter_context(nc.sbuf_tensor("sm_wdtT_%d" % ctx.uid(), [DTR, DI], BF16))
    s_wout = es2.enter_context(nc.sbuf_tensor("sm_wout_%d" % ctx.uid(), [128, NDI * 128], BF16))
    s_xs = es2.enter_context(nc.sbuf_tensor("sm_xs_%d" % ctx.uid(), [128, NDI * SEG], BF16))
    s_z = es2.enter_context(nc.sbuf_tensor("sm_z_%d" % ctx.uid(), [128, NDI * SEG], BF16))
    s_dl = es2.enter_context(nc.sbuf_tensor("sm_dl_%d" % ctx.uid(), [128, NDI * SEG], BF16))
    s_dx = es2.enter_context(nc.sbuf_tensor("sm_dx_%d" % ctx.uid(), [128, NDI * SEG], BF16))
    s_y = es2.enter_context(nc.sbuf_tensor("sm_y_%d" % ctx.uid(), [128, NDI * SEG], F32))
    s_draw = es2.enter_context(nc.sbuf_tensor("sm_draw_%d" % ctx.uid(), [DTR, SEG], BF16))
    s_bc = es2.enter_context(nc.sbuf_tensor("sm_bc_%d" % ctx.uid(), [R0, SEG], F32))
    s_bcb = es2.enter_context(nc.sbuf_tensor("sm_bcb_%d" % ctx.uid(), [R0, SEG], BF16))
    s_flat = es2.enter_context(nc.sbuf_tensor("sm_flat_%d" % ctx.uid(), [65, NS * SEG], BF16))
    s_ab = es2.enter_context(nc.sbuf_tensor("sm_ab_%d" % ctx.uid(), [128, 2 * SEG], F32))
    s_abig = es2.enter_context(nc.sbuf_tensor("sm_abig_%d" % ctx.uid(), [128, 8 * SEG], F32))
    s_bbig = es2.enter_context(nc.sbuf_tensor("sm_bbig_%d" % ctx.uid(), [128, 8 * SEG], BF16))
    s_hbig = es2.enter_context(nc.sbuf_tensor("sm_hbig_%d" % ctx.uid(), [128, 8 * SEG], F32))
    s_g1 = es2.enter_context(nc.sbuf_tensor("sm_g1_%d" % ctx.uid(), [128, SEG], F32))
    s_Bs = es2.enter_context(nc.sbuf_tensor("sm_Bs_%d" % ctx.uid(), [128, SEG], BF16))
    s_Cs = es2.enter_context(nc.sbuf_tensor("sm_Cs_%d" % ctx.uid(), [128, SEG], BF16))
    if True:
        s_yg = s_abig[:, :].bitcast(BF16)[:, 0:NDI * SEG]
        s_fix = s_g1

        ctx.wait("sync", ctx.now("vector"), ctx.now("tensor"), ctx.now("scalar"))
        tok_wx = ctx.sig("sync", lambda y: y.dma_start(
            out=_r(s_wxT[:, :], "p (c m) -> p c m", c=NDI),
            in_=_r(i_wxT, "(c p) m -> p c m", p=128)), dma=True)
        tok_wdt = ctx.sig("sync", lambda y: y.dma_start(
            out=s_wdtT[:, :], in_=i_wdtT), dma=True)
        def _h0(v):
            v.memset(s_hstate[:, :], 0.0)
            return v.drain()
        ctx.op("vector", _h0)

        win_tok = [None, None]
        win_used = [None, None]
        wout_tok = [None, None]
        wout_used = [None, None]
        a_toks = [None, None]
        a_free = [None, None]
        gp_hc = [None, None]
        bc_free = None

        for sg in range(NSEG):
            t0 = sg * SEG
            for m in range(32):
                ctx.wait("sync", win_used[0])
                win_tok[0] = ctx.sig("sync", lambda y, m=m: y.dma_start(
                    out=_r(s_win[:, :], "p (c m) -> p c m", c=NT),
                    in_=_r(i_winT[:, m * 128:(m + 1) * 128], "(c p) m -> p c m", p=128)),
                    dma=True)
                ctx.wait("tensor", win_tok[0])
                for k in range(NT):
                    ctx.sig("tensor", lambda t, k=k, t0=t0: t.matmul(
                        pb[0][:, 0:SEG],
                        s_win[:, k * 128:(k + 1) * 128],
                        s_xn[:, k * T + t0:k * T + t0 + SEG],
                        start=(k == 0), stop=(k == NT - 1)))
                win_used[0] = ctx.now("tensor")
                ctx.wait("vector", win_used[0])
                if m < NDI:
                    ctx.sig("vector", lambda v, m=m: v.tensor_scalar(
                        s_xs[:, m * SEG:(m + 1) * SEG], pb[0][:, 0:SEG],
                        s_binc[:, m:m + 1], None, ALU.add))
                    tok_ev = ctx.sig("vector", lambda v, m=m: v.tensor_scalar(
                        s_y[:, m * SEG:(m + 1) * SEG], pb[0][:, 0:SEG],
                        s_binc[:, m:m + 1], s_dc[:, m:m + 1], ALU.add, ALU.mult))
                else:
                    tok_ev = ctx.sig("vector", lambda v, m=m: v.tensor_scalar(
                        s_z[:, (m - NDI) * SEG:(m - NDI + 1) * SEG], pb[0][:, 0:SEG],
                        s_binc[:, m:m + 1], None, ALU.add))
                ctx.wait("tensor", tok_ev)
            ctx.wait("tensor", tok_wx)
            for k in range(NDI):
                ctx.sig("tensor", lambda t, k=k: t.matmul(
                    pb[1][:, 0:SEG], s_wxT[:, k * 128:(k + 1) * 128],
                    s_xs[:, k * SEG:(k + 1) * SEG], start=(k == 0), stop=(k == NDI - 1)))
            tok_dbc = ctx.now("tensor")
            ctx.wait("vector", tok_dbc)
            ctx.sig("vector", lambda v: v.tensor_scalar(
                s_draw[:, :], pb[1][0:DTR, 0:SEG], s_bxc[0:DTR, 0:1], None, ALU.add))
            ctx.sig("vector", lambda v: v.tensor_scalar(
                s_bc[:, :], pb[1][DTR:128, 0:SEG], s_bxc[DTR:128, 0:1], None, ALU.add))
            ctx.op("vector", lambda v: v.drain())
            tok_bc = ctx.sig("vector", lambda v: v.tensor_copy(s_bcb[:, :], s_bc[:, :]))
            ctx.wait("tensor", tok_bc)
            ctx.wait("sync", tok_bc)
            tok_fb = ctx.sig("sync", lambda y: y.dma_start(
                out=_r(s_flat[0:1, 0:NS * SEG], "a (n t) -> a n t", n=NS),
                in_=s_bcb[0:NS, :]), dma=True)
            tok_fc = ctx.sig("sync", lambda y: y.dma_start(
                out=_r(s_flat[64:65, 0:NS * SEG], "a (n t) -> a n t", n=NS),
                in_=s_bcb[32:32 + NS, :]), dma=True)
            ctx.wait("tensor", tok_wdt)
            dl_done = None
            for dt in range(NDI):
                ctx.wait("tensor", dl_done)
                tok_dm = ctx.sig("tensor", lambda t, dt=dt: t.matmul(
                    pb[2][:, 0:SEG], s_wdtT[:, dt * 128:(dt + 1) * 128],
                    s_draw[:, :], start=True, stop=True))
                ctx.wait("scalar", tok_dm)
                ctx.sig("scalar", lambda s, dt=dt: s.activation(
                    s_ab[:, 0:SEG], pb[2][:, 0:SEG], AF.Abs,
                    bias=s_bdtc[:, dt:dt + 1]))
                ctx.sig("scalar", lambda s: s.activation(
                    s_ab[:, 0:SEG], s_ab[:, 0:SEG], AF.Exp, scale=-1.0))
                ctx.sig("scalar", lambda s: s.activation(
                    s_ab[:, 0:SEG], s_ab[:, 0:SEG], AF.Ln, bias=s_kone[:, 0:1]))
                tok_sp = ctx.sig("scalar", lambda s, dt=dt: s.activation(
                    s_ab[:, SEG:2 * SEG], pb[2][:, 0:SEG], AF.Relu,
                    bias=s_bdtc[:, dt:dt + 1]))
                ctx.wait("vector", tok_sp)
                dl_done = ctx.sig("vector", lambda v, dt=dt: v.tensor_tensor(
                    s_dl[:, dt * SEG:(dt + 1) * SEG], s_ab[:, 0:SEG],
                    s_ab[:, SEG:2 * SEG], ALU.add))
                ctx.op("vector", lambda v: v.drain())
                ctx.sig("vector", lambda v, dt=dt: v.tensor_tensor(
                    s_dx[:, dt * SEG:(dt + 1) * SEG],
                    s_dl[:, dt * SEG:(dt + 1) * SEG],
                    s_xs[:, dt * SEG:(dt + 1) * SEG], ALU.mult))
                ctx.wait("scalar", dl_done)
            tok_dl = ctx.now("vector")

            ctx.wait("tensor", tok_fb, tok_fc, tok_dl)
            ctx.wait("gpsimd", tok_dl)
            for n in range(NS):
                ctx.wait("tensor", bc_free)
                tok_Bb = ctx.sig("tensor", lambda t, n=n: t.matmul(
                    pb[3][:, 0:SEG], s_onesb[0:1, :],
                    s_flat[0:1, n * SEG:(n + 1) * SEG], start=True, stop=True))
                tok_Cb = ctx.sig("tensor", lambda t, n=n: t.matmul(
                    pb[4][:, 0:SEG], s_onesb[64:65, :],
                    s_flat[64:65, n * SEG:(n + 1) * SEG],
                    start=True, stop=True))
                ctx.wait("vector", tok_Bb, tok_Cb)
                ctx.sig("vector", lambda v: v.tensor_copy(s_Bs[:, :], pb[3][:, 0:SEG]))
                tok_bc_ev = ctx.sig("vector", lambda v: v.tensor_copy(s_Cs[:, :], pb[4][:, 0:SEG]))
                bc_free = tok_bc_ev
                for hf in range(2):
                    ctx.wait("scalar", a_free[0])
                    for j in range(8):
                        dt = hf * 8 + j
                        a_toks[0] = ctx.sig("scalar", lambda s, dt=dt, n=n, j=j: s.activation(
                            s_abig[:, j * SEG:(j + 1) * SEG],
                            s_dl[:, dt * SEG:(dt + 1) * SEG],
                            AF.Exp, scale=s_A[:, dt * NS + n:dt * NS + n + 1]))
                    ctx.wait("gpsimd", tok_bc_ev, a_free[0])
                    tok_gb = ctx.sig("gpsimd", lambda g, hf=hf: g.tensor_tensor(
                        _r(s_bbig[:, :], "p (c t) -> p c t", c=8),
                        _r(s_dx[:, hf * 8 * SEG:(hf + 1) * 8 * SEG], "p (c t) -> p c t", c=8),
                        s_Bs[:, None, 0:SEG].broadcast_to([128, 8, SEG]), ALU.mult))
                    a_first = _r(s_abig[:, :], "p (c t) -> p c t", c=8)[:, 1:8, 0:1]
                    b_first = _r(s_bbig[:, :], "p (c t) -> p c t", c=8)[:, 1:8, 0:1]
                    hs_v = s_hstate[:, n * NDI + hf * 8 + 1:n * NDI + hf * 8 + 8, None]
                    ctx.wait("vector", a_toks[0], tok_gb, gp_hc[hf])
                    ctx.sig("vector", lambda v, a=a_first, h=hs_v: v.tensor_tensor(
                        s_fix[:, 0:7, None], a, h, ALU.mult))
                    ctx.op("vector", lambda v: v.drain())
                    ctx.sig("vector", lambda v, b=b_first: v.tensor_tensor(
                        b, b, s_fix[:, 0:7, None], ALU.add))
                    def _za(v, a=a_first):
                        v.memset(a, 0.0)
                        return v.drain()
                    ctx.op("vector", _za)
                    tok_scan = ctx.sig("vector", lambda v, hf=hf, n=n: v.tensor_tensor_scan(
                        s_hbig[:, :], s_abig[:, :], s_bbig[:, :],
                        s_hstate[:, n * NDI + hf * 8:n * NDI + hf * 8 + 1],
                        ALU.mult, ALU.add))
                    a_free[0] = tok_scan
                    ctx.wait("gpsimd", tok_scan)
                    ctx.sig("gpsimd", lambda g, hf=hf, n=n: g.tensor_copy(
                        s_hstate[:, n * NDI + hf * 8:n * NDI + hf * 8 + 8],
                        _r(s_hbig[:, :], "p (c t) -> p c t", c=8)[:, :, SEG - 1:SEG].opt()))
                    ctx.sig("gpsimd", lambda g, hf=hf: g.tensor_tensor(
                        _r(s_xs[:, hf * 8 * SEG:(hf + 1) * 8 * SEG], "p (c t) -> p c t", c=8),
                        _r(s_hbig[:, :], "p (c t) -> p c t", c=8),
                        s_Cs[:, None, 0:SEG].broadcast_to([128, 8, SEG]), ALU.mult))
                    gp_hc[hf] = ctx.sig("gpsimd", lambda g, hf=hf: g.tensor_tensor(
                        s_y[:, hf * 8 * SEG:(hf + 1) * 8 * SEG],
                        s_y[:, hf * 8 * SEG:(hf + 1) * 8 * SEG],
                        s_xs[:, hf * 8 * SEG:(hf + 1) * 8 * SEG], ALU.add))
                    ctx.wait("vector", gp_hc[hf])
            ctx.wait("vector", gp_hc[0], gp_hc[1])
            ctx.wait("scalar", gp_hc[0], gp_hc[1])
            ctx.wait("scalar", ctx.now("vector"))
            for dt in range(NDI):
                tok_tz = ctx.sig("scalar", lambda s, dt=dt: s.activation(
                    s_g1[:, 0:SEG], s_z[:, dt * SEG:(dt + 1) * SEG], AF.Tanh, scale=0.5))
                ctx.wait("vector", tok_tz)
                ctx.sig("vector", lambda v, dt=dt: v.scalar_tensor_tensor(
                    s_g1[:, 0:SEG], s_g1[:, 0:SEG], 1.0,
                    s_z[:, dt * SEG:(dt + 1) * SEG], ALU.add, ALU.mult))
                ctx.op("vector", lambda v: v.drain())
                tok_yg = ctx.sig("vector", lambda v, dt=dt: v.scalar_tensor_tensor(
                    s_yg[:, dt * SEG:(dt + 1) * SEG], s_y[:, dt * SEG:(dt + 1) * SEG],
                    0.5, s_g1[:, 0:SEG], ALU.mult, ALU.mult))
                ctx.wait("scalar", tok_yg)
            tok_yg_all = ctx.now("vector")
            if getattr(ctx, "cur_b", 0) == 0 and sg == 0:
                S2_mktap[0]("ssm_yg", (128, NDI * SEG), BF16)
                S2_dump[0]("ssm_yg", lambda: s_yg[:, :])
            ctx.wait("tensor", tok_yg_all)
            for dp in range(NT):
                ctx.wait("sync", wout_used[0])
                wout_tok[0] = ctx.sig("sync", lambda y, dp=dp: y.dma_start(
                    out=_r(s_wout[:, :], "p (c m) -> p c m", c=NDI),
                    in_=_r(i_woutT[:, dp * 128:(dp + 1) * 128], "(c p) m -> p c m", p=128)),
                    dma=True)
                ctx.wait("tensor", wout_tok[0])
                for k in range(NDI):
                    ctx.sig("tensor", lambda t, k=k: t.matmul(
                        pb[5][:, 0:SEG],
                        s_wout[:, k * 128:(k + 1) * 128],
                        s_yg[:, k * SEG:(k + 1) * SEG], start=(k == 0), stop=(k == NDI - 1)))
                wout_used[0] = ctx.now("tensor")
                ctx.wait("vector", wout_used[0])
                tok_xev = ctx.sig("vector", lambda v, dp=dp, t0=t0: v.scalar_tensor_tensor(
                    s_x[:, dp * T + t0:dp * T + t0 + SEG], pb[5][:, 0:SEG],
                    s_boutc[:, dp:dp + 1],
                    s_x[:, dp * T + t0:dp * T + t0 + SEG], ALU.add, ALU.add))
                ctx.wait("tensor", tok_xev)
            tok_seg = ctx.now("vector")
            ctx.wait("tensor", tok_seg)
            ctx.wait("scalar", tok_seg)
            ctx.wait("scalar", wout_used[0])
            ctx.wait("vector", wout_used[0])


    es2.close()


def emit_l1_moe(ctx, nc, S, i_gwT, i_uwT, i_dwT, i_sgwT, i_suwT, i_sdwT):
    s_xn, s_x, pb, pbh = S["xn"], S["x"], S["pb"], S["pbh"]
    s_rwT, s_rbb, s_onesb, s_onesf, s_ident, s_kone = (S["rwT"], S["rbb"], S["onesb"],
                                                       S["onesf"], S["ident"], S["kone"])
    es_moe = ExitStack()
    s_wTb = es_moe.enter_context(
        nc.sbuf_tensor("mo_wTb_%d" % ctx.uid(), [NE1, T], BF16))
    s_wrow = es_moe.enter_context(
        nc.sbuf_tensor("mo_wrow_%d" % ctx.uid(), [1, T], BF16))
    with nc.sbuf_tensor("mo_ro_%d" % ctx.uid(), [128, 16 * NE1], F32) as s_ro, \
         nc.sbuf_tensor("mo_wdb_%d" % ctx.uid(), [128, NE1], BF16) as s_wdb:
        C0 = 0
        def cols(j):
            return slice(j * NE1, (j + 1) * NE1)
        ctx.auto_drain = True
        for tile in range(16):
            tt = tile * 128
            for k in range(NT):
                ctx.sig("tensor", lambda t, k=k, tt=tt: t.matmul(
                    pb[0][:, 0:NE1], s_xn[:, k * T + tt:k * T + tt + 128],
                    s_rwT[:, k * NE1:(k + 1) * NE1], start=(k == 0), stop=(k == NT - 1)))
            tok_lg = ctx.now("tensor")
            ctx.wait("scalar", tok_lg)
            tok_th = ctx.sig("scalar", lambda s: s.activation(
                s_ro[:, cols(0)], pb[0][:, 0:NE1], AF.Tanh, scale=0.5))
            ctx.wait("vector", tok_th)
            ctx.sig("vector", lambda v: v.tensor_scalar(
                s_ro[:, cols(1)], s_ro[:, cols(0)], 1.0, 0.5, ALU.add, ALU.mult))
            tok_sfc = ctx.sig("vector", lambda v: v.tensor_tensor(
                s_ro[:, cols(2)], s_ro[:, cols(1)], s_rbb[:, :], ALU.add))
            ctx.wait("tensor", tok_sfc)
            mcol = 10 * NE1
            for g in range(4):
                gsl = slice(2 * NE1 + 8 * g, 2 * NE1 + 8 * (g + 1))
                ctx.op("vector", lambda v: v.drain())
                ctx.sig("vector", lambda v, gsl=gsl, mcol=mcol: v.tensor_reduce(
                    s_ro[:, mcol:mcol + 1], s_ro[:, gsl], AX.X, ALU.max))
                ctx.op("vector", lambda v: v.drain())
                ctx.sig("vector", lambda v, gsl=gsl, mcol=mcol: v.tensor_scalar(
                    s_ro[:, cols(6)][:, 0:8], s_ro[:, gsl], s_ro[:, mcol:mcol + 1],
                    None, ALU.is_equal))
                ctx.op("vector", lambda v: v.drain())
                ctx.sig("vector", lambda v, gsl=gsl: v.scalar_tensor_tensor(
                    s_ro[:, cols(6)][:, 8:16], s_ro[:, cols(6)][:, 0:8], -1e30,
                    s_ro[:, gsl], ALU.mult, ALU.add))
                ctx.op("vector", lambda v: v.drain())
                ctx.sig("vector", lambda v, mcol=mcol: v.tensor_reduce(
                    s_ro[:, mcol + 1:mcol + 2], s_ro[:, cols(6)][:, 8:16], AX.X, ALU.max))
                ctx.op("vector", lambda v: v.drain())
                ctx.sig("vector", lambda v, g=g, mcol=mcol: v.tensor_tensor(
                    s_ro[:, 3 * NE1 + g:3 * NE1 + g + 1], s_ro[:, mcol:mcol + 1],
                    s_ro[:, mcol + 1:mcol + 2], ALU.add))
            gsl4 = slice(3 * NE1, 3 * NE1 + 4)
            ctx.op("vector", lambda v: v.drain())
            ctx.sig("vector", lambda v: v.tensor_reduce(
                s_ro[:, mcol:mcol + 1], s_ro[:, gsl4], AX.X, ALU.max))
            ctx.op("vector", lambda v: v.drain())
            ctx.sig("vector", lambda v: v.tensor_scalar(
                s_ro[:, cols(6)][:, 0:4], s_ro[:, gsl4], s_ro[:, mcol:mcol + 1],
                None, ALU.is_equal))
            ctx.op("vector", lambda v: v.drain())
            ctx.sig("vector", lambda v: v.scalar_tensor_tensor(
                s_ro[:, cols(6)][:, 8:12], s_ro[:, cols(6)][:, 0:4], -1e30,
                s_ro[:, gsl4], ALU.mult, ALU.add))
            ctx.op("vector", lambda v: v.drain())
            ctx.sig("vector", lambda v: v.tensor_reduce(
                s_ro[:, mcol + 1:mcol + 2], s_ro[:, cols(6)][:, 8:12], AX.X, ALU.max))
            ctx.op("vector", lambda v: v.drain())
            ctx.sig("vector", lambda v: v.tensor_scalar(
                s_ro[:, 3 * NE1 + 8:3 * NE1 + 12], s_ro[:, gsl4],
                s_ro[:, mcol + 1:mcol + 2], None, ALU.is_ge))
            ctx.sig("vector", lambda v: v.tensor_tensor(
                _r(s_ro[:, cols(4)], "p (g j) -> p g j", g=4),
                _r(s_ro[:, cols(2)], "p (g j) -> p g j", g=4),
                s_ro[:, 3 * NE1 + 8:3 * NE1 + 12][:, :, None].broadcast_to([128, 4, 8]),
                ALU.mult))
            ctx.sig("vector", lambda v: v.tensor_copy(s_ro[:, cols(5)], s_ro[:, cols(4)]))
            ctx.op("vector", lambda v: v.drain())
            def _z(v):
                v.memset(s_ro[:, cols(8)], 0.0)
                return v.drain()
            ctx.op("vector", _z)
            for i4 in range(4):
                ctx.op("vector", lambda v: v.drain())
                ctx.sig("vector", lambda v, mcol=mcol: v.tensor_reduce(
                    s_ro[:, mcol + 2:mcol + 3], s_ro[:, cols(5)], AX.X, ALU.max))
                ctx.op("vector", lambda v: v.drain())
                ctx.sig("vector", lambda v, mcol=mcol: v.tensor_scalar(
                    s_ro[:, cols(6)], s_ro[:, cols(5)], s_ro[:, mcol + 2:mcol + 3],
                    None, ALU.is_equal))
                ctx.op("vector", lambda v: v.drain())
                ctx.sig("vector", lambda v: v.tensor_tensor_scan(
                    s_ro[:, cols(7)], s_onesf[:, 0:NE1], s_ro[:, cols(6)],
                    0.0, ALU.mult, ALU.add))
                ctx.op("vector", lambda v: v.drain())
                ctx.sig("vector", lambda v: v.tensor_scalar(
                    s_ro[:, cols(7)], s_ro[:, cols(7)], 1.0, None, ALU.is_equal))
                ctx.op("vector", lambda v: v.drain())
                ctx.sig("vector", lambda v: v.tensor_tensor(
                    s_ro[:, cols(6)], s_ro[:, cols(6)], s_ro[:, cols(7)], ALU.mult))
                ctx.op("vector", lambda v: v.drain())
                ctx.sig("vector", lambda v: v.tensor_tensor(
                    s_ro[:, cols(8)], s_ro[:, cols(8)], s_ro[:, cols(6)], ALU.add))
                ctx.sig("vector", lambda v: v.scalar_tensor_tensor(
                    s_ro[:, cols(5)], s_ro[:, cols(6)], -1e30, s_ro[:, cols(5)],
                    ALU.mult, ALU.add))
                ctx.op("vector", lambda v: v.drain())
            ctx.sig("vector", lambda v: v.tensor_tensor(
                s_ro[:, cols(9)], s_ro[:, cols(8)], s_ro[:, cols(1)], ALU.mult))
            ctx.op("vector", lambda v: v.drain())
            ctx.sig("vector", lambda v, mcol=mcol: v.tensor_reduce(
                s_ro[:, mcol + 3:mcol + 4], s_ro[:, cols(9)], AX.X, ALU.add))
            ctx.op("vector", lambda v: v.drain())
            tok_den = ctx.sig("vector", lambda v, mcol=mcol: v.tensor_scalar(
                s_ro[:, mcol + 3:mcol + 4], s_ro[:, mcol + 3:mcol + 4], 1e-20,
                None, ALU.add))
            ctx.wait("scalar", tok_den)
            ctx.sig("scalar", lambda s, mcol=mcol: s.activation(
                s_ro[:, mcol + 4:mcol + 5], s_ro[:, mcol + 3:mcol + 4], AF.Ln))
            tok_wi = ctx.sig("scalar", lambda s, mcol=mcol: s.activation(
                s_ro[:, mcol + 4:mcol + 5], s_ro[:, mcol + 4:mcol + 5], AF.Exp, scale=-1.0))
            ctx.wait("vector", tok_wi)
            tok_wd = ctx.sig("vector", lambda v, mcol=mcol: v.tensor_scalar(
                s_wdb[:, :], s_ro[:, cols(9)], s_ro[:, mcol + 4:mcol + 5], 2.5,
                ALU.mult, ALU.mult))
            ctx.wait("tensor", tok_wd)
            tok_tr = ctx.sig("tensor", lambda t: t.transpose(
                pbh[0:NE1, 0:128], s_wdb[:, :], s_ident[:, :]))
            ctx.wait("vector", tok_tr)
            tok_wT = ctx.sig("vector", lambda v, tt=tt: v.tensor_copy(
                s_wTb[:, tt:tt + 128], pbh[0:NE1, 0:128]))
            ctx.wait("tensor", tok_wT)
        tok_wTb = ctx.now("vector")
        ctx.auto_drain = False
        if getattr(ctx, "cur_b", 0) == 0:
            S2_mktap[0]("moe_ro", (128, 16 * NE1))
            S2_mktap[0]("rbb", (128, NE1))
            S2_dump[0]("rbb", lambda: s_rbb[:, :])
            S2_mktap[0]("wTb", (NE1, T), BF16)
            S2_dump[0]("moe_ro", lambda: s_ro[:, :])
            S2_dump[0]("wTb", lambda: s_wTb[:, :])

    with nc.sbuf_tensor("sh_gw_%d" % ctx.uid(), [128, NT * SHI], BF16) as s_sgw, \
         nc.sbuf_tensor("sh_uw_%d" % ctx.uid(), [128, NT * SHI], BF16) as s_suw, \
         nc.sbuf_tensor("sh_dw_%d" % ctx.uid(), [128, NT * D], BF16) as s_sdw, \
         nc.sbuf_tensor("sh_hid_%d" % ctx.uid(), [128, NT * TC], BF16) as s_shid, \
         nc.sbuf_tensor("sh_sg_%d" % ctx.uid(), [128, TC], F32) as s_ssg:
        ctx.wait("sync", ctx.now("vector"), ctx.now("tensor"), ctx.now("scalar"))
        tok_g = ctx.sig("sync", lambda y: y.dma_start(
            out=_r(s_sgw[:, :], "p (c m) -> p c m", c=NT),
            in_=_r(i_sgwT, "(c p) m -> p c m", p=128)), dma=True)
        tok_u = ctx.sig("sync", lambda y: y.dma_start(
            out=_r(s_suw[:, :], "p (c m) -> p c m", c=NT),
            in_=_r(i_suwT, "(c p) m -> p c m", p=128)), dma=True)
        tok_d = ctx.sig("sync", lambda y: y.dma_start(
            out=_r(s_sdw[:, :], "p (c m) -> p c m", c=NT),
            in_=_r(i_sdwT, "(c p) m -> p c m", p=128)), dma=True)
        ctx.wait("tensor", tok_g, tok_u, tok_d)
        shev_tok = [None, None]
        for c in range(NTC):
            for it in range(NT):
                for k in range(NT):
                    ctx.sig("tensor", lambda t, k=k, it=it, c=c: t.matmul(
                        pb[0][:, 0:TC], s_sgw[:, k * SHI + it * 128:k * SHI + (it + 1) * 128],
                        s_xn[:, k * T + c * TC:k * T + (c + 1) * TC],
                        start=(k == 0), stop=(k == NT - 1)))
                for k in range(NT):
                    ctx.sig("tensor", lambda t, k=k, it=it, c=c: t.matmul(
                        pb[1][:, 0:TC], s_suw[:, k * SHI + it * 128:k * SHI + (it + 1) * 128],
                        s_xn[:, k * T + c * TC:k * T + (c + 1) * TC],
                        start=(k == 0), stop=(k == NT - 1)))
                tok_gu = ctx.now("tensor")
                ctx.wait("scalar", tok_gu)
                tok_sg = ctx.sig("scalar", lambda s: s.activation(
                    s_ssg[:, :], pb[0][:, 0:TC], AF.Silu))
                ctx.wait("vector", tok_sg)
                tok_hid = ctx.sig("vector", lambda v, it=it: v.tensor_tensor(
                    s_shid[:, it * TC:(it + 1) * TC], s_ssg[:, :], pb[1][:, 0:TC],
                    ALU.mult))
                ctx.wait("tensor", tok_hid)
            for dp in range(NT):
                bnk = 2 + dp % 2
                ctx.wait("tensor", shev_tok[dp % 2])
                for k in range(NT):
                    ctx.sig("tensor", lambda t, k=k, dp=dp, bnk=bnk: t.matmul(
                        pb[bnk][:, 0:TC], s_sdw[:, k * D + dp * 128:k * D + (dp + 1) * 128],
                        s_shid[:, k * TC:(k + 1) * TC], start=(k == 0), stop=(k == NT - 1)))
                tok_o = ctx.now("tensor")
                ctx.wait("vector", tok_o)
                shev_tok[dp % 2] = ctx.sig("vector", lambda v, dp=dp, c=c, bnk=bnk:
                                           v.scalar_tensor_tensor(
                    s_x[:, dp * T + c * TC:dp * T + (c + 1) * TC], pb[bnk][:, 0:TC], 1.0,
                    s_x[:, dp * T + c * TC:dp * T + (c + 1) * TC], ALU.mult, ALU.add))
            ctx.wait("tensor", shev_tok[0], shev_tok[1])

    with nc.sbuf_tensor("mo_gw_%d" % ctx.uid(), [128, 2 * NT * I1], BF16) as s_gw, \
         nc.sbuf_tensor("mo_uw_%d" % ctx.uid(), [128, 2 * NT * I1], BF16) as s_uw, \
         nc.sbuf_tensor("mo_dw_%d" % ctx.uid(), [128, 2 * 4 * D], BF16) as s_dw, \
         nc.sbuf_tensor("mo_hid_%d" % ctx.uid(), [128, 4 * T], BF16) as s_hid, \
         nc.sbuf_tensor("mo_sg_%d" % ctx.uid(), [128, 2 * TC], F32) as s_sg, \
         nc.sbuf_tensor("mo_t1_%d" % ctx.uid(), [128, 2 * TC], F32) as s_t1:
        ld_tok = [None, None]
        used_tok = [None, None]
        hid_done = [None, None]
        sg_free = [None, None]
        tok_wb_last = None
        for e in range(NE1):
            sl = e % 2
            if e == 0:
                ctx.wait("sync", ctx.now("vector"), ctx.now("tensor"), tok_wTb)
            ctx.wait("sync", used_tok[sl])
            ctx.wait("sync", tok_wb_last)
            tok_row = ctx.sig("sync", lambda y, e=e: y.dma_start(
                out=s_wrow[0:1, :], in_=s_wTb[e:e + 1, :]), dma=True)
            ctx.sig("sync", lambda y, e=e, sl=sl: y.dma_start(
                out=_r(s_gw[:, sl * NT * I1:(sl + 1) * NT * I1], "p (c m) -> p c m", c=NT),
                in_=_r(i_gwT[e], "(c p) m -> p c m", p=128)), dma=True)
            ctx.sig("sync", lambda y, e=e, sl=sl: y.dma_start(
                out=_r(s_uw[:, sl * NT * I1:(sl + 1) * NT * I1], "p (c m) -> p c m", c=NT),
                in_=_r(i_uwT[e], "(c p) m -> p c m", p=128)), dma=True)
            ld_tok[sl] = ctx.sig("sync", lambda y, e=e, sl=sl: y.dma_start(
                out=_r(s_dw[:, sl * 4 * D:(sl + 1) * 4 * D], "p (c m) -> p c m", c=4),
                in_=_r(i_dwT[e], "(c p) m -> p c m", p=128)), dma=True)
            ctx.wait("tensor", ld_tok[sl])
            g0 = sl * NT * I1
            d0 = sl * 4 * D
            for c in range(NTC):
                wpar = c % 2
                ctx.wait("tensor", tok_row, hid_done[0], hid_done[1])
                tok_wb = ctx.sig("tensor", lambda t, c=c: t.matmul(
                    pb[2][:, 0:TC], s_onesb[0:1, :],
                    s_wrow[0:1, c * TC:(c + 1) * TC],
                    start=True, stop=True))
                tok_wb_last = tok_wb
                for it in range(4):
                    par = it % 2
                    pg, pu = pb[0 + 5 * par], pb[1 + 5 * par]
                    ctx.wait("tensor", hid_done[par])
                    for k in range(NT):
                        ctx.sig("tensor", lambda t, k=k, it=it, c=c, g0=g0, pg=pg: t.matmul(
                            pg[:, 0:TC],
                            s_gw[:, g0 + k * I1 + it * 128:g0 + k * I1 + (it + 1) * 128],
                            s_xn[:, k * T + c * TC:k * T + (c + 1) * TC],
                            start=(k == 0), stop=(k == NT - 1)))
                    for k in range(NT):
                        ctx.sig("tensor", lambda t, k=k, it=it, c=c, g0=g0, pu=pu: t.matmul(
                            pu[:, 0:TC],
                            s_uw[:, g0 + k * I1 + it * 128:g0 + k * I1 + (it + 1) * 128],
                            s_xn[:, k * T + c * TC:k * T + (c + 1) * TC],
                            start=(k == 0), stop=(k == NT - 1)))
                    tok_gu = ctx.now("tensor")
                    ctx.wait("scalar", tok_gu, sg_free[par])
                    tok_sg = ctx.sig("scalar", lambda s, par=par, pg=pg: s.activation(
                        s_sg[:, par * TC:(par + 1) * TC], pg[:, 0:TC], AF.Silu))
                    ctx.wait("vector", tok_sg, tok_wb)
                    ctx.sig("vector", lambda v, par=par, pu=pu: v.tensor_tensor(
                        s_t1[:, par * TC:(par + 1) * TC],
                        s_sg[:, par * TC:(par + 1) * TC], pu[:, 0:TC], ALU.mult))
                    ctx.op("vector", lambda v: v.drain())
                    tok_hid = ctx.sig("vector", lambda v, it=it, c=c, par=par: v.tensor_tensor(
                        s_hid[:, it * T + c * TC:it * T + (c + 1) * TC],
                        s_t1[:, par * TC:(par + 1) * TC],
                        pb[2][:, 0:TC], ALU.mult))
                    hid_done[par] = tok_hid
                    sg_free[par] = tok_hid
            ctx.wait("tensor", hid_done[0], hid_done[1])
            dev_tok = [None, None]
            for c in range(NTC):
                for dp in range(NT):
                    i2 = c * NT + dp
                    bnk = 3 + (i2 % 2)
                    ctx.wait("tensor", dev_tok[i2 % 2])
                    for k in range(4):
                        ctx.sig("tensor", lambda t, k=k, dp=dp, c=c, d0=d0, bnk=bnk: t.matmul(
                            pb[bnk][:, 0:TC],
                            s_dw[:, d0 + k * D + dp * 128:d0 + k * D + (dp + 1) * 128],
                            s_hid[:, k * T + c * TC:k * T + (c + 1) * TC],
                            start=(k == 0), stop=(k == 3)))
                    tok_o = ctx.now("tensor")
                    ctx.wait("vector", tok_o)
                    dev_tok[i2 % 2] = ctx.sig("vector", lambda v, dp=dp, c=c, bnk=bnk:
                                              v.scalar_tensor_tensor(
                        s_x[:, dp * T + c * TC:dp * T + (c + 1) * TC],
                        pb[bnk][:, 0:TC], 1.0,
                        s_x[:, dp * T + c * TC:dp * T + (c + 1) * TC],
                        ALU.mult, ALU.add))
            ctx.wait("tensor", dev_tok[0], dev_tok[1])
            used_tok[sl] = ctx.now("tensor")
    es_moe.close()


def build_program(debug_taps=()):
    nc = bass.Bass("TRN2", target_bir_lowering=False)
    es = ExitStack()

    def din(name, shape, dt=F32):
        return nc.dram_tensor(name, list(shape), dt, kind="ExternalInput").ap()

    d_x = din("x_in", (BPC, D, T))
    d_wq = din("w_q", (D, H)); d_wk = din("w_k", (D, H)); d_wv = din("w_v", (D, H))
    d_wo = din("w_o", (H, D))
    d_wbase = din("w_base", (D, D))
    d_wd1 = din("wd1", (NE0, D, H0)); d_wd2 = din("wd2", (NE0, H0, D))
    d_wad = din("wa_down", (NE0, D, R0)); d_wau = din("wa_up", (NE0, R0, D))
    d_bbase = din("bb_base", (128, NT))
    d_bd1 = din("bd1_c", (128, NE0, 2))
    d_bd2 = din("bd2_c", (128, NE0, NT))
    d_bad = din("bad_c", (R0, NE0))
    d_bau = din("bau_c", (128, NE0, NT))
    d_winT = din("w_inT", (D, 2 * DI))
    d_wxT = din("w_xT", (DI, 128))
    d_wdtT = din("w_dtT", (DTR, DI))
    d_woutT = din("w_outT", (DI, D))
    d_binc = din("b_in_c", (128, 32))
    d_bxc = din("b_x_c", (128, 1))
    d_bdtc = din("b_dt_c", (128, NDI))
    d_alogc = din("a_log_c", (128, NDI * NS))
    d_dc = din("d_c", (128, NDI))
    d_boutc = din("b_out_c", (128, NT))
    d_rwT = din("rw_T", (D, NE1))
    d_rb = din("r_bias", (1, NE1))
    d_gwT = din("gw_T", (NE1, D, I1))
    d_uwT = din("uw_T", (NE1, D, I1))
    d_dwT = din("dw_T", (NE1, I1, D))
    d_sgwT = din("sgw_T", (D, SHI))
    d_suwT = din("suw_T", (D, SHI))
    d_sdwT = din("sdw_T", (SHI, D))
    d_ident = din("c_ident", (128, 128), BF16)
    d_triu = din("c_triu", (128, 128), BF16)
    d_onesb = din("c_ones_b", (128, 128), BF16)
    d_onesf = din("c_ones_f", (128, 128), F32)

    d_out = nc.dram_tensor("x_out", [BPC, D, T], F32, kind="ExternalOutput").ap()

    def dint(name, shape):
        return nc.dram_tensor(name, list(shape), BF16, kind="Internal").ap()

    i_wq = dint("i_wq", (D, H)); i_wk = dint("i_wk", (D, H)); i_wv = dint("i_wv", (D, H))
    i_wo = dint("i_wo", (H, D))
    i_wbase = dint("i_wbase", (D, D))
    i_wd1 = dint("i_wd1", (NE0, D, H0)); i_wd2 = dint("i_wd2", (NE0, H0, D))
    i_wad = dint("i_wad", (NE0, D, R0)); i_wau = dint("i_wau", (NE0, R0, D))
    i_winT = dint("i_winT", (D, 2 * DI))
    i_wxT = dint("i_wxT", (DI, 128))
    i_wdtT = dint("i_wdtT", (DTR, DI))
    i_woutT = dint("i_woutT", (DI, D))
    i_rwT = dint("i_rwT", (D, NE1))
    i_gwT = dint("i_gwT", (NE1, D, I1))
    i_uwT = dint("i_uwT", (NE1, D, I1))
    i_dwT = dint("i_dwT", (NE1, I1, D))
    i_sgwT = dint("i_sgwT", (D, SHI))
    i_suwT = dint("i_suwT", (D, SHI))
    i_sdwT = dint("i_sdwT", (SHI, D))

    ctx = Ctx(nc, es)
    ctx.debug_taps = tuple(debug_taps)

    taps = {}
    def mktap(name, shape, dt=F32):
        if name in debug_taps and name not in taps:
            taps[name] = nc.dram_tensor("tap_" + name, list(shape), dt,
                                        kind="ExternalOutput").ap()

    for b in range(BPC):
        mktap("xn0_b%d" % b, (128, NT * T), BF16)
        mktap("x_l0_b%d" % b, (128, NT * T))
        mktap("oall_b%d" % b, (H, T), BF16)
        mktap("xn1_b%d" % b, (128, NT * T), BF16)
        mktap("x_ssm_b%d" % b, (128, NT * T))

    def dump(name, ap_fn):
        t = taps.get(name)
        if t is None:
            return
        bar = ctx.barrier()
        ctx.sig("gpsimd", lambda g, t=t: g.dma_start(out=t, in_=ap_fn()), dma=True)
        ctx.barrier()
    S2_dump[0] = dump
    S2_mktap[0] = mktap

    def sbr(name, shape, dt):
        return es.enter_context(nc.sbuf_tensor(name, list(shape), dt))

    S = {}
    S["x"] = sbr("s_x", (128, NT * T), F32)
    S["xn"] = sbr("s_xn", (128, NT * T), BF16)
    S["ident"] = sbr("s_ident", (128, 128), BF16)
    S["triu"] = sbr("s_triu", (128, 128), BF16)
    S["onesb"] = sbr("s_onesb", (128, 128), BF16)
    S["onesf"] = sbr("s_onesf", (128, 128), F32)
    S["wq"] = sbr("s_wq", (128, NT * H), BF16)
    S["wk"] = sbr("s_wk", (128, NT * H), BF16)
    S["wv"] = sbr("s_wv", (128, NT * H), BF16)
    S["wo"] = sbr("s_wo", (H, D), BF16)
    S["rwT"] = sbr("s_rwT", (128, NT * NE1), BF16)
    S["bbase"] = sbr("s_bbase", (128, NT), F32)
    S["bd1"] = sbr("s_bd1", (128, NE0 * 2), F32)
    S["bd2"] = sbr("s_bd2", (128, NE0 * NT), F32)
    S["bad"] = sbr("s_bad", (R0, NE0), F32)
    S["bau"] = sbr("s_bau", (128, NE0 * NT), F32)
    S["binc"] = sbr("s_binc", (128, 32), F32)
    S["bxc"] = sbr("s_bxc", (128, 1), F32)
    S["bdtc"] = sbr("s_bdtc", (128, NDI), F32)
    S["A"] = sbr("s_A", (128, NDI * NS), F32)
    S["dc"] = sbr("s_dc", (128, NDI), F32)
    S["boutc"] = sbr("s_boutc", (128, NT), F32)
    S["rbb"] = sbr("s_rbb", (128, NE1), F32)
    S["kone"] = sbr("s_kone", (128, 2), F32)
    S["hstate"] = sbr("s_hstate", (128, NDI * NS), F32)

    pb = [es.enter_context(nc.psum_tensor("pb%d" % i, [128, 512], F32))
          for i in range(7)]
    pbh = es.enter_context(nc.psum_tensor("pbh", [128, 1024], BF16))
    S["pb"] = pb
    S["pbh"] = pbh

    cast_toks = []
    for (dst, src) in [(i_wq, d_wq), (i_wk, d_wk), (i_wv, d_wv), (i_wo, d_wo),
                       (i_wbase, d_wbase), (i_wd1, d_wd1), (i_wd2, d_wd2),
                       (i_wad, d_wad), (i_wau, d_wau)]:
        cast_toks.append(ctx.sig("gpsimd", lambda g, a=dst, b=src: g.dma_start(out=a, in_=b), dma=True))
    tok_cast_l0 = cast_toks[-1]
    for (dst, src) in [(i_winT, d_winT), (i_wxT, d_wxT), (i_wdtT, d_wdtT),
                       (i_woutT, d_woutT), (i_rwT, d_rwT),
                       (i_sgwT, d_sgwT), (i_suwT, d_suwT), (i_sdwT, d_sdwT)]:
        cast_toks.append(ctx.sig("gpsimd", lambda g, a=dst, b=src: g.dma_start(out=a, in_=b), dma=True))
    tok_cast_l1 = cast_toks[-1]
    for (dst, src) in [(i_gwT, d_gwT), (i_uwT, d_uwT), (i_dwT, d_dwT)]:
        cast_toks.append(ctx.sig("gpsimd", lambda g, a=dst, b=src: g.dma_start(out=a, in_=b), dma=True))
    tok_cast_moe = cast_toks[-1]

    for (dst, src) in [(S["ident"], d_ident), (S["triu"], d_triu), (S["onesb"], d_onesb),
                       (S["onesf"], d_onesf), (S["bbase"], d_bbase),
                       (S["bad"], d_bad), (S["binc"], d_binc), (S["bxc"], d_bxc),
                       (S["bdtc"], d_bdtc), (S["dc"], d_dc), (S["boutc"], d_boutc)]:
        ctx.sig("sync", lambda y, a=dst, b=src: y.dma_start(out=a[:, :], in_=b), dma=True)
    ctx.sig("sync", lambda y: y.dma_start(out=S["bd1"][:, :], in_=_r(d_bd1, "p e c -> p (e c)")), dma=True)
    ctx.sig("sync", lambda y: y.dma_start(out=S["bd2"][:, :], in_=_r(d_bd2, "p e c -> p (e c)")), dma=True)
    ctx.sig("sync", lambda y: y.dma_start(out=S["bau"][:, :], in_=_r(d_bau, "p e c -> p (e c)")), dma=True)
    ctx.sig("sync", lambda y: y.dma_start(out=S["A"][:, :], in_=d_alogc), dma=True)
    ctx.sig("sync", lambda y: y.dma_start(out=S["rbb"][0:1, 0:NE1], in_=d_rb), dma=True)
    ctx.wait("sync", tok_cast_l1)
    for (dst, src) in [(S["wq"], i_wq), (S["wk"], i_wk), (S["wv"], i_wv)]:
        ctx.sig("sync", lambda y, a=dst, b=src: y.dma_start(
            out=_r(a[:, :], "p (c m) -> p c m", m=H),
            in_=_r(b, "(c p) m -> p c m", p=128)), dma=True)
    ctx.sig("sync", lambda y: y.dma_start(out=S["wo"][:, :], in_=i_wo), dma=True)
    ctx.sig("sync", lambda y: y.dma_start(
        out=_r(S["rwT"][:, :], "p (c m) -> p c m", m=NE1),
        in_=_r(i_rwT, "(c p) m -> p c m", p=128)), dma=True)
    tok_params = ctx.now("sync")

    ctx.wait("tensor", tok_params)
    tok_rbmm = ctx.sig("tensor", lambda t: t.matmul(
        pb[0][:, 0:NE1], S["onesf"][0:1, :], S["rbb"][0:1, 0:NE1], start=True, stop=True))
    ctx.wait("vector", tok_rbmm)
    ctx.sig("vector", lambda v: v.tensor_copy(S["rbb"][:, :], pb[0][:, 0:NE1]))
    ctx.wait("scalar", tok_params)
    tok_Ae = ctx.sig("scalar", lambda s: s.activation(S["A"][:, :], S["A"][:, :], AF.Exp))
    ctx.wait("vector", tok_Ae)
    ctx.sig("vector", lambda v: v.tensor_scalar(S["A"][:, :], S["A"][:, :], -1.0, None, ALU.mult))
    def _kone(v):
        v.memset(S["kone"][:, 0:1], 1.0)
        v.memset(S["kone"][:, 1:2], 1e-5)
        return v.drain()
    ctx.op("vector", _kone)
    ctx.barrier()

    for b in range(BPC):
        ctx.cur_b = b
        tok_x = ctx.sig("sync", lambda y, b=b: y.dma_start(
            out=_r(S["x"][:, :], "p (c t) -> p c t", c=NT),
            in_=_r(d_x[b], "(c p) t -> p c t", p=128)), dma=True)
        emit_layernorm(ctx, nc, S, [tok_x])
        ctx.barrier()
        dump("xn0_b%d" % b, lambda: S["xn"][:, :])
        es_l0 = ExitStack()
        S["oall"] = es_l0.enter_context(
            nc.sbuf_tensor("s_oall_%d" % ctx.uid(), [H, T], BF16))
        if PHASES["l0a"]:
            emit_l0_attn(ctx, nc, S)
        dump("oall_b%d" % b, lambda: S["oall"][:, :])
        if PHASES["l0f"]:
            emit_l0_ffn(ctx, nc, S, i_wd1, i_wd2, i_wad, i_wau, i_wbase)
        es_l0.close()
        ctx.barrier()
        dump("x_l0_b%d" % b, lambda: S["x"][:, :])
        emit_layernorm(ctx, nc, S, [])
        ctx.barrier()
        dump("xn1_b%d" % b, lambda: S["xn"][:, :])
        if PHASES["ssm"]:
            emit_l1_ssm(ctx, nc, S, i_winT, i_wxT, i_wdtT, i_woutT)
        ctx.barrier()
        dump("x_ssm_b%d" % b, lambda: S["x"][:, :])
        if PHASES["moe"]:
            ctx.wait("sync", tok_cast_moe)
            emit_l1_moe(ctx, nc, S, i_gwT, i_uwT, i_dwT, i_sgwT, i_suwT, i_sdwT)
        ctx.barrier()
        tok_store = ctx.sig("gpsimd", lambda g, b=b: g.dma_start(
            out=_r(d_out[b], "(c p) t -> p c t", p=128),
            in_=_r(S["x"][:, :], "p (c t) -> p c t", c=NT)), dma=True)
        ctx.wait("sync", tok_store)
        ctx.wait("gpsimd", tok_store)

    ctx.barrier()

    blk = es.enter_context(nc.Block())

    @blk.tensor
    def _(e):
        for f in ctx.q["tensor"]:
            f(e)

    @blk.vector
    def _(e):
        for f in ctx.q["vector"]:
            f(e)

    @blk.scalar
    def _(e):
        for f in ctx.q["scalar"]:
            f(e)

    @blk.gpsimd
    def _(e):
        for f in ctx.q["gpsimd"]:
            f(e)

    @blk.sync
    def _(e):
        for f in ctx.q["sync"]:
            f(e)

    es.close()
    return nc


def host_inputs(inp, core):
    f = np.ascontiguousarray

    def t(a):
        return f(np.asarray(a).transpose())

    x = np.asarray(inp["L0_attn_x"])
    xb = x[core * BPC:(core + 1) * BPC]
    m = {
        "x_in": f(xb.transpose(0, 2, 1)),
        "w_q": f(inp["L0_attn_W_q"]), "w_k": f(inp["L0_attn_W_k"]),
        "w_v": f(inp["L0_attn_W_v"]), "w_o": f(inp["L0_attn_W_o"]),
        "w_base": f(inp["L0_ffn_W_base"]),
        "wd1": f(inp["L0_ffn_Wd1"]), "wd2": f(inp["L0_ffn_Wd2"]),
        "wa_down": f(inp["L0_ffn_Wa_down"]), "wa_up": f(inp["L0_ffn_Wa_up"]),
        "bb_base": f(np.asarray(inp["L0_ffn_b_base"]).reshape(NT, 128).T),
        "bd1_c": f(np.asarray(inp["L0_ffn_bd1"]).reshape(NE0, 2, 128).transpose(2, 0, 1)),
        "bd2_c": f(np.asarray(inp["L0_ffn_bd2"]).reshape(NE0, NT, 128).transpose(2, 0, 1)),
        "bad_c": f(np.asarray(inp["L0_ffn_ba_down"]).T),
        "bau_c": f(np.asarray(inp["L0_ffn_ba_up"]).reshape(NE0, NT, 128).transpose(2, 0, 1)),
        "w_inT": t(inp["L1_attn_W_in"]),
        "w_dtT": t(inp["L1_attn_W_dt"]),
        "w_outT": t(inp["L1_attn_W_out"]),
        "b_in_c": f(np.asarray(inp["L1_attn_b_in"]).reshape(32, 128).T),
        "b_dt_c": f(np.asarray(inp["L1_attn_b_dt"]).reshape(NDI, 128).T),
        "a_log_c": f(np.asarray(inp["L1_attn_A_log"]).reshape(NDI, 128, NS)
                     .transpose(1, 0, 2).reshape(128, NDI * NS)),
        "d_c": f(np.asarray(inp["L1_attn_D"]).reshape(NDI, 128).T),
        "b_out_c": f(np.asarray(inp["L1_attn_b_out"]).reshape(NT, 128).T),
        "rw_T": t(inp["L1_ffn_router_weight"]),
        "r_bias": f(np.asarray(inp["L1_ffn_router_bias"]).reshape(1, NE1)),
        "gw_T": f(np.asarray(inp["L1_ffn_gate_w"]).transpose(0, 2, 1)),
        "uw_T": f(np.asarray(inp["L1_ffn_up_w"]).transpose(0, 2, 1)),
        "dw_T": f(np.asarray(inp["L1_ffn_down_w"]).transpose(0, 2, 1)),
        "sgw_T": t(inp["L1_ffn_shared_gate_w"]),
        "suw_T": t(inp["L1_ffn_shared_up_w"]),
        "sdw_T": t(inp["L1_ffn_shared_down_w"]),
    }
    wx = np.asarray(inp["L1_attn_W_x"])
    wxT = np.zeros((DI, 128), dtype=np.float32)
    wxT[:, 0:64] = wx[0:64].T
    wxT[:, 64:80] = wx[64:80].T
    wxT[:, 96:112] = wx[80:96].T
    m["w_xT"] = f(wxT)
    bx = np.asarray(inp["L1_attn_b_x"])
    bxc = np.zeros((128, 1), dtype=np.float32)
    bxc[0:64, 0] = bx[0:64]
    bxc[64:80, 0] = bx[64:80]
    bxc[96:112, 0] = bx[80:96]
    m["b_x_c"] = bxc
    m["c_ident"] = np.eye(128, dtype=ml_dtypes.bfloat16)
    m["c_triu"] = np.triu(np.ones((128, 128), dtype=np.float32)).astype(ml_dtypes.bfloat16)
    m["c_ones_b"] = np.ones((128, 128), dtype=ml_dtypes.bfloat16)
    m["c_ones_f"] = np.ones((128, 128), dtype=np.float32)
    return m


def kernel(**inp):
    nc = build_program(DEBUG_TAPS)
    in_maps = [host_inputs(inp, core) for core in range(NCORES)]
    res = run_bass_kernel_spmd(nc, in_maps, core_ids=list(range(NCORES)))
    outs = []
    for core in range(NCORES):
        o = res.results[core]["x_out"]
        outs.append(np.ascontiguousarray(o.transpose(0, 2, 1)))
    return np.concatenate(outs, axis=0).astype(np.float32)


if __name__ == "__main__":
    print("building program...")
    import time
    t0 = time.time()
    nc = build_program()
    print("built in %.1fs" % (time.time() - t0))


# version 10
# speedup vs baseline: 1.1592x; 1.1527x over previous
import numpy as np
import ml_dtypes
from contextlib import ExitStack

import concourse.bass as bass
import concourse.mybir as mybir
from concourse.bass_utils import run_bass_kernel_spmd

F32 = mybir.dt.float32
BF16 = mybir.dt.bfloat16
AF = mybir.ActivationFunctionType
ALU = mybir.AluOpType
AX = mybir.AxisListType

NCORES = 8
BPC = 4
T = 2048
D = 1024
NT = 8
TC = 512
NTC = T // TC
H = 64
CH = 128
NCH = T // CH
DI = 2048
NDI = 16
DTR = 64
NS = 16
SEG = 256
NSEG = T // SEG
NE0 = 8
H0 = 256
R0 = 64
NE1 = 32
I1 = 512
SHI = 1024

ENGINES = ("tensor", "vector", "scalar", "gpsimd", "sync")
DEBUG_TAPS = ()
PHASES = {"l0a": True, "l0f": True, "ssm": True, "moe": True}
S2_dump = [None]
S2_mktap = [None]


class Ctx:
    def __init__(self, nc, es):
        self.nc = nc
        self.q = {e: [] for e in ENGINES}
        self.sem = {e: es.enter_context(nc.semaphore("sem_" + e)) for e in ENGINES}
        self.debug_taps = ()
        self.auto_drain = False
        self.cnt = {e: 0 for e in ENGINES}

    _uid = [0]

    def uid(self):
        Ctx._uid[0] += 1
        return Ctx._uid[0]

    def op(self, eng, fn):
        self.q[eng].append(fn)

    def sig(self, eng, fn, dma=False):
        inc = 16 if dma else 1
        self.cnt[eng] += inc
        assert self.cnt[eng] < 2 ** 31
        sem = self.sem[eng]

        def g(e, fn=fn, sem=sem, inc=inc):
            fn(e).then_inc(sem, inc)

        self.q[eng].append(g)
        if self.auto_drain and eng in ("vector", "scalar") and not dma:
            self.q[eng].append(lambda e: e.drain())
        return (sem, self.cnt[eng])

    def wait(self, eng, *toks):
        for tk in toks:
            if tk is None:
                continue
            sem, val = tk
            if val > 0:
                self.q[eng].append(lambda e, s=sem, v=val: e.wait_ge(s, v))

    def now(self, eng):
        return (self.sem[eng], self.cnt[eng])

    def barrier(self):
        toks = {e: self.now(e) for e in ENGINES}
        for e in ENGINES:
            self.wait(e, *[toks[o] for o in ENGINES if o != e])
        return toks


def _r(ap, spec, **kw):
    return ap.rearrange(spec, **kw)


def emit_layernorm(ctx, nc, S, pre_toks):
    s_x, s_xn, s_onesf, s_kone, pb = S["x"], S["xn"], S["onesf"], S["kone"], S["pb"]
    with nc.sbuf_tensor("ln_xsq_%d" % ctx.uid(), [128, TC], F32) as s_xsq, \
         nc.sbuf_tensor("ln_st_%d" % ctx.uid(), [1, 4 * TC], F32) as s_st:
        ctx.wait("scalar", *pre_toks)
        ctx.wait("tensor", *pre_toks)
        for c in range(NTC):
            for i in range(NT):
                sl = (i * T + c * TC, i * T + (c + 1) * TC)
                tok_sq = ctx.sig("scalar", lambda s, sl=sl: s.activation(
                    s_xsq[:, :], s_x[:, sl[0]:sl[1]], AF.Square))
                ctx.sig("tensor", lambda t, sl=sl, i=i: t.matmul(
                    pb[0][0:1, 0:TC], s_onesf[:, 0:1], s_x[:, sl[0]:sl[1]],
                    start=(i == 0), stop=(i == NT - 1)))
                ctx.wait("tensor", tok_sq)
                tok_mm2 = ctx.sig("tensor", lambda t, i=i: t.matmul(
                    pb[1][0:1, 0:TC], s_onesf[:, 0:1], s_xsq[:, :],
                    start=(i == 0), stop=(i == NT - 1)))
                ctx.wait("scalar", tok_mm2)
            tok_sums = ctx.now("tensor")
            ctx.wait("vector", tok_sums)
            ctx.sig("vector", lambda v: v.tensor_scalar(
                s_st[0:1, 0:TC], pb[0][0:1, 0:TC], 1.0 / D, None, ALU.mult))
            ctx.sig("vector", lambda v: v.tensor_tensor(
                s_st[0:1, 2 * TC:3 * TC], s_st[0:1, 0:TC], s_st[0:1, 0:TC], ALU.mult))
            tok_m2 = ctx.sig("vector", lambda v: v.scalar_tensor_tensor(
                s_st[0:1, 3 * TC:4 * TC], pb[1][0:1, 0:TC], 1.0 / D,
                s_st[0:1, 2 * TC:3 * TC], ALU.mult, ALU.subtract))
            ctx.wait("scalar", tok_m2)
            ctx.sig("scalar", lambda s: s.activation(
                s_st[0:1, 3 * TC:4 * TC], s_st[0:1, 3 * TC:4 * TC],
                AF.Ln, bias=s_kone[0:1, 1:2]))
            tok_rstd = ctx.sig("scalar", lambda s: s.activation(
                s_st[0:1, TC:2 * TC], s_st[0:1, 3 * TC:4 * TC], AF.Exp, scale=-0.5))
            ctx.wait("tensor", tok_rstd)
            tok_bm = ctx.sig("tensor", lambda t: t.matmul(
                pb[2][:, 0:TC], s_onesf[0:1, :], s_st[0:1, 0:TC], start=True, stop=True))
            tok_br = ctx.sig("tensor", lambda t: t.matmul(
                pb[3][:, 0:TC], s_onesf[0:1, :], s_st[0:1, TC:2 * TC], start=True, stop=True))
            ctx.wait("vector", tok_bm, tok_br)
            for i in range(NT):
                sl = (i * T + c * TC, i * T + (c + 1) * TC)
                ctx.sig("vector", lambda v, sl=sl: v.tensor_tensor(
                    s_xsq[:, :], s_x[:, sl[0]:sl[1]], pb[2][:, 0:TC], ALU.subtract))
                ctx.sig("vector", lambda v, sl=sl: v.tensor_tensor(
                    s_xn[:, sl[0]:sl[1]], s_xsq[:, :], pb[3][:, 0:TC], ALU.mult))
            tok_done = ctx.now("vector")
            ctx.wait("tensor", tok_done)
            ctx.wait("scalar", tok_done)


def emit_l0_attn(ctx, nc, S):
    s_xn, s_oall, s_triu, s_onesb, pb = S["xn"], S["oall"], S["triu"], S["onesb"], S["pb"]
    s_wq, s_wk, s_wv = S["wq"], S["wk"], S["wv"]
    with nc.sbuf_tensor("at_q_%d" % ctx.uid(), [H, CH], BF16) as s_q, \
         nc.sbuf_tensor("at_kc_%d" % ctx.uid(), [CH, H], BF16) as s_kc, \
         nc.sbuf_tensor("at_vca_%d" % ctx.uid(), [CH, H + 1], BF16) as s_vca, \
         nc.sbuf_tensor("at_vt_%d" % ctx.uid(), [H, CH], BF16) as s_vt, \
         nc.sbuf_tensor("at_am_%d" % ctx.uid(), [CH, CH], BF16) as s_am, \
         nc.sbuf_tensor("at_st_%d" % ctx.uid(), [H + 1, H], BF16) as s_stz, \
         nc.sbuf_tensor("at_qz_%d" % ctx.uid(), [H, CH], BF16) as s_qz, \
         nc.sbuf_tensor("at_sc_%d" % ctx.uid(), [128, CH], F32) as s_sc, \
         nc.sbuf_tensor("at_dn_%d" % ctx.uid(), [1, 2 * CH], F32) as s_dn, \
         nc.sbuf_tensor("at_dnb_%d" % ctx.uid(), [1, CH], BF16) as s_dnb, \
         nc.sbuf_tensor("at_of_%d" % ctx.uid(), [H, CH], F32) as s_of:

        def _init(v):
            v.memset(s_stz[:, :], 0.0)
            v.memset(s_vca[:, H:H + 1], 1.0)
            return v.drain()
        ctx.op("vector", _init)

        for c in range(NCH):
            c0 = c * CH

            def phi_emit(src_pb, dst, n):
                tk1 = ctx.sig("vector", lambda v, s=src_pb, n=n: v.tensor_scalar(
                    s_sc[0:n[0], 0:n[1]], s, 0.0, None, ALU.min))
                ctx.wait("scalar", tk1)
                tk2 = ctx.sig("scalar", lambda s, n=n: s.activation(
                    s_sc[0:n[0], 0:n[1]], s_sc[0:n[0], 0:n[1]], AF.Exp))
                ctx.wait("vector", tk2)
                return ctx.sig("vector", lambda v, s=src_pb, d=dst, n=n: v.scalar_tensor_tensor(
                    d, s, 0.0, s_sc[0:n[0], 0:n[1]], ALU.max, ALU.add))

            for i in range(NT):
                xsl = s_xn[:, i * T + c0:i * T + c0 + CH]
                ctx.sig("tensor", lambda t, i=i, xs=xsl: t.matmul(
                    pb[0][0:H, 0:CH], s_wq[:, i * H:(i + 1) * H], xs,
                    start=(i == 0), stop=(i == NT - 1)))
                ctx.sig("tensor", lambda t, i=i, xs=xsl: t.matmul(
                    pb[1][0:H, 0:CH], s_wv[:, i * H:(i + 1) * H], xs,
                    start=(i == 0), stop=(i == NT - 1)))
                ctx.sig("tensor", lambda t, i=i, xs=xsl: t.matmul(
                    pb[2][0:CH, 0:H], xs, s_wk[:, i * H:(i + 1) * H],
                    start=(i == 0), stop=(i == NT - 1)))
                ctx.sig("tensor", lambda t, i=i, xs=xsl: t.matmul(
                    pb[3][0:CH, 0:H], xs, s_wv[:, i * H:(i + 1) * H],
                    start=(i == 0), stop=(i == NT - 1)))
            ctx.wait("vector", ctx.now("tensor"))
            tok_q = phi_emit(pb[0][0:H, 0:CH], s_q[:, :], (H, CH))
            tok_k = phi_emit(pb[2][0:CH, 0:H], s_kc[:, :], (CH, H))
            tok_vt = ctx.sig("vector", lambda v: v.tensor_copy(s_vt[:, :], pb[1][0:H, 0:CH]))
            tok_vc = ctx.sig("vector", lambda v: v.tensor_copy(
                s_vca[:, 0:H], pb[3][0:CH, 0:H]))

            ctx.wait("tensor", tok_q, tok_vt)
            tok_A = ctx.sig("tensor", lambda t: t.matmul(
                pb[4][0:CH, 0:CH], s_vt[:, :], s_q[:, :], start=True, stop=True))
            ctx.wait("vector", tok_A)
            tok_am = ctx.sig("vector", lambda v: v.tensor_tensor(
                s_am[:, :], pb[4][0:CH, 0:CH], s_triu[:, :], ALU.mult))

            ctx.wait("tensor", tok_am, tok_k)
            ctx.sig("tensor", lambda t: t.matmul(
                pb[4][0:H, CH:2 * CH], s_kc[:, :], s_am[:, :], start=True, stop=False))
            tok_O = ctx.sig("tensor", lambda t: t.matmul(
                pb[4][0:H, CH:2 * CH], s_stz[0:H, 0:H], s_q[:, :], start=False, stop=True))

            ctx.sig("tensor", lambda t: t.matmul(
                pb[5][0:H, 0:CH], s_kc[:, :], s_triu[:, :], start=True, stop=False))
            tok_Z = ctx.sig("tensor", lambda t: t.matmul(
                pb[5][0:H, 0:CH], s_stz[H:H + 1, 0:H], s_onesb[H:H + 1, 0:CH],
                start=False, stop=True))

            ctx.wait("tensor", tok_vc)
            tok_su = ctx.sig("tensor", lambda t, c=c: t.matmul(
                pb[6][0:H + 1, 0:H], s_vca[:, :], s_kc[:, :],
                start=(c == 0), stop=(c == NCH - 1)))

            ctx.wait("vector", tok_Z)
            tok_qz = ctx.sig("vector", lambda v: v.tensor_tensor(
                s_qz[:, :], s_q[:, :], pb[5][0:H, 0:CH], ALU.mult))
            ctx.wait("tensor", tok_qz)
            tok_dmm = ctx.sig("tensor", lambda t: t.matmul(
                pb[5][0:1, CH:2 * CH], s_onesb[0:H, 0:1], s_qz[:, :], start=True, stop=True))
            ctx.wait("vector", tok_dmm)
            tok_dn = ctx.sig("vector", lambda v: v.tensor_scalar(
                s_dn[0:1, 0:CH], pb[5][0:1, CH:2 * CH], 1e-6, None, ALU.max))
            ctx.wait("scalar", tok_dn)
            ctx.sig("scalar", lambda s: s.activation(
                s_dn[0:1, CH:2 * CH], s_dn[0:1, 0:CH], AF.Ln))
            tok_rcp = ctx.sig("scalar", lambda s: s.activation(
                s_dn[0:1, 0:CH], s_dn[0:1, CH:2 * CH], AF.Exp, scale=-1.0))
            ctx.wait("vector", tok_rcp)
            tok_dnb = ctx.sig("vector", lambda v: v.tensor_copy(s_dnb[0:1, :], s_dn[0:1, 0:CH]))
            ctx.wait("tensor", tok_dnb)
            tok_rb = ctx.sig("tensor", lambda t: t.matmul(
                pb[5][0:H, CH:2 * CH], s_onesb[0:1, 0:H], s_dnb[0:1, :], start=True, stop=True))
            ctx.wait("vector", tok_O, tok_rb)
            ctx.sig("vector", lambda v: v.tensor_copy(s_of[:, :], pb[4][0:H, CH:2 * CH]))
            ctx.op("vector", lambda v: v.drain())
            tok_of = ctx.sig("vector", lambda v, c0=c0: v.tensor_tensor(
                s_oall[:, c0:c0 + CH], s_of[:, :], pb[5][0:H, CH:2 * CH], ALU.mult))

            ctx.wait("vector", tok_su)
            tok_snap = ctx.sig("vector", lambda v: v.tensor_copy(
                s_stz[:, :], pb[6][0:H + 1, 0:H]))
            ctx.wait("tensor", tok_of, tok_snap, tok_vc, tok_k)


def emit_l0_ffn(ctx, nc, S, i_wd1, i_wd2, i_wad, i_wau, i_wbase):
    s_xn, s_x, s_oall, s_wo, pb = S["xn"], S["x"], S["oall"], S["wo"], S["pb"]
    s_onesb, s_onesf = S["onesb"], S["onesf"]
    s_bd1, s_bd2, s_bad, s_bau, s_bbase = S["bd1"], S["bd2"], S["bad"], S["bau"], S["bbase"]
    with nc.sbuf_tensor("f0_wd1_%d" % ctx.uid(), [128, 2 * NT * H0], BF16) as s_wd1, \
         nc.sbuf_tensor("f0_wd2_%d" % ctx.uid(), [128, 2 * 2 * D], BF16) as s_wd2, \
         nc.sbuf_tensor("f0_h_%d" % ctx.uid(), [128, 2 * T], BF16) as s_h, \
         nc.sbuf_tensor("f0_diff_%d" % ctx.uid(), [128, 3 * TC], F32) as s_diff, \
         nc.sbuf_tensor("f0_loss_%d" % ctx.uid(), [128, NE0 * 33], F32) as s_loss, \
         nc.sbuf_tensor("f0_l8_%d" % ctx.uid(), [128, NE0], F32) as s_l8, \
         nc.sbuf_tensor("f0_lrow_%d" % ctx.uid(), [1, 3 * NE0], F32) as s_lrow, \
         nc.sbuf_tensor("f0_lrb_%d" % ctx.uid(), [1, NE0], BF16) as s_lrb, \
         nc.sbuf_tensor("f0_ind_%d" % ctx.uid(), [128, 2 * NE0 + 1], F32) as s_ind, \
         nc.sbuf_tensor("f0_wad_%d" % ctx.uid(), [128, NE0 * NT * R0], BF16) as s_wad, \
         nc.sbuf_tensor("f0_wau_%d" % ctx.uid(), [R0, NE0 * D], BF16) as s_wau, \
         nc.sbuf_tensor("f0_wadsel_%d" % ctx.uid(), [128, NT * R0], BF16) as s_wadsel, \
         nc.sbuf_tensor("f0_wausel_%d" % ctx.uid(), [R0, D], BF16) as s_wausel, \
         nc.sbuf_tensor("f0_badsel_%d" % ctx.uid(), [R0, 2], F32) as s_badsel, \
         nc.sbuf_tensor("f0_l0b_%d" % ctx.uid(), [128, NT + NE0], F32) as s_l0b, \
         nc.sbuf_tensor("f0_abf_%d" % ctx.uid(), [R0, T], BF16) as s_abf, \
         nc.sbuf_tensor("f0_wbase_%d" % ctx.uid(), [128, NT * D], BF16) as s_wbase:

        ctx.wait("sync", ctx.now("vector"), ctx.now("tensor"), ctx.now("scalar"))
        tok_wd = []
        for e in range(2):
            tok_wd.append(ctx.sig("sync", lambda y, e=e: y.dma_start(
                out=_r(s_wd1[:, e * NT * H0:(e + 1) * NT * H0], "p (c m) -> p c m", c=NT),
                in_=_r(i_wd1[e], "(c p) m -> p c m", p=128)), dma=True))
            tok_wd.append(ctx.sig("sync", lambda y, e=e: y.dma_start(
                out=_r(s_wd2[:, e * 2 * D:(e + 1) * 2 * D], "p (c m) -> p c m", c=2),
                in_=_r(i_wd2[e], "(c p) m -> p c m", p=128)), dma=True))
        tok_wad = ctx.sig("sync", lambda y: y.dma_start(
            out=_r(s_wad[:, :], "p (e c m) -> p e c m", e=NE0, c=NT),
            in_=_r(i_wad, "e (c p) m -> p e c m", p=128)), dma=True)
        tok_wau = ctx.sig("sync", lambda y: y.dma_start(
            out=_r(s_wau[:, :], "p (e m) -> p e m", e=NE0),
            in_=_r(i_wau, "e p m -> p e m")), dma=True)
        tok_wbase = ctx.sig("sync", lambda y: y.dma_start(
            out=_r(s_wbase[:, :], "p (c m) -> p c m", c=NT),
            in_=_r(i_wbase, "(c p) m -> p c m", p=128)), dma=True)

        diff_par = [None, None]
        for e in range(NE0):
            sl = e % 2
            base1 = sl * NT * H0
            base2 = sl * 2 * D
            ctx.wait("tensor", tok_wd[2 * e], tok_wd[2 * e + 1])
            hr_tok = [None, None]
            for it in range(2):
                for c in range(NTC):
                    hpar = (it * NTC + c) % 2
                    hp = pb[0] if hpar == 0 else pb[6]
                    ctx.wait("tensor", hr_tok[hpar])
                    for k in range(NT):
                        ctx.sig("tensor", lambda t, k=k, it=it, c=c, b1=base1, hp=hp: t.matmul(
                            hp[:, 0:TC],
                            s_wd1[:, b1 + k * H0 + it * 128: b1 + k * H0 + (it + 1) * 128],
                            s_xn[:, k * T + c * TC:k * T + (c + 1) * TC],
                            start=(k == 0), stop=(k == NT - 1)))
                    tok_h = ctx.now("tensor")
                    ctx.wait("scalar", tok_h)
                    hr_tok[hpar] = ctx.sig("scalar", lambda s, it=it, c=c, e=e, hp=hp: s.activation(
                        s_h[:, it * T + c * TC:it * T + (c + 1) * TC], hp[:, 0:TC],
                        AF.Relu, bias=s_bd1[:, e * 2 + it:e * 2 + it + 1]))
            tok_hdone = ctx.now("scalar")
            df_tok = [None, None]
            for dt in range(NT):
                for c in range(NTC):
                    slot = (dt * NTC + c) % 2
                    rp = pb[1] if slot == 0 else pb[2]
                    ctx.wait("tensor", tok_hdone, df_tok[slot])
                    for k2 in range(2):
                        ctx.sig("tensor", lambda t, k2=k2, dt=dt, c=c, b2=base2, rp=rp: t.matmul(
                            rp[:, 0:TC],
                            s_wd2[:, b2 + k2 * D + dt * 128: b2 + k2 * D + (dt + 1) * 128],
                            s_h[:, k2 * T + c * TC:k2 * T + (c + 1) * TC],
                            start=(k2 == 0), stop=(k2 == 1)))
                    tok_rc = ctx.now("tensor")
                    ctx.wait("vector", tok_rc, diff_par[slot])
                    tok_df = ctx.sig("vector", lambda v, dt=dt, c=c, e=e, slot=slot, rp=rp:
                                     v.scalar_tensor_tensor(
                        s_diff[:, slot * TC:(slot + 1) * TC], rp[:, 0:TC],
                        s_bd2[:, e * NT + dt:e * NT + dt + 1],
                        s_xn[:, dt * T + c * TC:dt * T + (c + 1) * TC],
                        ALU.add, ALU.subtract))
                    df_tok[slot] = tok_df
                    ctx.wait("scalar", tok_df)
                    diff_par[slot] = ctx.sig("scalar", lambda s, e=e, dt=dt, c=c, slot=slot:
                                             s.activation(
                        s_diff[:, 2 * TC:3 * TC], s_diff[:, slot * TC:(slot + 1) * TC],
                        AF.Square, accum_out=s_loss[:, e * 33 + dt * NTC + c:
                                                    e * 33 + dt * NTC + c + 1]))
            if e + 2 < NE0:
                ctx.wait("sync", ctx.now("tensor"))
                tok_wd.append(ctx.sig("sync", lambda y, e=e, sl=sl: y.dma_start(
                    out=_r(s_wd1[:, sl * NT * H0:(sl + 1) * NT * H0], "p (c m) -> p c m", c=NT),
                    in_=_r(i_wd1[e + 2], "(c p) m -> p c m", p=128)), dma=True))
                tok_wd.append(ctx.sig("sync", lambda y, e=e, sl=sl: y.dma_start(
                    out=_r(s_wd2[:, sl * 2 * D:(sl + 1) * 2 * D], "p (c m) -> p c m", c=2),
                    in_=_r(i_wd2[e + 2], "(c p) m -> p c m", p=128)), dma=True))
            ctx.wait("vector", diff_par[0], diff_par[1])
            ctx.op("vector", lambda v: v.drain())
            ctx.sig("vector", lambda v, e=e: v.tensor_reduce(
                s_l8[:, e:e + 1], s_loss[:, e * 33:e * 33 + 32], AX.X, ALU.add))
        ctx.auto_drain = True
        tok_l8 = ctx.now("vector")
        ctx.wait("tensor", tok_l8)
        tok_lmm = ctx.sig("tensor", lambda t: t.matmul(
            pb[2][0:1, 0:NE0], s_onesf[:, 0:1], s_l8[:, :], start=True, stop=True))
        ctx.wait("vector", tok_lmm)
        tok_lrow = ctx.sig("vector", lambda v: v.tensor_copy(
            s_lrow[0:1, 0:NE0], pb[2][0:1, 0:NE0]))
        ctx.wait("tensor", tok_lrow)
        tok_lbc = ctx.sig("tensor", lambda t: t.matmul(
            pb[3][:, 0:NE0], s_onesf[0:1, :], s_lrow[0:1, 0:NE0], start=True, stop=True))
        ctx.wait("vector", tok_lbc, tok_wad, tok_wau)
        ctx.sig("vector", lambda v: v.tensor_copy(
            s_ind[:, NE0:2 * NE0], pb[3][:, 0:NE0]))
        ctx.op("vector", lambda v: v.drain())
        ctx.sig("vector", lambda v: v.tensor_reduce(
            s_ind[:, 2 * NE0:2 * NE0 + 1], s_ind[:, NE0:2 * NE0], AX.X, ALU.min))
        ctx.op("vector", lambda v: v.drain())
        tok_indbc = ctx.sig("vector", lambda v: v.tensor_scalar(
            s_ind[:, 0:NE0], s_ind[:, NE0:2 * NE0],
            s_ind[:, 2 * NE0:2 * NE0 + 1], None, ALU.is_equal))

        ctx.op("vector", lambda v: v.drain())
        for kc in range(NT):
            ctx.sig("vector", lambda v, kc=kc: v.tensor_scalar(
                s_wadsel[:, kc * R0:(kc + 1) * R0],
                s_wad[:, (0 * NT + kc) * R0:(0 * NT + kc + 1) * R0],
                s_ind[:, 0:1], None, ALU.mult))
            for e in range(1, NE0):
                ctx.op("vector", lambda v: v.drain())
                ctx.sig("vector", lambda v, kc=kc, e=e: v.scalar_tensor_tensor(
                    s_wadsel[:, kc * R0:(kc + 1) * R0],
                    s_wad[:, (e * NT + kc) * R0:(e * NT + kc + 1) * R0],
                    s_ind[:, e:e + 1], s_wadsel[:, kc * R0:(kc + 1) * R0],
                    ALU.mult, ALU.add))
        ctx.sig("vector", lambda v: v.tensor_scalar(
            s_wausel[:, :], s_wau[:, 0:D], s_ind[0:R0, 0:1], None, ALU.mult))
        for e in range(1, NE0):
            ctx.op("vector", lambda v: v.drain())
            ctx.sig("vector", lambda v, e=e: v.scalar_tensor_tensor(
                s_wausel[:, :], s_wau[:, e * D:(e + 1) * D],
                s_ind[0:R0, e:e + 1], s_wausel[:, :], ALU.mult, ALU.add))
        ctx.sig("vector", lambda v: v.tensor_scalar(
            s_badsel[:, 0:1], s_bad[:, 0:1], s_ind[0:R0, 0:1], None, ALU.mult))
        for e in range(1, NE0):
            ctx.op("vector", lambda v: v.drain())
            ctx.sig("vector", lambda v, e=e: v.scalar_tensor_tensor(
                s_badsel[:, 0:1], s_bad[:, e:e + 1], s_ind[0:R0, e:e + 1],
                s_badsel[:, 0:1], ALU.mult, ALU.add))
        for dt in range(NT):
            ctx.sig("vector", lambda v, dt=dt: v.tensor_tensor(
                s_l0b[:, NT:NT + NE0],
                _r(s_bau[:, :], "p (e c) -> p e c", e=NE0)[:, :, dt],
                s_ind[:, 0:NE0], ALU.mult))
            ctx.op("vector", lambda v: v.drain())
            ctx.sig("vector", lambda v, dt=dt: v.tensor_reduce(
                s_l0b[:, dt:dt + 1], s_l0b[:, NT:NT + NE0], AX.X, ALU.add))
            ctx.op("vector", lambda v: v.drain())
            ctx.sig("vector", lambda v, dt=dt: v.tensor_tensor(
                s_l0b[:, dt:dt + 1], s_l0b[:, dt:dt + 1],
                s_bbase[:, dt:dt + 1], ALU.add))
        tok_sel = ctx.now("vector")

        S2_mktap[0]("ffn_dbg", (128, 96))
        S2_mktap[0]("ffn_h7", (128, 2 * T), BF16)
        if "ffn_dbg" in ctx.debug_taps and getattr(ctx, "cur_b", 0) == 0:
            with nc.sbuf_tensor("f0_dbg_%d" % ctx.uid(), [128, 96], F32) as s_dbg_:
                ctx.sig("vector", lambda v: v.tensor_copy(s_dbg_[:, 0:33], s_loss[:, 0 * 33:1 * 33]))
                ctx.sig("vector", lambda v: v.tensor_copy(s_dbg_[:, 33:41], s_l8[:, :]))
                ctx.sig("vector", lambda v: v.tensor_copy(s_dbg_[:, 41:49], s_ind[:, 0:NE0]))
                ctx.sig("vector", lambda v: v.tensor_copy(s_dbg_[:, 49:57], s_l0b[:, 0:8]))
                ctx.sig("vector", lambda v: v.tensor_copy(s_dbg_[0:1, 57:57 + 24], s_lrow[0:1, :]))
                ctx.sig("vector", lambda v: v.tensor_copy(s_dbg_[:, 81:90], s_ind[:, NE0:2 * NE0 + 1]))
                S2_dump[0]("ffn_dbg", lambda: s_dbg_[:, :])
                S2_dump[0]("ffn_h7", lambda: s_h[:, :])
        ctx.wait("tensor", tok_sel)
        for c in range(NTC):
            for k in range(NT):
                ctx.sig("tensor", lambda t, k=k, c=c: t.matmul(
                    pb[3][0:R0, 0:TC], s_wadsel[:, k * R0:(k + 1) * R0],
                    s_xn[:, k * T + c * TC:k * T + (c + 1) * TC],
                    start=(k == 0), stop=(k == NT - 1)))
            tok_a = ctx.now("tensor")
            ctx.wait("scalar", tok_a)
            tok_ar = ctx.sig("scalar", lambda s, c=c: s.activation(
                s_abf[:, c * TC:(c + 1) * TC], pb[3][0:R0, 0:TC],
                AF.Relu, bias=s_badsel[:, 0:1]))
            ctx.wait("tensor", tok_ar)

        ctx.wait("tensor", tok_wbase)
        evac = [None, None]
        for dt in range(NT):
            for c in range(NTC):
                bnk = (dt * NTC + c) % 2
                p = pb[4 + bnk]
                ctx.wait("tensor", evac[bnk])
                ctx.sig("tensor", lambda t, dt=dt, c=c, p=p: t.matmul(
                    p[:, 0:TC], s_wo[:, dt * 128:(dt + 1) * 128],
                    s_oall[:, c * TC:(c + 1) * TC], start=True, stop=False))
                for k in range(NT):
                    ctx.sig("tensor", lambda t, k=k, dt=dt, c=c, p=p: t.matmul(
                        p[:, 0:TC], s_wbase[:, k * D + dt * 128:k * D + (dt + 1) * 128],
                        s_xn[:, k * T + c * TC:k * T + (c + 1) * TC],
                        start=False, stop=False))
                tok_mm = ctx.sig("tensor", lambda t, dt=dt, c=c, p=p: t.matmul(
                    p[:, 0:TC], s_wausel[:, dt * 128:(dt + 1) * 128],
                    s_abf[:, c * TC:(c + 1) * TC], start=False, stop=True))
                ctx.wait("vector", tok_mm)
                evac[bnk] = ctx.sig("vector", lambda v, dt=dt, c=c, p=p:
                                    v.scalar_tensor_tensor(
                    s_x[:, dt * T + c * TC:dt * T + (c + 1) * TC], p[:, 0:TC],
                    s_l0b[:, dt:dt + 1],
                    s_x[:, dt * T + c * TC:dt * T + (c + 1) * TC], ALU.add, ALU.add))
        ctx.wait("tensor", evac[0], evac[1])


def emit_l1_ssm(ctx, nc, S, i_winT, i_wxT, i_wdtT, i_woutT):
    s_xn, s_x, pb = S["xn"], S["x"], S["pb"]
    s_binc, s_bxc, s_bdtc, s_A, s_dc, s_boutc = (S["binc"], S["bxc"], S["bdtc"],
                                                 S["A"], S["dc"], S["boutc"])
    s_hstate, s_kone, s_onesb = S["hstate"], S["kone"], S["onesb"]
    es2 = ExitStack()
    s_win = es2.enter_context(nc.sbuf_tensor("sm_win_%d" % ctx.uid(), [128, 2 * NT * 128], BF16))
    s_wxT = es2.enter_context(nc.sbuf_tensor("sm_wxT_%d" % ctx.uid(), [128, NDI * 128], BF16))
    s_wdtT = es2.enter_context(nc.sbuf_tensor("sm_wdtT_%d" % ctx.uid(), [DTR, DI], BF16))
    s_wout = es2.enter_context(nc.sbuf_tensor("sm_wout_%d" % ctx.uid(), [128, NDI * 128], BF16))
    s_xs = es2.enter_context(nc.sbuf_tensor("sm_xs_%d" % ctx.uid(), [128, NDI * SEG], BF16))
    s_z = es2.enter_context(nc.sbuf_tensor("sm_z_%d" % ctx.uid(), [128, NDI * SEG], BF16))
    s_dl = es2.enter_context(nc.sbuf_tensor("sm_dl_%d" % ctx.uid(), [128, NDI * SEG], BF16))
    s_dx = es2.enter_context(nc.sbuf_tensor("sm_dx_%d" % ctx.uid(), [128, NDI * SEG], BF16))
    s_y = es2.enter_context(nc.sbuf_tensor("sm_y_%d" % ctx.uid(), [128, NDI * SEG], F32))
    s_draw = es2.enter_context(nc.sbuf_tensor("sm_draw_%d" % ctx.uid(), [DTR, SEG], BF16))
    s_bc = es2.enter_context(nc.sbuf_tensor("sm_bc_%d" % ctx.uid(), [R0, SEG], F32))
    s_bcb = es2.enter_context(nc.sbuf_tensor("sm_bcb_%d" % ctx.uid(), [R0, SEG], BF16))
    s_flat = es2.enter_context(nc.sbuf_tensor("sm_flat_%d" % ctx.uid(), [65, NS * SEG], BF16))
    s_ab = es2.enter_context(nc.sbuf_tensor("sm_ab_%d" % ctx.uid(), [128, 4 * SEG], F32))
    s_abig = es2.enter_context(nc.sbuf_tensor("sm_abig_%d" % ctx.uid(), [128, 8 * SEG], F32))
    s_bbig = es2.enter_context(nc.sbuf_tensor("sm_bbig_%d" % ctx.uid(), [128, NDI * SEG], BF16))
    s_hbig = es2.enter_context(nc.sbuf_tensor("sm_hbig_%d" % ctx.uid(), [128, 8 * SEG], BF16))
    s_g1 = es2.enter_context(nc.sbuf_tensor("sm_g1_%d" % ctx.uid(), [128, SEG], F32))
    s_Bs = es2.enter_context(nc.sbuf_tensor("sm_Bs_%d" % ctx.uid(), [128, SEG], BF16))
    s_Cs = es2.enter_context(nc.sbuf_tensor("sm_Cs_%d" % ctx.uid(), [128, SEG], BF16))
    if True:
        s_yg = s_abig[:, :].bitcast(BF16)[:, 0:NDI * SEG]
        s_fix = s_g1

        ctx.wait("sync", ctx.now("vector"), ctx.now("tensor"), ctx.now("scalar"))
        tok_wx = ctx.sig("sync", lambda y: y.dma_start(
            out=_r(s_wxT[:, :], "p (c m) -> p c m", c=NDI),
            in_=_r(i_wxT, "(c p) m -> p c m", p=128)), dma=True)
        tok_wdt = ctx.sig("sync", lambda y: y.dma_start(
            out=s_wdtT[:, :], in_=i_wdtT), dma=True)
        def _h0(v):
            v.memset(s_hstate[:, :], 0.0)
            return v.drain()
        ctx.op("vector", _h0)

        win_tok = [None, None]
        win_used = [None, None]
        wout_tok = [None, None]
        wout_used = [None, None]
        a_toks = [None, None]
        a_free = [None, None]
        gp_b = [None, None]
        gp_y = [None, None]
        scan_done = [None, None]
        gp_st = None
        bc_free = None

        for sg in range(NSEG):
            t0 = sg * SEG
            for m in range(32):
                sl = m % 2
                w0 = sl * NT * 128
                qe = "sync" if sl == 0 else "gpsimd"
                ctx.wait(qe, win_used[sl])
                win_tok[sl] = ctx.sig(qe, lambda y, m=m, w0=w0: y.dma_start(
                    out=_r(s_win[:, w0:w0 + NT * 128], "p (c m) -> p c m", c=NT),
                    in_=_r(i_winT[:, m * 128:(m + 1) * 128], "(c p) m -> p c m", p=128)),
                    dma=True)
                ctx.wait("tensor", win_tok[sl])
                for k in range(NT):
                    ctx.sig("tensor", lambda t, k=k, t0=t0, w0=w0: t.matmul(
                        pb[0][:, 0:SEG],
                        s_win[:, w0 + k * 128:w0 + (k + 1) * 128],
                        s_xn[:, k * T + t0:k * T + t0 + SEG],
                        start=(k == 0), stop=(k == NT - 1)))
                win_used[sl] = ctx.now("tensor")
                ctx.wait("vector", win_used[sl])
                if m < NDI:
                    ctx.sig("vector", lambda v, m=m: v.tensor_scalar(
                        s_xs[:, m * SEG:(m + 1) * SEG], pb[0][:, 0:SEG],
                        s_binc[:, m:m + 1], None, ALU.add))
                    tok_ev = ctx.sig("vector", lambda v, m=m: v.tensor_scalar(
                        s_y[:, m * SEG:(m + 1) * SEG], pb[0][:, 0:SEG],
                        s_binc[:, m:m + 1], s_dc[:, m:m + 1], ALU.add, ALU.mult))
                else:
                    tok_ev = ctx.sig("vector", lambda v, m=m: v.tensor_scalar(
                        s_z[:, (m - NDI) * SEG:(m - NDI + 1) * SEG], pb[0][:, 0:SEG],
                        s_binc[:, m:m + 1], None, ALU.add))
                ctx.wait("tensor", tok_ev)
            ctx.wait("tensor", tok_wx)
            for k in range(NDI):
                ctx.sig("tensor", lambda t, k=k: t.matmul(
                    pb[1][:, 0:SEG], s_wxT[:, k * 128:(k + 1) * 128],
                    s_xs[:, k * SEG:(k + 1) * SEG], start=(k == 0), stop=(k == NDI - 1)))
            tok_dbc = ctx.now("tensor")
            ctx.wait("vector", tok_dbc)
            ctx.sig("vector", lambda v: v.tensor_scalar(
                s_draw[:, :], pb[1][0:DTR, 0:SEG], s_bxc[0:DTR, 0:1], None, ALU.add))
            ctx.sig("vector", lambda v: v.tensor_scalar(
                s_bc[:, :], pb[1][DTR:128, 0:SEG], s_bxc[DTR:128, 0:1], None, ALU.add))
            ctx.op("vector", lambda v: v.drain())
            tok_bc = ctx.sig("vector", lambda v: v.tensor_copy(s_bcb[:, :], s_bc[:, :]))
            ctx.wait("tensor", tok_bc)
            ctx.wait("sync", tok_bc)
            tok_fb = ctx.sig("sync", lambda y: y.dma_start(
                out=_r(s_flat[0:1, 0:NS * SEG], "a (n t) -> a n t", n=NS),
                in_=s_bcb[0:NS, :]), dma=True)
            tok_fc = ctx.sig("sync", lambda y: y.dma_start(
                out=_r(s_flat[64:65, 0:NS * SEG], "a (n t) -> a n t", n=NS),
                in_=s_bcb[32:32 + NS, :]), dma=True)
            ctx.wait("tensor", tok_wdt)
            dl_done = [None, None]
            ab_fr = [None, None]
            for dt in range(NDI):
                par = dt % 2
                p = pb[2] if par == 0 else pb[5]
                a0 = par * 2 * SEG
                ctx.wait("tensor", dl_done[par])
                tok_dm = ctx.sig("tensor", lambda t, dt=dt, p=p: t.matmul(
                    p[:, 0:SEG], s_wdtT[:, dt * 128:(dt + 1) * 128],
                    s_draw[:, :], start=True, stop=True))
                ctx.wait("scalar", tok_dm, ab_fr[par])
                ctx.sig("scalar", lambda s, dt=dt, p=p, a0=a0: s.activation(
                    s_ab[:, a0:a0 + SEG], p[:, 0:SEG], AF.Abs,
                    bias=s_bdtc[:, dt:dt + 1]))
                ctx.sig("scalar", lambda s, a0=a0: s.activation(
                    s_ab[:, a0:a0 + SEG], s_ab[:, a0:a0 + SEG], AF.Exp, scale=-1.0))
                ctx.sig("scalar", lambda s, a0=a0: s.activation(
                    s_ab[:, a0:a0 + SEG], s_ab[:, a0:a0 + SEG], AF.Ln, bias=s_kone[:, 0:1]))
                tok_sp = ctx.sig("scalar", lambda s, dt=dt, p=p, a0=a0: s.activation(
                    s_ab[:, a0 + SEG:a0 + 2 * SEG], p[:, 0:SEG], AF.Relu,
                    bias=s_bdtc[:, dt:dt + 1]))
                ctx.wait("vector", tok_sp)
                dl_done[par] = ctx.sig("vector", lambda v, dt=dt, a0=a0: v.tensor_tensor(
                    s_dl[:, dt * SEG:(dt + 1) * SEG], s_ab[:, a0:a0 + SEG],
                    s_ab[:, a0 + SEG:a0 + 2 * SEG], ALU.add))
                ctx.op("vector", lambda v: v.drain())
                ab_fr[par] = ctx.sig("vector", lambda v, dt=dt: v.tensor_tensor(
                    s_dx[:, dt * SEG:(dt + 1) * SEG],
                    s_dl[:, dt * SEG:(dt + 1) * SEG],
                    s_xs[:, dt * SEG:(dt + 1) * SEG], ALU.mult))
            tok_dl = ctx.now("vector")

            ctx.wait("tensor", tok_fb, tok_fc, tok_dl)
            ctx.wait("gpsimd", tok_dl)
            for n in range(NS):
                ctx.wait("tensor", bc_free)
                tok_Bb = ctx.sig("tensor", lambda t, n=n: t.matmul(
                    pb[3][:, 0:SEG], s_onesb[0:1, :],
                    s_flat[0:1, n * SEG:(n + 1) * SEG], start=True, stop=True))
                tok_Cb = ctx.sig("tensor", lambda t, n=n: t.matmul(
                    pb[4][:, 0:SEG], s_onesb[64:65, :],
                    s_flat[64:65, n * SEG:(n + 1) * SEG],
                    start=True, stop=True))
                ctx.wait("vector", tok_Bb, tok_Cb, gp_b[0], gp_b[1])
                tok_bc_ev = ctx.sig("vector", lambda v: v.tensor_copy(s_Bs[:, :], pb[3][:, 0:SEG]))
                ctx.wait("gpsimd", tok_bc_ev)
                for hf in range(2):
                    ctx.wait("gpsimd", scan_done[hf])
                    gp_b[hf] = ctx.sig("gpsimd", lambda g, hf=hf: g.tensor_tensor(
                        _r(s_bbig[:, hf * 8 * SEG:(hf + 1) * 8 * SEG], "p (c t) -> p c t", c=8),
                        _r(s_dx[:, hf * 8 * SEG:(hf + 1) * 8 * SEG], "p (c t) -> p c t", c=8),
                        s_Bs[:, None, 0:SEG].broadcast_to([128, 8, SEG]), ALU.mult))
                for hf in range(2):
                    ctx.wait("scalar", a_free[0])
                    for j in range(8):
                        dt = hf * 8 + j
                        a_toks[0] = ctx.sig("scalar", lambda s, dt=dt, n=n, j=j: s.activation(
                            s_abig[:, j * SEG:(j + 1) * SEG],
                            s_dl[:, dt * SEG:(dt + 1) * SEG],
                            AF.Exp, scale=s_A[:, dt * NS + n:dt * NS + n + 1]))
                    a_first = _r(s_abig[:, :], "p (c t) -> p c t", c=8)[:, 1:8, 0:1]
                    b_first = _r(s_bbig[:, hf * 8 * SEG:(hf + 1) * 8 * SEG],
                                 "p (c t) -> p c t", c=8)[:, 1:8, 0:1]
                    hs_v = s_hstate[:, n * NDI + hf * 8 + 1:n * NDI + hf * 8 + 8, None]
                    ctx.wait("vector", a_toks[0], gp_b[hf], gp_st)
                    ctx.sig("vector", lambda v, a=a_first, h=hs_v: v.tensor_tensor(
                        s_fix[:, 0:7, None], a, h, ALU.mult))
                    ctx.op("vector", lambda v: v.drain())
                    ctx.sig("vector", lambda v, b=b_first: v.tensor_tensor(
                        b, b, s_fix[:, 0:7, None], ALU.add))
                    def _za(v, a=a_first):
                        v.memset(a, 0.0)
                        return v.drain()
                    ctx.op("vector", _za)
                    tok_scan = ctx.sig("vector", lambda v, hf=hf, n=n: v.tensor_tensor_scan(
                        s_hbig[:, :], s_abig[:, :],
                        s_bbig[:, hf * 8 * SEG:(hf + 1) * 8 * SEG],
                        s_hstate[:, n * NDI + hf * 8:n * NDI + hf * 8 + 1],
                        ALU.mult, ALU.add))
                    a_free[0] = tok_scan
                    scan_done[hf] = tok_scan
                    ctx.wait("gpsimd", tok_scan)
                    gp_st = ctx.sig("gpsimd", lambda g, hf=hf, n=n: g.tensor_copy(
                        s_hstate[:, n * NDI + hf * 8:n * NDI + hf * 8 + 8],
                        _r(s_hbig[:, :], "p (c t) -> p c t", c=8)[:, :, SEG - 1:SEG].opt()))
                    ctx.op("vector", lambda v: v.drain())
                    ctx.wait("vector", gp_y[hf])
                    tok_hc = ctx.sig("vector", lambda v, hf=hf: v.tensor_tensor(
                        _r(s_xs[:, hf * 8 * SEG:(hf + 1) * 8 * SEG], "p (c t) -> p c t", c=8),
                        _r(s_hbig[:, :], "p (c t) -> p c t", c=8),
                        pb[4][:, None, 0:SEG].broadcast_to([128, 8, SEG]), ALU.mult))
                    ctx.wait("gpsimd", tok_hc)
                    gp_y[hf] = ctx.sig("gpsimd", lambda g, hf=hf: g.tensor_tensor(
                        s_y[:, hf * 8 * SEG:(hf + 1) * 8 * SEG],
                        s_y[:, hf * 8 * SEG:(hf + 1) * 8 * SEG],
                        s_xs[:, hf * 8 * SEG:(hf + 1) * 8 * SEG], ALU.add))
                    bc_free = tok_hc
            ctx.wait("vector", gp_y[0], gp_y[1])
            ctx.wait("scalar", gp_y[0], gp_y[1])
            ctx.wait("scalar", ctx.now("vector"))
            for dt in range(NDI):
                tok_tz = ctx.sig("scalar", lambda s, dt=dt: s.activation(
                    s_g1[:, 0:SEG], s_z[:, dt * SEG:(dt + 1) * SEG], AF.Tanh, scale=0.5))
                ctx.wait("vector", tok_tz)
                ctx.sig("vector", lambda v, dt=dt: v.scalar_tensor_tensor(
                    s_g1[:, 0:SEG], s_g1[:, 0:SEG], 1.0,
                    s_z[:, dt * SEG:(dt + 1) * SEG], ALU.add, ALU.mult))
                ctx.op("vector", lambda v: v.drain())
                tok_yg = ctx.sig("vector", lambda v, dt=dt: v.scalar_tensor_tensor(
                    s_yg[:, dt * SEG:(dt + 1) * SEG], s_y[:, dt * SEG:(dt + 1) * SEG],
                    0.5, s_g1[:, 0:SEG], ALU.mult, ALU.mult))
                ctx.wait("scalar", tok_yg)
            tok_yg_all = ctx.now("vector")
            if getattr(ctx, "cur_b", 0) == 0 and sg == 0:
                S2_mktap[0]("ssm_yg", (128, NDI * SEG), BF16)
                S2_dump[0]("ssm_yg", lambda: s_yg[:, :])
            ctx.wait("tensor", tok_yg_all)
            for dp in range(NT):
                ctx.wait("sync", wout_used[0])
                wout_tok[0] = ctx.sig("sync", lambda y, dp=dp: y.dma_start(
                    out=_r(s_wout[:, :], "p (c m) -> p c m", c=NDI),
                    in_=_r(i_woutT[:, dp * 128:(dp + 1) * 128], "(c p) m -> p c m", p=128)),
                    dma=True)
                ctx.wait("tensor", wout_tok[0])
                for k in range(NDI):
                    ctx.sig("tensor", lambda t, k=k: t.matmul(
                        pb[5][:, 0:SEG],
                        s_wout[:, k * 128:(k + 1) * 128],
                        s_yg[:, k * SEG:(k + 1) * SEG], start=(k == 0), stop=(k == NDI - 1)))
                wout_used[0] = ctx.now("tensor")
                ctx.wait("vector", wout_used[0])
                tok_xev = ctx.sig("vector", lambda v, dp=dp, t0=t0: v.scalar_tensor_tensor(
                    s_x[:, dp * T + t0:dp * T + t0 + SEG], pb[5][:, 0:SEG],
                    s_boutc[:, dp:dp + 1],
                    s_x[:, dp * T + t0:dp * T + t0 + SEG], ALU.add, ALU.add))
                ctx.wait("tensor", tok_xev)
            tok_seg = ctx.now("vector")
            ctx.wait("tensor", tok_seg)
            ctx.wait("scalar", tok_seg)
            ctx.wait("scalar", wout_used[0])
            ctx.wait("vector", wout_used[0])


    es2.close()


def emit_l1_moe(ctx, nc, S, i_gwT, i_uwT, i_dwT, i_sgwT, i_suwT, i_sdwT):
    s_xn, s_x, pb, pbh = S["xn"], S["x"], S["pb"], S["pbh"]
    s_rwT, s_rbb, s_onesb, s_onesf, s_ident, s_kone = (S["rwT"], S["rbb"], S["onesb"],
                                                       S["onesf"], S["ident"], S["kone"])
    es_moe = ExitStack()
    s_wTb = es_moe.enter_context(
        nc.sbuf_tensor("mo_wTb_%d" % ctx.uid(), [NE1, T], BF16))
    s_wrow = es_moe.enter_context(
        nc.sbuf_tensor("mo_wrow_%d" % ctx.uid(), [1, T], BF16))
    with nc.sbuf_tensor("mo_ro_%d" % ctx.uid(), [128, 16 * NE1], F32) as s_ro, \
         nc.sbuf_tensor("mo_wdb_%d" % ctx.uid(), [128, NE1], BF16) as s_wdb:
        C0 = 0
        def cols(j):
            return slice(j * NE1, (j + 1) * NE1)
        ctx.auto_drain = True
        for tile in range(16):
            tt = tile * 128
            for k in range(NT):
                ctx.sig("tensor", lambda t, k=k, tt=tt: t.matmul(
                    pb[0][:, 0:NE1], s_xn[:, k * T + tt:k * T + tt + 128],
                    s_rwT[:, k * NE1:(k + 1) * NE1], start=(k == 0), stop=(k == NT - 1)))
            tok_lg = ctx.now("tensor")
            ctx.wait("scalar", tok_lg)
            tok_th = ctx.sig("scalar", lambda s: s.activation(
                s_ro[:, cols(0)], pb[0][:, 0:NE1], AF.Tanh, scale=0.5))
            ctx.wait("vector", tok_th)
            ctx.sig("vector", lambda v: v.tensor_scalar(
                s_ro[:, cols(1)], s_ro[:, cols(0)], 1.0, 0.5, ALU.add, ALU.mult))
            tok_sfc = ctx.sig("vector", lambda v: v.tensor_tensor(
                s_ro[:, cols(2)], s_ro[:, cols(1)], s_rbb[:, :], ALU.add))
            ctx.wait("tensor", tok_sfc)
            mcol = 10 * NE1
            for g in range(4):
                gsl = slice(2 * NE1 + 8 * g, 2 * NE1 + 8 * (g + 1))
                ctx.op("vector", lambda v: v.drain())
                ctx.sig("vector", lambda v, gsl=gsl, mcol=mcol: v.tensor_reduce(
                    s_ro[:, mcol:mcol + 1], s_ro[:, gsl], AX.X, ALU.max))
                ctx.op("vector", lambda v: v.drain())
                ctx.sig("vector", lambda v, gsl=gsl, mcol=mcol: v.tensor_scalar(
                    s_ro[:, cols(6)][:, 0:8], s_ro[:, gsl], s_ro[:, mcol:mcol + 1],
                    None, ALU.is_equal))
                ctx.op("vector", lambda v: v.drain())
                ctx.sig("vector", lambda v, gsl=gsl: v.scalar_tensor_tensor(
                    s_ro[:, cols(6)][:, 8:16], s_ro[:, cols(6)][:, 0:8], -1e30,
                    s_ro[:, gsl], ALU.mult, ALU.add))
                ctx.op("vector", lambda v: v.drain())
                ctx.sig("vector", lambda v, mcol=mcol: v.tensor_reduce(
                    s_ro[:, mcol + 1:mcol + 2], s_ro[:, cols(6)][:, 8:16], AX.X, ALU.max))
                ctx.op("vector", lambda v: v.drain())
                ctx.sig("vector", lambda v, g=g, mcol=mcol: v.tensor_tensor(
                    s_ro[:, 3 * NE1 + g:3 * NE1 + g + 1], s_ro[:, mcol:mcol + 1],
                    s_ro[:, mcol + 1:mcol + 2], ALU.add))
            gsl4 = slice(3 * NE1, 3 * NE1 + 4)
            ctx.op("vector", lambda v: v.drain())
            ctx.sig("vector", lambda v: v.tensor_reduce(
                s_ro[:, mcol:mcol + 1], s_ro[:, gsl4], AX.X, ALU.max))
            ctx.op("vector", lambda v: v.drain())
            ctx.sig("vector", lambda v: v.tensor_scalar(
                s_ro[:, cols(6)][:, 0:4], s_ro[:, gsl4], s_ro[:, mcol:mcol + 1],
                None, ALU.is_equal))
            ctx.op("vector", lambda v: v.drain())
            ctx.sig("vector", lambda v: v.scalar_tensor_tensor(
                s_ro[:, cols(6)][:, 8:12], s_ro[:, cols(6)][:, 0:4], -1e30,
                s_ro[:, gsl4], ALU.mult, ALU.add))
            ctx.op("vector", lambda v: v.drain())
            ctx.sig("vector", lambda v: v.tensor_reduce(
                s_ro[:, mcol + 1:mcol + 2], s_ro[:, cols(6)][:, 8:12], AX.X, ALU.max))
            ctx.op("vector", lambda v: v.drain())
            ctx.sig("vector", lambda v: v.tensor_scalar(
                s_ro[:, 3 * NE1 + 8:3 * NE1 + 12], s_ro[:, gsl4],
                s_ro[:, mcol + 1:mcol + 2], None, ALU.is_ge))
            ctx.sig("vector", lambda v: v.tensor_tensor(
                _r(s_ro[:, cols(4)], "p (g j) -> p g j", g=4),
                _r(s_ro[:, cols(2)], "p (g j) -> p g j", g=4),
                s_ro[:, 3 * NE1 + 8:3 * NE1 + 12][:, :, None].broadcast_to([128, 4, 8]),
                ALU.mult))
            ctx.sig("vector", lambda v: v.tensor_copy(s_ro[:, cols(5)], s_ro[:, cols(4)]))
            ctx.op("vector", lambda v: v.drain())
            def _z(v):
                v.memset(s_ro[:, cols(8)], 0.0)
                return v.drain()
            ctx.op("vector", _z)
            for i4 in range(4):
                ctx.op("vector", lambda v: v.drain())
                ctx.sig("vector", lambda v, mcol=mcol: v.tensor_reduce(
                    s_ro[:, mcol + 2:mcol + 3], s_ro[:, cols(5)], AX.X, ALU.max))
                ctx.op("vector", lambda v: v.drain())
                ctx.sig("vector", lambda v, mcol=mcol: v.tensor_scalar(
                    s_ro[:, cols(6)], s_ro[:, cols(5)], s_ro[:, mcol + 2:mcol + 3],
                    None, ALU.is_equal))
                ctx.op("vector", lambda v: v.drain())
                ctx.sig("vector", lambda v: v.tensor_tensor_scan(
                    s_ro[:, cols(7)], s_onesf[:, 0:NE1], s_ro[:, cols(6)],
                    0.0, ALU.mult, ALU.add))
                ctx.op("vector", lambda v: v.drain())
                ctx.sig("vector", lambda v: v.tensor_scalar(
                    s_ro[:, cols(7)], s_ro[:, cols(7)], 1.0, None, ALU.is_equal))
                ctx.op("vector", lambda v: v.drain())
                ctx.sig("vector", lambda v: v.tensor_tensor(
                    s_ro[:, cols(6)], s_ro[:, cols(6)], s_ro[:, cols(7)], ALU.mult))
                ctx.op("vector", lambda v: v.drain())
                ctx.sig("vector", lambda v: v.tensor_tensor(
                    s_ro[:, cols(8)], s_ro[:, cols(8)], s_ro[:, cols(6)], ALU.add))
                ctx.sig("vector", lambda v: v.scalar_tensor_tensor(
                    s_ro[:, cols(5)], s_ro[:, cols(6)], -1e30, s_ro[:, cols(5)],
                    ALU.mult, ALU.add))
                ctx.op("vector", lambda v: v.drain())
            ctx.sig("vector", lambda v: v.tensor_tensor(
                s_ro[:, cols(9)], s_ro[:, cols(8)], s_ro[:, cols(1)], ALU.mult))
            ctx.op("vector", lambda v: v.drain())
            ctx.sig("vector", lambda v, mcol=mcol: v.tensor_reduce(
                s_ro[:, mcol + 3:mcol + 4], s_ro[:, cols(9)], AX.X, ALU.add))
            ctx.op("vector", lambda v: v.drain())
            tok_den = ctx.sig("vector", lambda v, mcol=mcol: v.tensor_scalar(
                s_ro[:, mcol + 3:mcol + 4], s_ro[:, mcol + 3:mcol + 4], 1e-20,
                None, ALU.add))
            ctx.wait("scalar", tok_den)
            ctx.sig("scalar", lambda s, mcol=mcol: s.activation(
                s_ro[:, mcol + 4:mcol + 5], s_ro[:, mcol + 3:mcol + 4], AF.Ln))
            tok_wi = ctx.sig("scalar", lambda s, mcol=mcol: s.activation(
                s_ro[:, mcol + 4:mcol + 5], s_ro[:, mcol + 4:mcol + 5], AF.Exp, scale=-1.0))
            ctx.wait("vector", tok_wi)
            tok_wd = ctx.sig("vector", lambda v, mcol=mcol: v.tensor_scalar(
                s_wdb[:, :], s_ro[:, cols(9)], s_ro[:, mcol + 4:mcol + 5], 2.5,
                ALU.mult, ALU.mult))
            ctx.wait("tensor", tok_wd)
            tok_tr = ctx.sig("tensor", lambda t: t.transpose(
                pbh[0:NE1, 0:128], s_wdb[:, :], s_ident[:, :]))
            ctx.wait("vector", tok_tr)
            tok_wT = ctx.sig("vector", lambda v, tt=tt: v.tensor_copy(
                s_wTb[:, tt:tt + 128], pbh[0:NE1, 0:128]))
            ctx.wait("tensor", tok_wT)
        tok_wTb = ctx.now("vector")
        ctx.auto_drain = False
        if getattr(ctx, "cur_b", 0) == 0:
            S2_mktap[0]("moe_ro", (128, 16 * NE1))
            S2_mktap[0]("rbb", (128, NE1))
            S2_dump[0]("rbb", lambda: s_rbb[:, :])
            S2_mktap[0]("wTb", (NE1, T), BF16)
            S2_dump[0]("moe_ro", lambda: s_ro[:, :])
            S2_dump[0]("wTb", lambda: s_wTb[:, :])

    with nc.sbuf_tensor("sh_gw_%d" % ctx.uid(), [128, NT * SHI], BF16) as s_sgw, \
         nc.sbuf_tensor("sh_uw_%d" % ctx.uid(), [128, NT * SHI], BF16) as s_suw, \
         nc.sbuf_tensor("sh_dw_%d" % ctx.uid(), [128, NT * D], BF16) as s_sdw, \
         nc.sbuf_tensor("sh_hid_%d" % ctx.uid(), [128, NT * TC], BF16) as s_shid, \
         nc.sbuf_tensor("sh_sg_%d" % ctx.uid(), [128, TC], F32) as s_ssg:
        ctx.wait("sync", ctx.now("vector"), ctx.now("tensor"), ctx.now("scalar"))
        tok_g = ctx.sig("sync", lambda y: y.dma_start(
            out=_r(s_sgw[:, :], "p (c m) -> p c m", c=NT),
            in_=_r(i_sgwT, "(c p) m -> p c m", p=128)), dma=True)
        tok_u = ctx.sig("sync", lambda y: y.dma_start(
            out=_r(s_suw[:, :], "p (c m) -> p c m", c=NT),
            in_=_r(i_suwT, "(c p) m -> p c m", p=128)), dma=True)
        tok_d = ctx.sig("sync", lambda y: y.dma_start(
            out=_r(s_sdw[:, :], "p (c m) -> p c m", c=NT),
            in_=_r(i_sdwT, "(c p) m -> p c m", p=128)), dma=True)
        ctx.wait("tensor", tok_g, tok_u, tok_d)
        shev_tok = [None, None]
        for c in range(NTC):
            for it in range(NT):
                for k in range(NT):
                    ctx.sig("tensor", lambda t, k=k, it=it, c=c: t.matmul(
                        pb[0][:, 0:TC], s_sgw[:, k * SHI + it * 128:k * SHI + (it + 1) * 128],
                        s_xn[:, k * T + c * TC:k * T + (c + 1) * TC],
                        start=(k == 0), stop=(k == NT - 1)))
                for k in range(NT):
                    ctx.sig("tensor", lambda t, k=k, it=it, c=c: t.matmul(
                        pb[1][:, 0:TC], s_suw[:, k * SHI + it * 128:k * SHI + (it + 1) * 128],
                        s_xn[:, k * T + c * TC:k * T + (c + 1) * TC],
                        start=(k == 0), stop=(k == NT - 1)))
                tok_gu = ctx.now("tensor")
                ctx.wait("scalar", tok_gu)
                tok_sg = ctx.sig("scalar", lambda s: s.activation(
                    s_ssg[:, :], pb[0][:, 0:TC], AF.Silu))
                ctx.wait("vector", tok_sg)
                tok_hid = ctx.sig("vector", lambda v, it=it: v.tensor_tensor(
                    s_shid[:, it * TC:(it + 1) * TC], s_ssg[:, :], pb[1][:, 0:TC],
                    ALU.mult))
                ctx.wait("tensor", tok_hid)
            for dp in range(NT):
                bnk = 2 + dp % 2
                ctx.wait("tensor", shev_tok[dp % 2])
                for k in range(NT):
                    ctx.sig("tensor", lambda t, k=k, dp=dp, bnk=bnk: t.matmul(
                        pb[bnk][:, 0:TC], s_sdw[:, k * D + dp * 128:k * D + (dp + 1) * 128],
                        s_shid[:, k * TC:(k + 1) * TC], start=(k == 0), stop=(k == NT - 1)))
                tok_o = ctx.now("tensor")
                ctx.wait("vector", tok_o)
                shev_tok[dp % 2] = ctx.sig("vector", lambda v, dp=dp, c=c, bnk=bnk:
                                           v.scalar_tensor_tensor(
                    s_x[:, dp * T + c * TC:dp * T + (c + 1) * TC], pb[bnk][:, 0:TC], 1.0,
                    s_x[:, dp * T + c * TC:dp * T + (c + 1) * TC], ALU.mult, ALU.add))
            ctx.wait("tensor", shev_tok[0], shev_tok[1])

    with nc.sbuf_tensor("mo_gw_%d" % ctx.uid(), [128, 2 * NT * I1], BF16) as s_gw, \
         nc.sbuf_tensor("mo_uw_%d" % ctx.uid(), [128, 2 * NT * I1], BF16) as s_uw, \
         nc.sbuf_tensor("mo_dw_%d" % ctx.uid(), [128, 2 * 4 * D], BF16) as s_dw, \
         nc.sbuf_tensor("mo_hid_%d" % ctx.uid(), [128, 4 * T], BF16) as s_hid, \
         nc.sbuf_tensor("mo_sg_%d" % ctx.uid(), [128, 2 * TC], F32) as s_sg, \
         nc.sbuf_tensor("mo_t1_%d" % ctx.uid(), [128, 2 * TC], F32) as s_t1:
        ld_tok = [None, None]
        used_tok = [None, None]
        hid_done = [None, None]
        sg_free = [None, None]
        tok_wb_last = None
        for e in range(NE1):
            sl = e % 2
            if e == 0:
                ctx.wait("sync", ctx.now("vector"), ctx.now("tensor"), tok_wTb)
            ctx.wait("sync", used_tok[sl])
            ctx.wait("sync", tok_wb_last)
            tok_row = ctx.sig("sync", lambda y, e=e: y.dma_start(
                out=s_wrow[0:1, :], in_=s_wTb[e:e + 1, :]), dma=True)
            ctx.sig("sync", lambda y, e=e, sl=sl: y.dma_start(
                out=_r(s_gw[:, sl * NT * I1:(sl + 1) * NT * I1], "p (c m) -> p c m", c=NT),
                in_=_r(i_gwT[e], "(c p) m -> p c m", p=128)), dma=True)
            ctx.sig("sync", lambda y, e=e, sl=sl: y.dma_start(
                out=_r(s_uw[:, sl * NT * I1:(sl + 1) * NT * I1], "p (c m) -> p c m", c=NT),
                in_=_r(i_uwT[e], "(c p) m -> p c m", p=128)), dma=True)
            ld_tok[sl] = ctx.sig("sync", lambda y, e=e, sl=sl: y.dma_start(
                out=_r(s_dw[:, sl * 4 * D:(sl + 1) * 4 * D], "p (c m) -> p c m", c=4),
                in_=_r(i_dwT[e], "(c p) m -> p c m", p=128)), dma=True)
            ctx.wait("tensor", ld_tok[sl])
            g0 = sl * NT * I1
            d0 = sl * 4 * D
            for c in range(NTC):
                wpar = c % 2
                ctx.wait("tensor", tok_row, hid_done[0], hid_done[1])
                tok_wb = ctx.sig("tensor", lambda t, c=c: t.matmul(
                    pb[2][:, 0:TC], s_onesb[0:1, :],
                    s_wrow[0:1, c * TC:(c + 1) * TC],
                    start=True, stop=True))
                tok_wb_last = tok_wb
                for it in range(4):
                    par = it % 2
                    pg, pu = pb[0 + 5 * par], pb[1 + 5 * par]
                    ctx.wait("tensor", hid_done[par])
                    for k in range(NT):
                        ctx.sig("tensor", lambda t, k=k, it=it, c=c, g0=g0, pg=pg: t.matmul(
                            pg[:, 0:TC],
                            s_gw[:, g0 + k * I1 + it * 128:g0 + k * I1 + (it + 1) * 128],
                            s_xn[:, k * T + c * TC:k * T + (c + 1) * TC],
                            start=(k == 0), stop=(k == NT - 1)))
                    for k in range(NT):
                        ctx.sig("tensor", lambda t, k=k, it=it, c=c, g0=g0, pu=pu: t.matmul(
                            pu[:, 0:TC],
                            s_uw[:, g0 + k * I1 + it * 128:g0 + k * I1 + (it + 1) * 128],
                            s_xn[:, k * T + c * TC:k * T + (c + 1) * TC],
                            start=(k == 0), stop=(k == NT - 1)))
                    tok_gu = ctx.now("tensor")
                    ctx.wait("scalar", tok_gu, sg_free[par])
                    tok_sg = ctx.sig("scalar", lambda s, par=par, pg=pg: s.activation(
                        s_sg[:, par * TC:(par + 1) * TC], pg[:, 0:TC], AF.Silu))
                    ctx.wait("vector", tok_sg, tok_wb)
                    ctx.sig("vector", lambda v, par=par, pu=pu: v.tensor_tensor(
                        s_t1[:, par * TC:(par + 1) * TC],
                        s_sg[:, par * TC:(par + 1) * TC], pu[:, 0:TC], ALU.mult))
                    ctx.op("vector", lambda v: v.drain())
                    tok_hid = ctx.sig("vector", lambda v, it=it, c=c, par=par: v.tensor_tensor(
                        s_hid[:, it * T + c * TC:it * T + (c + 1) * TC],
                        s_t1[:, par * TC:(par + 1) * TC],
                        pb[2][:, 0:TC], ALU.mult))
                    hid_done[par] = tok_hid
                    sg_free[par] = tok_hid
            ctx.wait("tensor", hid_done[0], hid_done[1])
            dev_tok = [None, None]
            for c in range(NTC):
                for dp in range(NT):
                    i2 = c * NT + dp
                    bnk = 3 + (i2 % 2)
                    ctx.wait("tensor", dev_tok[i2 % 2])
                    for k in range(4):
                        ctx.sig("tensor", lambda t, k=k, dp=dp, c=c, d0=d0, bnk=bnk: t.matmul(
                            pb[bnk][:, 0:TC],
                            s_dw[:, d0 + k * D + dp * 128:d0 + k * D + (dp + 1) * 128],
                            s_hid[:, k * T + c * TC:k * T + (c + 1) * TC],
                            start=(k == 0), stop=(k == 3)))
                    tok_o = ctx.now("tensor")
                    ctx.wait("vector", tok_o)
                    dev_tok[i2 % 2] = ctx.sig("vector", lambda v, dp=dp, c=c, bnk=bnk:
                                              v.scalar_tensor_tensor(
                        s_x[:, dp * T + c * TC:dp * T + (c + 1) * TC],
                        pb[bnk][:, 0:TC], 1.0,
                        s_x[:, dp * T + c * TC:dp * T + (c + 1) * TC],
                        ALU.mult, ALU.add))
            ctx.wait("tensor", dev_tok[0], dev_tok[1])
            used_tok[sl] = ctx.now("tensor")
    es_moe.close()


def build_program(debug_taps=()):
    nc = bass.Bass("TRN2", target_bir_lowering=False)
    es = ExitStack()

    def din(name, shape, dt=F32):
        return nc.dram_tensor(name, list(shape), dt, kind="ExternalInput").ap()

    d_x = din("x_in", (BPC, D, T))
    d_wq = din("w_q", (D, H)); d_wk = din("w_k", (D, H)); d_wv = din("w_v", (D, H))
    d_wo = din("w_o", (H, D))
    d_wbase = din("w_base", (D, D))
    d_wd1 = din("wd1", (NE0, D, H0)); d_wd2 = din("wd2", (NE0, H0, D))
    d_wad = din("wa_down", (NE0, D, R0)); d_wau = din("wa_up", (NE0, R0, D))
    d_bbase = din("bb_base", (128, NT))
    d_bd1 = din("bd1_c", (128, NE0, 2))
    d_bd2 = din("bd2_c", (128, NE0, NT))
    d_bad = din("bad_c", (R0, NE0))
    d_bau = din("bau_c", (128, NE0, NT))
    d_winT = din("w_inT", (D, 2 * DI))
    d_wxT = din("w_xT", (DI, 128))
    d_wdtT = din("w_dtT", (DTR, DI))
    d_woutT = din("w_outT", (DI, D))
    d_binc = din("b_in_c", (128, 32))
    d_bxc = din("b_x_c", (128, 1))
    d_bdtc = din("b_dt_c", (128, NDI))
    d_alogc = din("a_log_c", (128, NDI * NS))
    d_dc = din("d_c", (128, NDI))
    d_boutc = din("b_out_c", (128, NT))
    d_rwT = din("rw_T", (D, NE1))
    d_rb = din("r_bias", (1, NE1))
    d_gwT = din("gw_T", (NE1, D, I1))
    d_uwT = din("uw_T", (NE1, D, I1))
    d_dwT = din("dw_T", (NE1, I1, D))
    d_sgwT = din("sgw_T", (D, SHI))
    d_suwT = din("suw_T", (D, SHI))
    d_sdwT = din("sdw_T", (SHI, D))
    d_ident = din("c_ident", (128, 128), BF16)
    d_triu = din("c_triu", (128, 128), BF16)
    d_onesb = din("c_ones_b", (128, 128), BF16)
    d_onesf = din("c_ones_f", (128, 128), F32)

    d_out = nc.dram_tensor("x_out", [BPC, D, T], F32, kind="ExternalOutput").ap()

    def dint(name, shape):
        return nc.dram_tensor(name, list(shape), BF16, kind="Internal").ap()

    i_wq = dint("i_wq", (D, H)); i_wk = dint("i_wk", (D, H)); i_wv = dint("i_wv", (D, H))
    i_wo = dint("i_wo", (H, D))
    i_wbase = dint("i_wbase", (D, D))
    i_wd1 = dint("i_wd1", (NE0, D, H0)); i_wd2 = dint("i_wd2", (NE0, H0, D))
    i_wad = dint("i_wad", (NE0, D, R0)); i_wau = dint("i_wau", (NE0, R0, D))
    i_winT = dint("i_winT", (D, 2 * DI))
    i_wxT = dint("i_wxT", (DI, 128))
    i_wdtT = dint("i_wdtT", (DTR, DI))
    i_woutT = dint("i_woutT", (DI, D))
    i_rwT = dint("i_rwT", (D, NE1))
    i_gwT = dint("i_gwT", (NE1, D, I1))
    i_uwT = dint("i_uwT", (NE1, D, I1))
    i_dwT = dint("i_dwT", (NE1, I1, D))
    i_sgwT = dint("i_sgwT", (D, SHI))
    i_suwT = dint("i_suwT", (D, SHI))
    i_sdwT = dint("i_sdwT", (SHI, D))

    ctx = Ctx(nc, es)
    ctx.debug_taps = tuple(debug_taps)

    taps = {}
    def mktap(name, shape, dt=F32):
        if name in debug_taps and name not in taps:
            taps[name] = nc.dram_tensor("tap_" + name, list(shape), dt,
                                        kind="ExternalOutput").ap()

    for b in range(BPC):
        mktap("xn0_b%d" % b, (128, NT * T), BF16)
        mktap("x_l0_b%d" % b, (128, NT * T))
        mktap("oall_b%d" % b, (H, T), BF16)
        mktap("xn1_b%d" % b, (128, NT * T), BF16)
        mktap("x_ssm_b%d" % b, (128, NT * T))

    def dump(name, ap_fn):
        t = taps.get(name)
        if t is None:
            return
        bar = ctx.barrier()
        ctx.sig("gpsimd", lambda g, t=t: g.dma_start(out=t, in_=ap_fn()), dma=True)
        ctx.barrier()
    S2_dump[0] = dump
    S2_mktap[0] = mktap

    def sbr(name, shape, dt):
        return es.enter_context(nc.sbuf_tensor(name, list(shape), dt))

    S = {}
    S["x"] = sbr("s_x", (128, NT * T), F32)
    S["xn"] = sbr("s_xn", (128, NT * T), BF16)
    S["ident"] = sbr("s_ident", (128, 128), BF16)
    S["triu"] = sbr("s_triu", (128, 128), BF16)
    S["onesb"] = sbr("s_onesb", (128, 128), BF16)
    S["onesf"] = sbr("s_onesf", (128, 128), F32)
    S["wq"] = sbr("s_wq", (128, NT * H), BF16)
    S["wk"] = sbr("s_wk", (128, NT * H), BF16)
    S["wv"] = sbr("s_wv", (128, NT * H), BF16)
    S["wo"] = sbr("s_wo", (H, D), BF16)
    S["rwT"] = sbr("s_rwT", (128, NT * NE1), BF16)
    S["bbase"] = sbr("s_bbase", (128, NT), F32)
    S["bd1"] = sbr("s_bd1", (128, NE0 * 2), F32)
    S["bd2"] = sbr("s_bd2", (128, NE0 * NT), F32)
    S["bad"] = sbr("s_bad", (R0, NE0), F32)
    S["bau"] = sbr("s_bau", (128, NE0 * NT), F32)
    S["binc"] = sbr("s_binc", (128, 32), F32)
    S["bxc"] = sbr("s_bxc", (128, 1), F32)
    S["bdtc"] = sbr("s_bdtc", (128, NDI), F32)
    S["A"] = sbr("s_A", (128, NDI * NS), F32)
    S["dc"] = sbr("s_dc", (128, NDI), F32)
    S["boutc"] = sbr("s_boutc", (128, NT), F32)
    S["rbb"] = sbr("s_rbb", (128, NE1), F32)
    S["kone"] = sbr("s_kone", (128, 2), F32)
    S["hstate"] = sbr("s_hstate", (128, NDI * NS), F32)

    pb = [es.enter_context(nc.psum_tensor("pb%d" % i, [128, 512], F32))
          for i in range(7)]
    pbh = es.enter_context(nc.psum_tensor("pbh", [128, 1024], BF16))
    S["pb"] = pb
    S["pbh"] = pbh

    cast_toks = []
    for (dst, src) in [(i_wq, d_wq), (i_wk, d_wk), (i_wv, d_wv), (i_wo, d_wo),
                       (i_wbase, d_wbase), (i_wd1, d_wd1), (i_wd2, d_wd2),
                       (i_wad, d_wad), (i_wau, d_wau)]:
        cast_toks.append(ctx.sig("gpsimd", lambda g, a=dst, b=src: g.dma_start(out=a, in_=b), dma=True))
    tok_cast_l0 = cast_toks[-1]
    for (dst, src) in [(i_winT, d_winT), (i_wxT, d_wxT), (i_wdtT, d_wdtT),
                       (i_woutT, d_woutT), (i_rwT, d_rwT),
                       (i_sgwT, d_sgwT), (i_suwT, d_suwT), (i_sdwT, d_sdwT)]:
        cast_toks.append(ctx.sig("gpsimd", lambda g, a=dst, b=src: g.dma_start(out=a, in_=b), dma=True))
    tok_cast_l1 = cast_toks[-1]
    for (dst, src) in [(i_gwT, d_gwT), (i_uwT, d_uwT), (i_dwT, d_dwT)]:
        cast_toks.append(ctx.sig("gpsimd", lambda g, a=dst, b=src: g.dma_start(out=a, in_=b), dma=True))
    tok_cast_moe = cast_toks[-1]

    for (dst, src) in [(S["ident"], d_ident), (S["triu"], d_triu), (S["onesb"], d_onesb),
                       (S["onesf"], d_onesf), (S["bbase"], d_bbase),
                       (S["bad"], d_bad), (S["binc"], d_binc), (S["bxc"], d_bxc),
                       (S["bdtc"], d_bdtc), (S["dc"], d_dc), (S["boutc"], d_boutc)]:
        ctx.sig("sync", lambda y, a=dst, b=src: y.dma_start(out=a[:, :], in_=b), dma=True)
    ctx.sig("sync", lambda y: y.dma_start(out=S["bd1"][:, :], in_=_r(d_bd1, "p e c -> p (e c)")), dma=True)
    ctx.sig("sync", lambda y: y.dma_start(out=S["bd2"][:, :], in_=_r(d_bd2, "p e c -> p (e c)")), dma=True)
    ctx.sig("sync", lambda y: y.dma_start(out=S["bau"][:, :], in_=_r(d_bau, "p e c -> p (e c)")), dma=True)
    ctx.sig("sync", lambda y: y.dma_start(out=S["A"][:, :], in_=d_alogc), dma=True)
    ctx.sig("sync", lambda y: y.dma_start(out=S["rbb"][0:1, 0:NE1], in_=d_rb), dma=True)
    ctx.wait("sync", tok_cast_l1)
    for (dst, src) in [(S["wq"], i_wq), (S["wk"], i_wk), (S["wv"], i_wv)]:
        ctx.sig("sync", lambda y, a=dst, b=src: y.dma_start(
            out=_r(a[:, :], "p (c m) -> p c m", m=H),
            in_=_r(b, "(c p) m -> p c m", p=128)), dma=True)
    ctx.sig("sync", lambda y: y.dma_start(out=S["wo"][:, :], in_=i_wo), dma=True)
    ctx.sig("sync", lambda y: y.dma_start(
        out=_r(S["rwT"][:, :], "p (c m) -> p c m", m=NE1),
        in_=_r(i_rwT, "(c p) m -> p c m", p=128)), dma=True)
    tok_params = ctx.now("sync")

    ctx.wait("tensor", tok_params)
    tok_rbmm = ctx.sig("tensor", lambda t: t.matmul(
        pb[0][:, 0:NE1], S["onesf"][0:1, :], S["rbb"][0:1, 0:NE1], start=True, stop=True))
    ctx.wait("vector", tok_rbmm)
    ctx.sig("vector", lambda v: v.tensor_copy(S["rbb"][:, :], pb[0][:, 0:NE1]))
    ctx.wait("scalar", tok_params)
    tok_Ae = ctx.sig("scalar", lambda s: s.activation(S["A"][:, :], S["A"][:, :], AF.Exp))
    ctx.wait("vector", tok_Ae)
    ctx.sig("vector", lambda v: v.tensor_scalar(S["A"][:, :], S["A"][:, :], -1.0, None, ALU.mult))
    def _kone(v):
        v.memset(S["kone"][:, 0:1], 1.0)
        v.memset(S["kone"][:, 1:2], 1e-5)
        return v.drain()
    ctx.op("vector", _kone)
    ctx.barrier()

    for b in range(BPC):
        ctx.cur_b = b
        tok_x = ctx.sig("sync", lambda y, b=b: y.dma_start(
            out=_r(S["x"][:, :], "p (c t) -> p c t", c=NT),
            in_=_r(d_x[b], "(c p) t -> p c t", p=128)), dma=True)
        emit_layernorm(ctx, nc, S, [tok_x])
        ctx.barrier()
        dump("xn0_b%d" % b, lambda: S["xn"][:, :])
        es_l0 = ExitStack()
        S["oall"] = es_l0.enter_context(
            nc.sbuf_tensor("s_oall_%d" % ctx.uid(), [H, T], BF16))
        if PHASES["l0a"]:
            emit_l0_attn(ctx, nc, S)
        dump("oall_b%d" % b, lambda: S["oall"][:, :])
        if PHASES["l0f"]:
            emit_l0_ffn(ctx, nc, S, i_wd1, i_wd2, i_wad, i_wau, i_wbase)
        es_l0.close()
        ctx.barrier()
        dump("x_l0_b%d" % b, lambda: S["x"][:, :])
        emit_layernorm(ctx, nc, S, [])
        ctx.barrier()
        dump("xn1_b%d" % b, lambda: S["xn"][:, :])
        if PHASES["ssm"]:
            emit_l1_ssm(ctx, nc, S, i_winT, i_wxT, i_wdtT, i_woutT)
        ctx.barrier()
        dump("x_ssm_b%d" % b, lambda: S["x"][:, :])
        if PHASES["moe"]:
            ctx.wait("sync", tok_cast_moe)
            emit_l1_moe(ctx, nc, S, i_gwT, i_uwT, i_dwT, i_sgwT, i_suwT, i_sdwT)
        ctx.barrier()
        tok_store = ctx.sig("gpsimd", lambda g, b=b: g.dma_start(
            out=_r(d_out[b], "(c p) t -> p c t", p=128),
            in_=_r(S["x"][:, :], "p (c t) -> p c t", c=NT)), dma=True)
        ctx.wait("sync", tok_store)
        ctx.wait("gpsimd", tok_store)

    ctx.barrier()

    blk = es.enter_context(nc.Block())

    @blk.tensor
    def _(e):
        for f in ctx.q["tensor"]:
            f(e)

    @blk.vector
    def _(e):
        for f in ctx.q["vector"]:
            f(e)

    @blk.scalar
    def _(e):
        for f in ctx.q["scalar"]:
            f(e)

    @blk.gpsimd
    def _(e):
        for f in ctx.q["gpsimd"]:
            f(e)

    @blk.sync
    def _(e):
        for f in ctx.q["sync"]:
            f(e)

    es.close()
    return nc


def host_inputs(inp, core):
    f = np.ascontiguousarray

    def t(a):
        return f(np.asarray(a).transpose())

    x = np.asarray(inp["L0_attn_x"])
    xb = x[core * BPC:(core + 1) * BPC]
    m = {
        "x_in": f(xb.transpose(0, 2, 1)),
        "w_q": f(inp["L0_attn_W_q"]), "w_k": f(inp["L0_attn_W_k"]),
        "w_v": f(inp["L0_attn_W_v"]), "w_o": f(inp["L0_attn_W_o"]),
        "w_base": f(inp["L0_ffn_W_base"]),
        "wd1": f(inp["L0_ffn_Wd1"]), "wd2": f(inp["L0_ffn_Wd2"]),
        "wa_down": f(inp["L0_ffn_Wa_down"]), "wa_up": f(inp["L0_ffn_Wa_up"]),
        "bb_base": f(np.asarray(inp["L0_ffn_b_base"]).reshape(NT, 128).T),
        "bd1_c": f(np.asarray(inp["L0_ffn_bd1"]).reshape(NE0, 2, 128).transpose(2, 0, 1)),
        "bd2_c": f(np.asarray(inp["L0_ffn_bd2"]).reshape(NE0, NT, 128).transpose(2, 0, 1)),
        "bad_c": f(np.asarray(inp["L0_ffn_ba_down"]).T),
        "bau_c": f(np.asarray(inp["L0_ffn_ba_up"]).reshape(NE0, NT, 128).transpose(2, 0, 1)),
        "w_inT": t(inp["L1_attn_W_in"]),
        "w_dtT": t(inp["L1_attn_W_dt"]),
        "w_outT": t(inp["L1_attn_W_out"]),
        "b_in_c": f(np.asarray(inp["L1_attn_b_in"]).reshape(32, 128).T),
        "b_dt_c": f(np.asarray(inp["L1_attn_b_dt"]).reshape(NDI, 128).T),
        "a_log_c": f(np.asarray(inp["L1_attn_A_log"]).reshape(NDI, 128, NS)
                     .transpose(1, 0, 2).reshape(128, NDI * NS)),
        "d_c": f(np.asarray(inp["L1_attn_D"]).reshape(NDI, 128).T),
        "b_out_c": f(np.asarray(inp["L1_attn_b_out"]).reshape(NT, 128).T),
        "rw_T": t(inp["L1_ffn_router_weight"]),
        "r_bias": f(np.asarray(inp["L1_ffn_router_bias"]).reshape(1, NE1)),
        "gw_T": f(np.asarray(inp["L1_ffn_gate_w"]).transpose(0, 2, 1)),
        "uw_T": f(np.asarray(inp["L1_ffn_up_w"]).transpose(0, 2, 1)),
        "dw_T": f(np.asarray(inp["L1_ffn_down_w"]).transpose(0, 2, 1)),
        "sgw_T": t(inp["L1_ffn_shared_gate_w"]),
        "suw_T": t(inp["L1_ffn_shared_up_w"]),
        "sdw_T": t(inp["L1_ffn_shared_down_w"]),
    }
    wx = np.asarray(inp["L1_attn_W_x"])
    wxT = np.zeros((DI, 128), dtype=np.float32)
    wxT[:, 0:64] = wx[0:64].T
    wxT[:, 64:80] = wx[64:80].T
    wxT[:, 96:112] = wx[80:96].T
    m["w_xT"] = f(wxT)
    bx = np.asarray(inp["L1_attn_b_x"])
    bxc = np.zeros((128, 1), dtype=np.float32)
    bxc[0:64, 0] = bx[0:64]
    bxc[64:80, 0] = bx[64:80]
    bxc[96:112, 0] = bx[80:96]
    m["b_x_c"] = bxc
    m["c_ident"] = np.eye(128, dtype=ml_dtypes.bfloat16)
    m["c_triu"] = np.triu(np.ones((128, 128), dtype=np.float32)).astype(ml_dtypes.bfloat16)
    m["c_ones_b"] = np.ones((128, 128), dtype=ml_dtypes.bfloat16)
    m["c_ones_f"] = np.ones((128, 128), dtype=np.float32)
    return m


def kernel(**inp):
    nc = build_program(DEBUG_TAPS)
    in_maps = [host_inputs(inp, core) for core in range(NCORES)]
    res = run_bass_kernel_spmd(nc, in_maps, core_ids=list(range(NCORES)))
    outs = []
    for core in range(NCORES):
        o = res.results[core]["x_out"]
        outs.append(np.ascontiguousarray(o.transpose(0, 2, 1)))
    return np.concatenate(outs, axis=0).astype(np.float32)


if __name__ == "__main__":
    print("building program...")
    import time
    t0 = time.time()
    nc = build_program()
    print("built in %.1fs" % (time.time() - t0))


# version 11
# speedup vs baseline: 1.1649x; 1.0050x over previous
import numpy as np
import ml_dtypes
from contextlib import ExitStack

import concourse.bass as bass
import concourse.mybir as mybir
from concourse.bass_utils import run_bass_kernel_spmd

F32 = mybir.dt.float32
BF16 = mybir.dt.bfloat16
AF = mybir.ActivationFunctionType
ALU = mybir.AluOpType
AX = mybir.AxisListType

NCORES = 8
BPC = 4
T = 2048
D = 1024
NT = 8
TC = 512
NTC = T // TC
H = 64
CH = 128
NCH = T // CH
DI = 2048
NDI = 16
DTR = 64
NS = 16
SEG = 256
NSEG = T // SEG
NE0 = 8
H0 = 256
R0 = 64
NE1 = 32
I1 = 512
SHI = 1024

ENGINES = ("tensor", "vector", "scalar", "gpsimd", "sync")
DEBUG_TAPS = ()
PHASES = {"l0a": True, "l0f": True, "ssm": True, "moe": True}
S2_dump = [None]
S2_mktap = [None]


class Ctx:
    def __init__(self, nc, es):
        self.nc = nc
        self.q = {e: [] for e in ENGINES}
        self.sem = {e: es.enter_context(nc.semaphore("sem_" + e)) for e in ENGINES}
        self.debug_taps = ()
        self.auto_drain = False
        self.cnt = {e: 0 for e in ENGINES}

    _uid = [0]

    def uid(self):
        Ctx._uid[0] += 1
        return Ctx._uid[0]

    def op(self, eng, fn):
        self.q[eng].append(fn)

    def sig(self, eng, fn, dma=False):
        inc = 16 if dma else 1
        self.cnt[eng] += inc
        assert self.cnt[eng] < 2 ** 31
        sem = self.sem[eng]

        def g(e, fn=fn, sem=sem, inc=inc):
            fn(e).then_inc(sem, inc)

        self.q[eng].append(g)
        if self.auto_drain and eng in ("vector", "scalar") and not dma:
            self.q[eng].append(lambda e: e.drain())
        return (sem, self.cnt[eng])

    def wait(self, eng, *toks):
        for tk in toks:
            if tk is None:
                continue
            sem, val = tk
            if val > 0:
                self.q[eng].append(lambda e, s=sem, v=val: e.wait_ge(s, v))

    def now(self, eng):
        return (self.sem[eng], self.cnt[eng])

    def barrier(self):
        toks = {e: self.now(e) for e in ENGINES}
        for e in ENGINES:
            self.wait(e, *[toks[o] for o in ENGINES if o != e])
        return toks


def _r(ap, spec, **kw):
    return ap.rearrange(spec, **kw)


def emit_layernorm(ctx, nc, S, pre_toks):
    s_x, s_xn, s_onesf, s_kone, pb = S["x"], S["xn"], S["onesf"], S["kone"], S["pb"]
    with nc.sbuf_tensor("ln_xsq_%d" % ctx.uid(), [128, TC], F32) as s_xsq, \
         nc.sbuf_tensor("ln_st_%d" % ctx.uid(), [1, 4 * TC], F32) as s_st:
        ctx.wait("scalar", *pre_toks)
        ctx.wait("tensor", *pre_toks)
        for c in range(NTC):
            for i in range(NT):
                sl = (i * T + c * TC, i * T + (c + 1) * TC)
                tok_sq = ctx.sig("scalar", lambda s, sl=sl: s.activation(
                    s_xsq[:, :], s_x[:, sl[0]:sl[1]], AF.Square))
                ctx.sig("tensor", lambda t, sl=sl, i=i: t.matmul(
                    pb[0][0:1, 0:TC], s_onesf[:, 0:1], s_x[:, sl[0]:sl[1]],
                    start=(i == 0), stop=(i == NT - 1)))
                ctx.wait("tensor", tok_sq)
                tok_mm2 = ctx.sig("tensor", lambda t, i=i: t.matmul(
                    pb[1][0:1, 0:TC], s_onesf[:, 0:1], s_xsq[:, :],
                    start=(i == 0), stop=(i == NT - 1)))
                ctx.wait("scalar", tok_mm2)
            tok_sums = ctx.now("tensor")
            ctx.wait("vector", tok_sums)
            ctx.sig("vector", lambda v: v.tensor_scalar(
                s_st[0:1, 0:TC], pb[0][0:1, 0:TC], 1.0 / D, None, ALU.mult))
            ctx.sig("vector", lambda v: v.tensor_tensor(
                s_st[0:1, 2 * TC:3 * TC], s_st[0:1, 0:TC], s_st[0:1, 0:TC], ALU.mult))
            tok_m2 = ctx.sig("vector", lambda v: v.scalar_tensor_tensor(
                s_st[0:1, 3 * TC:4 * TC], pb[1][0:1, 0:TC], 1.0 / D,
                s_st[0:1, 2 * TC:3 * TC], ALU.mult, ALU.subtract))
            ctx.wait("scalar", tok_m2)
            ctx.sig("scalar", lambda s: s.activation(
                s_st[0:1, 3 * TC:4 * TC], s_st[0:1, 3 * TC:4 * TC],
                AF.Ln, bias=s_kone[0:1, 1:2]))
            tok_rstd = ctx.sig("scalar", lambda s: s.activation(
                s_st[0:1, TC:2 * TC], s_st[0:1, 3 * TC:4 * TC], AF.Exp, scale=-0.5))
            ctx.wait("tensor", tok_rstd)
            tok_bm = ctx.sig("tensor", lambda t: t.matmul(
                pb[2][:, 0:TC], s_onesf[0:1, :], s_st[0:1, 0:TC], start=True, stop=True))
            tok_br = ctx.sig("tensor", lambda t: t.matmul(
                pb[3][:, 0:TC], s_onesf[0:1, :], s_st[0:1, TC:2 * TC], start=True, stop=True))
            ctx.wait("vector", tok_bm, tok_br)
            for i in range(NT):
                sl = (i * T + c * TC, i * T + (c + 1) * TC)
                ctx.sig("vector", lambda v, sl=sl: v.tensor_tensor(
                    s_xsq[:, :], s_x[:, sl[0]:sl[1]], pb[2][:, 0:TC], ALU.subtract))
                ctx.sig("vector", lambda v, sl=sl: v.tensor_tensor(
                    s_xn[:, sl[0]:sl[1]], s_xsq[:, :], pb[3][:, 0:TC], ALU.mult))
            tok_done = ctx.now("vector")
            ctx.wait("tensor", tok_done)
            ctx.wait("scalar", tok_done)


def emit_l0_attn(ctx, nc, S):
    s_xn, s_oall, s_triu, s_onesb, pb = S["xn"], S["oall"], S["triu"], S["onesb"], S["pb"]
    s_wq, s_wk, s_wv = S["wq"], S["wk"], S["wv"]
    with nc.sbuf_tensor("at_q_%d" % ctx.uid(), [H, CH], BF16) as s_q, \
         nc.sbuf_tensor("at_kc_%d" % ctx.uid(), [CH, H], BF16) as s_kc, \
         nc.sbuf_tensor("at_vca_%d" % ctx.uid(), [CH, H + 1], BF16) as s_vca, \
         nc.sbuf_tensor("at_vt_%d" % ctx.uid(), [H, CH], BF16) as s_vt, \
         nc.sbuf_tensor("at_am_%d" % ctx.uid(), [CH, CH], BF16) as s_am, \
         nc.sbuf_tensor("at_st_%d" % ctx.uid(), [H + 1, H], BF16) as s_stz, \
         nc.sbuf_tensor("at_qz_%d" % ctx.uid(), [H, CH], BF16) as s_qz, \
         nc.sbuf_tensor("at_sc_%d" % ctx.uid(), [128, CH], F32) as s_sc, \
         nc.sbuf_tensor("at_dn_%d" % ctx.uid(), [1, 2 * CH], F32) as s_dn, \
         nc.sbuf_tensor("at_dnb_%d" % ctx.uid(), [1, CH], BF16) as s_dnb, \
         nc.sbuf_tensor("at_of_%d" % ctx.uid(), [H, CH], F32) as s_of:

        def _init(v):
            v.memset(s_stz[:, :], 0.0)
            v.memset(s_vca[:, H:H + 1], 1.0)
            return v.drain()
        ctx.op("vector", _init)

        for c in range(NCH):
            c0 = c * CH

            def phi_emit(src_pb, dst, n):
                tk1 = ctx.sig("vector", lambda v, s=src_pb, n=n: v.tensor_scalar(
                    s_sc[0:n[0], 0:n[1]], s, 0.0, None, ALU.min))
                ctx.wait("scalar", tk1)
                tk2 = ctx.sig("scalar", lambda s, n=n: s.activation(
                    s_sc[0:n[0], 0:n[1]], s_sc[0:n[0], 0:n[1]], AF.Exp))
                ctx.wait("vector", tk2)
                return ctx.sig("vector", lambda v, s=src_pb, d=dst, n=n: v.scalar_tensor_tensor(
                    d, s, 0.0, s_sc[0:n[0], 0:n[1]], ALU.max, ALU.add))

            for i in range(NT):
                xsl = s_xn[:, i * T + c0:i * T + c0 + CH]
                ctx.sig("tensor", lambda t, i=i, xs=xsl: t.matmul(
                    pb[0][0:H, 0:CH], s_wq[:, i * H:(i + 1) * H], xs,
                    start=(i == 0), stop=(i == NT - 1)))
                ctx.sig("tensor", lambda t, i=i, xs=xsl: t.matmul(
                    pb[1][0:H, 0:CH], s_wv[:, i * H:(i + 1) * H], xs,
                    start=(i == 0), stop=(i == NT - 1)))
                ctx.sig("tensor", lambda t, i=i, xs=xsl: t.matmul(
                    pb[2][0:CH, 0:H], xs, s_wk[:, i * H:(i + 1) * H],
                    start=(i == 0), stop=(i == NT - 1)))
                ctx.sig("tensor", lambda t, i=i, xs=xsl: t.matmul(
                    pb[3][0:CH, 0:H], xs, s_wv[:, i * H:(i + 1) * H],
                    start=(i == 0), stop=(i == NT - 1)))
            ctx.wait("vector", ctx.now("tensor"))
            tok_q = phi_emit(pb[0][0:H, 0:CH], s_q[:, :], (H, CH))
            tok_k = phi_emit(pb[2][0:CH, 0:H], s_kc[:, :], (CH, H))
            tok_vt = ctx.sig("vector", lambda v: v.tensor_copy(s_vt[:, :], pb[1][0:H, 0:CH]))
            tok_vc = ctx.sig("vector", lambda v: v.tensor_copy(
                s_vca[:, 0:H], pb[3][0:CH, 0:H]))

            ctx.wait("tensor", tok_q, tok_vt)
            tok_A = ctx.sig("tensor", lambda t: t.matmul(
                pb[4][0:CH, 0:CH], s_vt[:, :], s_q[:, :], start=True, stop=True))
            ctx.wait("vector", tok_A)
            tok_am = ctx.sig("vector", lambda v: v.tensor_tensor(
                s_am[:, :], pb[4][0:CH, 0:CH], s_triu[:, :], ALU.mult))

            ctx.wait("tensor", tok_am, tok_k)
            ctx.sig("tensor", lambda t: t.matmul(
                pb[4][0:H, CH:2 * CH], s_kc[:, :], s_am[:, :], start=True, stop=False))
            tok_O = ctx.sig("tensor", lambda t: t.matmul(
                pb[4][0:H, CH:2 * CH], s_stz[0:H, 0:H], s_q[:, :], start=False, stop=True))

            ctx.sig("tensor", lambda t: t.matmul(
                pb[5][0:H, 0:CH], s_kc[:, :], s_triu[:, :], start=True, stop=False))
            tok_Z = ctx.sig("tensor", lambda t: t.matmul(
                pb[5][0:H, 0:CH], s_stz[H:H + 1, 0:H], s_onesb[H:H + 1, 0:CH],
                start=False, stop=True))

            ctx.wait("tensor", tok_vc)
            tok_su = ctx.sig("tensor", lambda t, c=c: t.matmul(
                pb[6][0:H + 1, 0:H], s_vca[:, :], s_kc[:, :],
                start=(c == 0), stop=(c == NCH - 1)))

            ctx.wait("vector", tok_Z)
            tok_qz = ctx.sig("vector", lambda v: v.tensor_tensor(
                s_qz[:, :], s_q[:, :], pb[5][0:H, 0:CH], ALU.mult))
            ctx.wait("tensor", tok_qz)
            tok_dmm = ctx.sig("tensor", lambda t: t.matmul(
                pb[5][0:1, CH:2 * CH], s_onesb[0:H, 0:1], s_qz[:, :], start=True, stop=True))
            ctx.wait("vector", tok_dmm)
            tok_dn = ctx.sig("vector", lambda v: v.tensor_scalar(
                s_dn[0:1, 0:CH], pb[5][0:1, CH:2 * CH], 1e-6, None, ALU.max))
            ctx.wait("scalar", tok_dn)
            ctx.sig("scalar", lambda s: s.activation(
                s_dn[0:1, CH:2 * CH], s_dn[0:1, 0:CH], AF.Ln))
            tok_rcp = ctx.sig("scalar", lambda s: s.activation(
                s_dn[0:1, 0:CH], s_dn[0:1, CH:2 * CH], AF.Exp, scale=-1.0))
            ctx.wait("vector", tok_rcp)
            tok_dnb = ctx.sig("vector", lambda v: v.tensor_copy(s_dnb[0:1, :], s_dn[0:1, 0:CH]))
            ctx.wait("tensor", tok_dnb)
            tok_rb = ctx.sig("tensor", lambda t: t.matmul(
                pb[5][0:H, CH:2 * CH], s_onesb[0:1, 0:H], s_dnb[0:1, :], start=True, stop=True))
            ctx.wait("vector", tok_O, tok_rb)
            ctx.sig("vector", lambda v: v.tensor_copy(s_of[:, :], pb[4][0:H, CH:2 * CH]))
            ctx.op("vector", lambda v: v.drain())
            tok_of = ctx.sig("vector", lambda v, c0=c0: v.tensor_tensor(
                s_oall[:, c0:c0 + CH], s_of[:, :], pb[5][0:H, CH:2 * CH], ALU.mult))

            ctx.wait("vector", tok_su)
            tok_snap = ctx.sig("vector", lambda v: v.tensor_copy(
                s_stz[:, :], pb[6][0:H + 1, 0:H]))
            ctx.wait("tensor", tok_of, tok_snap, tok_vc, tok_k)


def emit_l0_ffn(ctx, nc, S, i_wd1, i_wd2, i_wad, i_wau, i_wbase):
    s_xn, s_x, s_oall, s_wo, pb = S["xn"], S["x"], S["oall"], S["wo"], S["pb"]
    s_onesb, s_onesf = S["onesb"], S["onesf"]
    s_bd1, s_bd2, s_bad, s_bau, s_bbase = S["bd1"], S["bd2"], S["bad"], S["bau"], S["bbase"]
    with nc.sbuf_tensor("f0_wd1_%d" % ctx.uid(), [128, 2 * NT * H0], BF16) as s_wd1, \
         nc.sbuf_tensor("f0_wd2_%d" % ctx.uid(), [128, 2 * 2 * D], BF16) as s_wd2, \
         nc.sbuf_tensor("f0_h_%d" % ctx.uid(), [128, 2 * T], BF16) as s_h, \
         nc.sbuf_tensor("f0_diff_%d" % ctx.uid(), [128, 3 * TC], F32) as s_diff, \
         nc.sbuf_tensor("f0_loss_%d" % ctx.uid(), [128, NE0 * 33], F32) as s_loss, \
         nc.sbuf_tensor("f0_l8_%d" % ctx.uid(), [128, NE0], F32) as s_l8, \
         nc.sbuf_tensor("f0_lrow_%d" % ctx.uid(), [1, 3 * NE0], F32) as s_lrow, \
         nc.sbuf_tensor("f0_lrb_%d" % ctx.uid(), [1, NE0], BF16) as s_lrb, \
         nc.sbuf_tensor("f0_ind_%d" % ctx.uid(), [128, 2 * NE0 + 1], F32) as s_ind, \
         nc.sbuf_tensor("f0_wad_%d" % ctx.uid(), [128, NE0 * NT * R0], BF16) as s_wad, \
         nc.sbuf_tensor("f0_wau_%d" % ctx.uid(), [R0, NE0 * D], BF16) as s_wau, \
         nc.sbuf_tensor("f0_wadsel_%d" % ctx.uid(), [128, NT * R0], BF16) as s_wadsel, \
         nc.sbuf_tensor("f0_wausel_%d" % ctx.uid(), [R0, D], BF16) as s_wausel, \
         nc.sbuf_tensor("f0_badsel_%d" % ctx.uid(), [R0, 2], F32) as s_badsel, \
         nc.sbuf_tensor("f0_l0b_%d" % ctx.uid(), [128, NT + NE0], F32) as s_l0b, \
         nc.sbuf_tensor("f0_abf_%d" % ctx.uid(), [R0, T], BF16) as s_abf, \
         nc.sbuf_tensor("f0_wbase_%d" % ctx.uid(), [128, NT * D], BF16) as s_wbase:

        ctx.wait("sync", ctx.now("vector"), ctx.now("tensor"), ctx.now("scalar"))
        tok_wd = []
        for e in range(2):
            tok_wd.append(ctx.sig("sync", lambda y, e=e: y.dma_start(
                out=_r(s_wd1[:, e * NT * H0:(e + 1) * NT * H0], "p (c m) -> p c m", c=NT),
                in_=_r(i_wd1[e], "(c p) m -> p c m", p=128)), dma=True))
            tok_wd.append(ctx.sig("sync", lambda y, e=e: y.dma_start(
                out=_r(s_wd2[:, e * 2 * D:(e + 1) * 2 * D], "p (c m) -> p c m", c=2),
                in_=_r(i_wd2[e], "(c p) m -> p c m", p=128)), dma=True))
        tok_wad = ctx.sig("sync", lambda y: y.dma_start(
            out=_r(s_wad[:, :], "p (e c m) -> p e c m", e=NE0, c=NT),
            in_=_r(i_wad, "e (c p) m -> p e c m", p=128)), dma=True)
        tok_wau = ctx.sig("sync", lambda y: y.dma_start(
            out=_r(s_wau[:, :], "p (e m) -> p e m", e=NE0),
            in_=_r(i_wau, "e p m -> p e m")), dma=True)
        tok_wbase = ctx.sig("sync", lambda y: y.dma_start(
            out=_r(s_wbase[:, :], "p (c m) -> p c m", c=NT),
            in_=_r(i_wbase, "(c p) m -> p c m", p=128)), dma=True)

        diff_par = [None, None]
        for e in range(NE0):
            sl = e % 2
            base1 = sl * NT * H0
            base2 = sl * 2 * D
            ctx.wait("tensor", tok_wd[2 * e], tok_wd[2 * e + 1])
            hr_tok = [None, None]
            for it in range(2):
                for c in range(NTC):
                    hpar = (it * NTC + c) % 2
                    hp = pb[0] if hpar == 0 else pb[6]
                    ctx.wait("tensor", hr_tok[hpar])
                    for k in range(NT):
                        ctx.sig("tensor", lambda t, k=k, it=it, c=c, b1=base1, hp=hp: t.matmul(
                            hp[:, 0:TC],
                            s_wd1[:, b1 + k * H0 + it * 128: b1 + k * H0 + (it + 1) * 128],
                            s_xn[:, k * T + c * TC:k * T + (c + 1) * TC],
                            start=(k == 0), stop=(k == NT - 1)))
                    tok_h = ctx.now("tensor")
                    ctx.wait("scalar", tok_h)
                    hr_tok[hpar] = ctx.sig("scalar", lambda s, it=it, c=c, e=e, hp=hp: s.activation(
                        s_h[:, it * T + c * TC:it * T + (c + 1) * TC], hp[:, 0:TC],
                        AF.Relu, bias=s_bd1[:, e * 2 + it:e * 2 + it + 1]))
            tok_hdone = ctx.now("scalar")
            df_tok = [None, None]
            for dt in range(NT):
                for c in range(NTC):
                    slot = (dt * NTC + c) % 2
                    rp = pb[1] if slot == 0 else pb[2]
                    ctx.wait("tensor", tok_hdone, df_tok[slot])
                    for k2 in range(2):
                        ctx.sig("tensor", lambda t, k2=k2, dt=dt, c=c, b2=base2, rp=rp: t.matmul(
                            rp[:, 0:TC],
                            s_wd2[:, b2 + k2 * D + dt * 128: b2 + k2 * D + (dt + 1) * 128],
                            s_h[:, k2 * T + c * TC:k2 * T + (c + 1) * TC],
                            start=(k2 == 0), stop=(k2 == 1)))
                    tok_rc = ctx.now("tensor")
                    ctx.wait("vector", tok_rc, diff_par[slot])
                    tok_df = ctx.sig("vector", lambda v, dt=dt, c=c, e=e, slot=slot, rp=rp:
                                     v.scalar_tensor_tensor(
                        s_diff[:, slot * TC:(slot + 1) * TC], rp[:, 0:TC],
                        s_bd2[:, e * NT + dt:e * NT + dt + 1],
                        s_xn[:, dt * T + c * TC:dt * T + (c + 1) * TC],
                        ALU.add, ALU.subtract))
                    df_tok[slot] = tok_df
                    ctx.wait("scalar", tok_df)
                    diff_par[slot] = ctx.sig("scalar", lambda s, e=e, dt=dt, c=c, slot=slot:
                                             s.activation(
                        s_diff[:, 2 * TC:3 * TC], s_diff[:, slot * TC:(slot + 1) * TC],
                        AF.Square, accum_out=s_loss[:, e * 33 + dt * NTC + c:
                                                    e * 33 + dt * NTC + c + 1]))
            if e + 2 < NE0:
                ctx.wait("sync", ctx.now("tensor"))
                tok_wd.append(ctx.sig("sync", lambda y, e=e, sl=sl: y.dma_start(
                    out=_r(s_wd1[:, sl * NT * H0:(sl + 1) * NT * H0], "p (c m) -> p c m", c=NT),
                    in_=_r(i_wd1[e + 2], "(c p) m -> p c m", p=128)), dma=True))
                tok_wd.append(ctx.sig("sync", lambda y, e=e, sl=sl: y.dma_start(
                    out=_r(s_wd2[:, sl * 2 * D:(sl + 1) * 2 * D], "p (c m) -> p c m", c=2),
                    in_=_r(i_wd2[e + 2], "(c p) m -> p c m", p=128)), dma=True))
            ctx.wait("vector", diff_par[0], diff_par[1])
            ctx.op("vector", lambda v: v.drain())
            ctx.sig("vector", lambda v, e=e: v.tensor_reduce(
                s_l8[:, e:e + 1], s_loss[:, e * 33:e * 33 + 32], AX.X, ALU.add))
        ctx.auto_drain = True
        tok_l8 = ctx.now("vector")
        ctx.wait("tensor", tok_l8)
        tok_lmm = ctx.sig("tensor", lambda t: t.matmul(
            pb[2][0:1, 0:NE0], s_onesf[:, 0:1], s_l8[:, :], start=True, stop=True))
        ctx.wait("vector", tok_lmm)
        tok_lrow = ctx.sig("vector", lambda v: v.tensor_copy(
            s_lrow[0:1, 0:NE0], pb[2][0:1, 0:NE0]))
        ctx.wait("tensor", tok_lrow)
        tok_lbc = ctx.sig("tensor", lambda t: t.matmul(
            pb[3][:, 0:NE0], s_onesf[0:1, :], s_lrow[0:1, 0:NE0], start=True, stop=True))
        ctx.wait("vector", tok_lbc, tok_wad, tok_wau)
        ctx.sig("vector", lambda v: v.tensor_copy(
            s_ind[:, NE0:2 * NE0], pb[3][:, 0:NE0]))
        ctx.op("vector", lambda v: v.drain())
        ctx.sig("vector", lambda v: v.tensor_reduce(
            s_ind[:, 2 * NE0:2 * NE0 + 1], s_ind[:, NE0:2 * NE0], AX.X, ALU.min))
        ctx.op("vector", lambda v: v.drain())
        tok_indbc = ctx.sig("vector", lambda v: v.tensor_scalar(
            s_ind[:, 0:NE0], s_ind[:, NE0:2 * NE0],
            s_ind[:, 2 * NE0:2 * NE0 + 1], None, ALU.is_equal))

        ctx.op("vector", lambda v: v.drain())
        for kc in range(NT):
            ctx.sig("vector", lambda v, kc=kc: v.tensor_scalar(
                s_wadsel[:, kc * R0:(kc + 1) * R0],
                s_wad[:, (0 * NT + kc) * R0:(0 * NT + kc + 1) * R0],
                s_ind[:, 0:1], None, ALU.mult))
            for e in range(1, NE0):
                ctx.op("vector", lambda v: v.drain())
                ctx.sig("vector", lambda v, kc=kc, e=e: v.scalar_tensor_tensor(
                    s_wadsel[:, kc * R0:(kc + 1) * R0],
                    s_wad[:, (e * NT + kc) * R0:(e * NT + kc + 1) * R0],
                    s_ind[:, e:e + 1], s_wadsel[:, kc * R0:(kc + 1) * R0],
                    ALU.mult, ALU.add))
        ctx.sig("vector", lambda v: v.tensor_scalar(
            s_wausel[:, :], s_wau[:, 0:D], s_ind[0:R0, 0:1], None, ALU.mult))
        for e in range(1, NE0):
            ctx.op("vector", lambda v: v.drain())
            ctx.sig("vector", lambda v, e=e: v.scalar_tensor_tensor(
                s_wausel[:, :], s_wau[:, e * D:(e + 1) * D],
                s_ind[0:R0, e:e + 1], s_wausel[:, :], ALU.mult, ALU.add))
        ctx.sig("vector", lambda v: v.tensor_scalar(
            s_badsel[:, 0:1], s_bad[:, 0:1], s_ind[0:R0, 0:1], None, ALU.mult))
        for e in range(1, NE0):
            ctx.op("vector", lambda v: v.drain())
            ctx.sig("vector", lambda v, e=e: v.scalar_tensor_tensor(
                s_badsel[:, 0:1], s_bad[:, e:e + 1], s_ind[0:R0, e:e + 1],
                s_badsel[:, 0:1], ALU.mult, ALU.add))
        for dt in range(NT):
            ctx.sig("vector", lambda v, dt=dt: v.tensor_tensor(
                s_l0b[:, NT:NT + NE0],
                _r(s_bau[:, :], "p (e c) -> p e c", e=NE0)[:, :, dt],
                s_ind[:, 0:NE0], ALU.mult))
            ctx.op("vector", lambda v: v.drain())
            ctx.sig("vector", lambda v, dt=dt: v.tensor_reduce(
                s_l0b[:, dt:dt + 1], s_l0b[:, NT:NT + NE0], AX.X, ALU.add))
            ctx.op("vector", lambda v: v.drain())
            ctx.sig("vector", lambda v, dt=dt: v.tensor_tensor(
                s_l0b[:, dt:dt + 1], s_l0b[:, dt:dt + 1],
                s_bbase[:, dt:dt + 1], ALU.add))
        tok_sel = ctx.now("vector")

        S2_mktap[0]("ffn_dbg", (128, 96))
        S2_mktap[0]("ffn_h7", (128, 2 * T), BF16)
        if "ffn_dbg" in ctx.debug_taps and getattr(ctx, "cur_b", 0) == 0:
            with nc.sbuf_tensor("f0_dbg_%d" % ctx.uid(), [128, 96], F32) as s_dbg_:
                ctx.sig("vector", lambda v: v.tensor_copy(s_dbg_[:, 0:33], s_loss[:, 0 * 33:1 * 33]))
                ctx.sig("vector", lambda v: v.tensor_copy(s_dbg_[:, 33:41], s_l8[:, :]))
                ctx.sig("vector", lambda v: v.tensor_copy(s_dbg_[:, 41:49], s_ind[:, 0:NE0]))
                ctx.sig("vector", lambda v: v.tensor_copy(s_dbg_[:, 49:57], s_l0b[:, 0:8]))
                ctx.sig("vector", lambda v: v.tensor_copy(s_dbg_[0:1, 57:57 + 24], s_lrow[0:1, :]))
                ctx.sig("vector", lambda v: v.tensor_copy(s_dbg_[:, 81:90], s_ind[:, NE0:2 * NE0 + 1]))
                S2_dump[0]("ffn_dbg", lambda: s_dbg_[:, :])
                S2_dump[0]("ffn_h7", lambda: s_h[:, :])
        ctx.wait("tensor", tok_sel)
        for c in range(NTC):
            for k in range(NT):
                ctx.sig("tensor", lambda t, k=k, c=c: t.matmul(
                    pb[3][0:R0, 0:TC], s_wadsel[:, k * R0:(k + 1) * R0],
                    s_xn[:, k * T + c * TC:k * T + (c + 1) * TC],
                    start=(k == 0), stop=(k == NT - 1)))
            tok_a = ctx.now("tensor")
            ctx.wait("scalar", tok_a)
            tok_ar = ctx.sig("scalar", lambda s, c=c: s.activation(
                s_abf[:, c * TC:(c + 1) * TC], pb[3][0:R0, 0:TC],
                AF.Relu, bias=s_badsel[:, 0:1]))
            ctx.wait("tensor", tok_ar)

        ctx.wait("tensor", tok_wbase)
        evac = [None, None]
        for dt in range(NT):
            for c in range(NTC):
                bnk = (dt * NTC + c) % 2
                p = pb[4 + bnk]
                ctx.wait("tensor", evac[bnk])
                ctx.sig("tensor", lambda t, dt=dt, c=c, p=p: t.matmul(
                    p[:, 0:TC], s_wo[:, dt * 128:(dt + 1) * 128],
                    s_oall[:, c * TC:(c + 1) * TC], start=True, stop=False))
                for k in range(NT):
                    ctx.sig("tensor", lambda t, k=k, dt=dt, c=c, p=p: t.matmul(
                        p[:, 0:TC], s_wbase[:, k * D + dt * 128:k * D + (dt + 1) * 128],
                        s_xn[:, k * T + c * TC:k * T + (c + 1) * TC],
                        start=False, stop=False))
                tok_mm = ctx.sig("tensor", lambda t, dt=dt, c=c, p=p: t.matmul(
                    p[:, 0:TC], s_wausel[:, dt * 128:(dt + 1) * 128],
                    s_abf[:, c * TC:(c + 1) * TC], start=False, stop=True))
                ctx.wait("vector", tok_mm)
                evac[bnk] = ctx.sig("vector", lambda v, dt=dt, c=c, p=p:
                                    v.scalar_tensor_tensor(
                    s_x[:, dt * T + c * TC:dt * T + (c + 1) * TC], p[:, 0:TC],
                    s_l0b[:, dt:dt + 1],
                    s_x[:, dt * T + c * TC:dt * T + (c + 1) * TC], ALU.add, ALU.add))
        ctx.wait("tensor", evac[0], evac[1])


def emit_l1_ssm(ctx, nc, S, i_winT, i_wxT, i_wdtT, i_woutT):
    s_xn, s_x, pb = S["xn"], S["x"], S["pb"]
    s_binc, s_bxc, s_bdtc, s_A, s_dc, s_boutc = (S["binc"], S["bxc"], S["bdtc"],
                                                 S["A"], S["dc"], S["boutc"])
    s_hstate, s_kone, s_onesb = S["hstate"], S["kone"], S["onesb"]
    es2 = ExitStack()
    s_win = es2.enter_context(nc.sbuf_tensor("sm_win_%d" % ctx.uid(), [128, 2 * NT * 128], BF16))
    s_wxT = es2.enter_context(nc.sbuf_tensor("sm_wxT_%d" % ctx.uid(), [128, NDI * 128], BF16))
    s_wdtT = es2.enter_context(nc.sbuf_tensor("sm_wdtT_%d" % ctx.uid(), [DTR, DI], BF16))
    s_wout = es2.enter_context(nc.sbuf_tensor("sm_wout_%d" % ctx.uid(), [128, NDI * 128], BF16))
    s_xs = es2.enter_context(nc.sbuf_tensor("sm_xs_%d" % ctx.uid(), [128, NDI * SEG], BF16))
    s_z = es2.enter_context(nc.sbuf_tensor("sm_z_%d" % ctx.uid(), [128, NDI * SEG], BF16))
    s_dl = es2.enter_context(nc.sbuf_tensor("sm_dl_%d" % ctx.uid(), [128, NDI * SEG], BF16))
    s_dx = es2.enter_context(nc.sbuf_tensor("sm_dx_%d" % ctx.uid(), [128, NDI * SEG], BF16))
    s_y = es2.enter_context(nc.sbuf_tensor("sm_y_%d" % ctx.uid(), [128, NDI * SEG], F32))
    s_draw = es2.enter_context(nc.sbuf_tensor("sm_draw_%d" % ctx.uid(), [DTR, SEG], BF16))
    s_bc = es2.enter_context(nc.sbuf_tensor("sm_bc_%d" % ctx.uid(), [R0, SEG], F32))
    s_bcb = es2.enter_context(nc.sbuf_tensor("sm_bcb_%d" % ctx.uid(), [R0, SEG], BF16))
    s_flat = es2.enter_context(nc.sbuf_tensor("sm_flat_%d" % ctx.uid(), [65, NS * SEG], BF16))
    s_ab = es2.enter_context(nc.sbuf_tensor("sm_ab_%d" % ctx.uid(), [128, 4 * SEG], F32))
    s_abig = es2.enter_context(nc.sbuf_tensor("sm_abig_%d" % ctx.uid(), [128, 8 * SEG], F32))
    s_bbig = es2.enter_context(nc.sbuf_tensor("sm_bbig_%d" % ctx.uid(), [128, NDI * SEG], BF16))
    s_hbig = es2.enter_context(nc.sbuf_tensor("sm_hbig_%d" % ctx.uid(), [128, 8 * SEG], BF16))
    s_g1 = es2.enter_context(nc.sbuf_tensor("sm_g1_%d" % ctx.uid(), [128, SEG], F32))
    s_Bs = es2.enter_context(nc.sbuf_tensor("sm_Bs_%d" % ctx.uid(), [128, SEG], BF16))
    s_Cs = es2.enter_context(nc.sbuf_tensor("sm_Cs_%d" % ctx.uid(), [128, SEG], BF16))
    if True:
        s_yg = s_abig[:, :].bitcast(BF16)[:, 0:NDI * SEG]
        s_fix = s_g1

        ctx.wait("sync", ctx.now("vector"), ctx.now("tensor"), ctx.now("scalar"))
        tok_wx = ctx.sig("sync", lambda y: y.dma_start(
            out=_r(s_wxT[:, :], "p (c m) -> p c m", c=NDI),
            in_=_r(i_wxT, "(c p) m -> p c m", p=128)), dma=True)
        tok_wdt = ctx.sig("sync", lambda y: y.dma_start(
            out=s_wdtT[:, :], in_=i_wdtT), dma=True)
        def _h0(v):
            v.memset(s_hstate[:, :], 0.0)
            return v.drain()
        ctx.op("vector", _h0)

        win_tok = [None, None]
        win_used = [None, None]
        wout_tok = [None, None]
        wout_used = [None, None]
        a_toks = [None, None]
        a_free = [None, None]
        gp_b = [None, None]
        gp_y = [None, None]
        scan_done = [None, None]
        gp_st = None
        bc_free = None
        uev = [None, None]
        xev = [None, None]

        for sg in range(NSEG):
            t0 = sg * SEG
            for m in range(32):
                sl = m % 2
                w0 = sl * NT * 128
                qe = "sync" if sl == 0 else "gpsimd"
                ctx.wait(qe, win_used[sl])
                win_tok[sl] = ctx.sig(qe, lambda y, m=m, w0=w0: y.dma_start(
                    out=_r(s_win[:, w0:w0 + NT * 128], "p (c m) -> p c m", c=NT),
                    in_=_r(i_winT[:, m * 128:(m + 1) * 128], "(c p) m -> p c m", p=128)),
                    dma=True)
                up = pb[0] if sl == 0 else pb[6]
                ctx.wait("tensor", win_tok[sl], uev[sl])
                for k in range(NT):
                    ctx.sig("tensor", lambda t, k=k, t0=t0, w0=w0, up=up: t.matmul(
                        up[:, 0:SEG],
                        s_win[:, w0 + k * 128:w0 + (k + 1) * 128],
                        s_xn[:, k * T + t0:k * T + t0 + SEG],
                        start=(k == 0), stop=(k == NT - 1)))
                win_used[sl] = ctx.now("tensor")
                ctx.wait("vector", win_used[sl])
                if m < NDI:
                    ctx.sig("vector", lambda v, m=m, up=up: v.tensor_scalar(
                        s_xs[:, m * SEG:(m + 1) * SEG], up[:, 0:SEG],
                        s_binc[:, m:m + 1], None, ALU.add))
                    tok_ev = ctx.sig("vector", lambda v, m=m, up=up: v.tensor_scalar(
                        s_y[:, m * SEG:(m + 1) * SEG], up[:, 0:SEG],
                        s_binc[:, m:m + 1], s_dc[:, m:m + 1], ALU.add, ALU.mult))
                else:
                    tok_ev = ctx.sig("vector", lambda v, m=m, up=up: v.tensor_scalar(
                        s_z[:, (m - NDI) * SEG:(m - NDI + 1) * SEG], up[:, 0:SEG],
                        s_binc[:, m:m + 1], None, ALU.add))
                uev[sl] = tok_ev
            ctx.wait("tensor", tok_wx, uev[0], uev[1])
            for k in range(NDI):
                ctx.sig("tensor", lambda t, k=k: t.matmul(
                    pb[1][:, 0:SEG], s_wxT[:, k * 128:(k + 1) * 128],
                    s_xs[:, k * SEG:(k + 1) * SEG], start=(k == 0), stop=(k == NDI - 1)))
            tok_dbc = ctx.now("tensor")
            ctx.wait("vector", tok_dbc)
            ctx.sig("vector", lambda v: v.tensor_scalar(
                s_draw[:, :], pb[1][0:DTR, 0:SEG], s_bxc[0:DTR, 0:1], None, ALU.add))
            ctx.sig("vector", lambda v: v.tensor_scalar(
                s_bc[:, :], pb[1][DTR:128, 0:SEG], s_bxc[DTR:128, 0:1], None, ALU.add))
            ctx.op("vector", lambda v: v.drain())
            tok_bc = ctx.sig("vector", lambda v: v.tensor_copy(s_bcb[:, :], s_bc[:, :]))
            ctx.wait("tensor", tok_bc)
            ctx.wait("sync", tok_bc)
            tok_fb = ctx.sig("sync", lambda y: y.dma_start(
                out=_r(s_flat[0:1, 0:NS * SEG], "a (n t) -> a n t", n=NS),
                in_=s_bcb[0:NS, :]), dma=True)
            tok_fc = ctx.sig("sync", lambda y: y.dma_start(
                out=_r(s_flat[64:65, 0:NS * SEG], "a (n t) -> a n t", n=NS),
                in_=s_bcb[32:32 + NS, :]), dma=True)
            ctx.wait("tensor", tok_wdt)
            dl_done = [None, None]
            ab_fr = [None, None]
            for dt in range(NDI):
                par = dt % 2
                p = pb[2] if par == 0 else pb[5]
                a0 = par * 2 * SEG
                ctx.wait("tensor", dl_done[par])
                tok_dm = ctx.sig("tensor", lambda t, dt=dt, p=p: t.matmul(
                    p[:, 0:SEG], s_wdtT[:, dt * 128:(dt + 1) * 128],
                    s_draw[:, :], start=True, stop=True))
                ctx.wait("scalar", tok_dm, ab_fr[par])
                ctx.sig("scalar", lambda s, dt=dt, p=p, a0=a0: s.activation(
                    s_ab[:, a0:a0 + SEG], p[:, 0:SEG], AF.Abs,
                    bias=s_bdtc[:, dt:dt + 1]))
                ctx.sig("scalar", lambda s, a0=a0: s.activation(
                    s_ab[:, a0:a0 + SEG], s_ab[:, a0:a0 + SEG], AF.Exp, scale=-1.0))
                ctx.sig("scalar", lambda s, a0=a0: s.activation(
                    s_ab[:, a0:a0 + SEG], s_ab[:, a0:a0 + SEG], AF.Ln, bias=s_kone[:, 0:1]))
                tok_sp = ctx.sig("scalar", lambda s, dt=dt, p=p, a0=a0: s.activation(
                    s_ab[:, a0 + SEG:a0 + 2 * SEG], p[:, 0:SEG], AF.Relu,
                    bias=s_bdtc[:, dt:dt + 1]))
                ctx.wait("vector", tok_sp)
                dl_done[par] = ctx.sig("vector", lambda v, dt=dt, a0=a0: v.tensor_tensor(
                    s_dl[:, dt * SEG:(dt + 1) * SEG], s_ab[:, a0:a0 + SEG],
                    s_ab[:, a0 + SEG:a0 + 2 * SEG], ALU.add))
                ctx.op("vector", lambda v: v.drain())
                ab_fr[par] = ctx.sig("vector", lambda v, dt=dt: v.tensor_tensor(
                    s_dx[:, dt * SEG:(dt + 1) * SEG],
                    s_dl[:, dt * SEG:(dt + 1) * SEG],
                    s_xs[:, dt * SEG:(dt + 1) * SEG], ALU.mult))
            tok_dl = ctx.now("vector")

            ctx.wait("tensor", tok_fb, tok_fc, tok_dl)
            ctx.wait("gpsimd", tok_dl)
            for n in range(NS):
                ctx.wait("tensor", bc_free)
                tok_Bb = ctx.sig("tensor", lambda t, n=n: t.matmul(
                    pb[3][:, 0:SEG], s_onesb[0:1, :],
                    s_flat[0:1, n * SEG:(n + 1) * SEG], start=True, stop=True))
                tok_Cb = ctx.sig("tensor", lambda t, n=n: t.matmul(
                    pb[4][:, 0:SEG], s_onesb[64:65, :],
                    s_flat[64:65, n * SEG:(n + 1) * SEG],
                    start=True, stop=True))
                ctx.wait("vector", tok_Bb, tok_Cb, gp_b[0], gp_b[1])
                tok_bc_ev = ctx.sig("vector", lambda v: v.tensor_copy(s_Bs[:, :], pb[3][:, 0:SEG]))
                ctx.wait("gpsimd", tok_bc_ev)
                for hf in range(2):
                    ctx.wait("gpsimd", scan_done[hf])
                    gp_b[hf] = ctx.sig("gpsimd", lambda g, hf=hf: g.tensor_tensor(
                        _r(s_bbig[:, hf * 8 * SEG:(hf + 1) * 8 * SEG], "p (c t) -> p c t", c=8),
                        _r(s_dx[:, hf * 8 * SEG:(hf + 1) * 8 * SEG], "p (c t) -> p c t", c=8),
                        s_Bs[:, None, 0:SEG].broadcast_to([128, 8, SEG]), ALU.mult))
                for hf in range(2):
                    ctx.wait("scalar", a_free[0])
                    for j in range(8):
                        dt = hf * 8 + j
                        a_toks[0] = ctx.sig("scalar", lambda s, dt=dt, n=n, j=j: s.activation(
                            s_abig[:, j * SEG:(j + 1) * SEG],
                            s_dl[:, dt * SEG:(dt + 1) * SEG],
                            AF.Exp, scale=s_A[:, dt * NS + n:dt * NS + n + 1]))
                    a_first = _r(s_abig[:, :], "p (c t) -> p c t", c=8)[:, 1:8, 0:1]
                    b_first = _r(s_bbig[:, hf * 8 * SEG:(hf + 1) * 8 * SEG],
                                 "p (c t) -> p c t", c=8)[:, 1:8, 0:1]
                    hs_v = s_hstate[:, n * NDI + hf * 8 + 1:n * NDI + hf * 8 + 8, None]
                    ctx.wait("vector", a_toks[0], gp_b[hf], gp_st)
                    ctx.sig("vector", lambda v, a=a_first, h=hs_v: v.tensor_tensor(
                        s_fix[:, 0:7, None], a, h, ALU.mult))
                    ctx.op("vector", lambda v: v.drain())
                    ctx.sig("vector", lambda v, b=b_first: v.tensor_tensor(
                        b, b, s_fix[:, 0:7, None], ALU.add))
                    def _za(v, a=a_first):
                        v.memset(a, 0.0)
                        return v.drain()
                    ctx.op("vector", _za)
                    tok_scan = ctx.sig("vector", lambda v, hf=hf, n=n: v.tensor_tensor_scan(
                        s_hbig[:, :], s_abig[:, :],
                        s_bbig[:, hf * 8 * SEG:(hf + 1) * 8 * SEG],
                        s_hstate[:, n * NDI + hf * 8:n * NDI + hf * 8 + 1],
                        ALU.mult, ALU.add))
                    a_free[0] = tok_scan
                    scan_done[hf] = tok_scan
                    ctx.wait("gpsimd", tok_scan)
                    gp_st = ctx.sig("gpsimd", lambda g, hf=hf, n=n: g.tensor_copy(
                        s_hstate[:, n * NDI + hf * 8:n * NDI + hf * 8 + 8],
                        _r(s_hbig[:, :], "p (c t) -> p c t", c=8)[:, :, SEG - 1:SEG].opt()))
                    ctx.op("vector", lambda v: v.drain())
                    ctx.wait("vector", gp_y[hf])
                    tok_hc = ctx.sig("vector", lambda v, hf=hf: v.tensor_tensor(
                        _r(s_xs[:, hf * 8 * SEG:(hf + 1) * 8 * SEG], "p (c t) -> p c t", c=8),
                        _r(s_hbig[:, :], "p (c t) -> p c t", c=8),
                        pb[4][:, None, 0:SEG].broadcast_to([128, 8, SEG]), ALU.mult))
                    ctx.wait("gpsimd", tok_hc)
                    gp_y[hf] = ctx.sig("gpsimd", lambda g, hf=hf: g.tensor_tensor(
                        s_y[:, hf * 8 * SEG:(hf + 1) * 8 * SEG],
                        s_y[:, hf * 8 * SEG:(hf + 1) * 8 * SEG],
                        s_xs[:, hf * 8 * SEG:(hf + 1) * 8 * SEG], ALU.add))
                    bc_free = tok_hc
            ctx.wait("vector", gp_y[0], gp_y[1])
            ctx.wait("scalar", gp_y[0], gp_y[1])
            ctx.wait("scalar", ctx.now("vector"))
            for dt in range(NDI):
                tok_tz = ctx.sig("scalar", lambda s, dt=dt: s.activation(
                    s_g1[:, 0:SEG], s_z[:, dt * SEG:(dt + 1) * SEG], AF.Tanh, scale=0.5))
                ctx.wait("vector", tok_tz)
                ctx.sig("vector", lambda v, dt=dt: v.scalar_tensor_tensor(
                    s_g1[:, 0:SEG], s_g1[:, 0:SEG], 1.0,
                    s_z[:, dt * SEG:(dt + 1) * SEG], ALU.add, ALU.mult))
                ctx.op("vector", lambda v: v.drain())
                tok_yg = ctx.sig("vector", lambda v, dt=dt: v.scalar_tensor_tensor(
                    s_yg[:, dt * SEG:(dt + 1) * SEG], s_y[:, dt * SEG:(dt + 1) * SEG],
                    0.5, s_g1[:, 0:SEG], ALU.mult, ALU.mult))
                ctx.wait("scalar", tok_yg)
            tok_yg_all = ctx.now("vector")
            if getattr(ctx, "cur_b", 0) == 0 and sg == 0:
                S2_mktap[0]("ssm_yg", (128, NDI * SEG), BF16)
                S2_dump[0]("ssm_yg", lambda: s_yg[:, :])
            ctx.wait("tensor", tok_yg_all)
            for dp in range(NT):
                par = dp % 2
                wp = pb[5] if par == 0 else pb[6]
                ctx.wait("sync", wout_used[0])
                wout_tok[0] = ctx.sig("sync", lambda y, dp=dp: y.dma_start(
                    out=_r(s_wout[:, :], "p (c m) -> p c m", c=NDI),
                    in_=_r(i_woutT[:, dp * 128:(dp + 1) * 128], "(c p) m -> p c m", p=128)),
                    dma=True)
                ctx.wait("tensor", wout_tok[0], xev[par])
                for k in range(NDI):
                    ctx.sig("tensor", lambda t, k=k, wp=wp: t.matmul(
                        wp[:, 0:SEG],
                        s_wout[:, k * 128:(k + 1) * 128],
                        s_yg[:, k * SEG:(k + 1) * SEG], start=(k == 0), stop=(k == NDI - 1)))
                wout_used[0] = ctx.now("tensor")
                ctx.wait("vector", wout_used[0])
                xev[par] = ctx.sig("vector", lambda v, dp=dp, t0=t0, wp=wp: v.scalar_tensor_tensor(
                    s_x[:, dp * T + t0:dp * T + t0 + SEG], wp[:, 0:SEG],
                    s_boutc[:, dp:dp + 1],
                    s_x[:, dp * T + t0:dp * T + t0 + SEG], ALU.add, ALU.add))
            tok_seg = ctx.now("vector")
            ctx.wait("tensor", tok_seg)
            ctx.wait("scalar", tok_seg)
            ctx.wait("scalar", wout_used[0])
            ctx.wait("vector", wout_used[0])


    es2.close()


def emit_l1_moe(ctx, nc, S, i_gwT, i_uwT, i_dwT, i_sgwT, i_suwT, i_sdwT):
    s_xn, s_x, pb, pbh = S["xn"], S["x"], S["pb"], S["pbh"]
    s_rwT, s_rbb, s_onesb, s_onesf, s_ident, s_kone = (S["rwT"], S["rbb"], S["onesb"],
                                                       S["onesf"], S["ident"], S["kone"])
    es_moe = ExitStack()
    s_wTb = es_moe.enter_context(
        nc.sbuf_tensor("mo_wTb_%d" % ctx.uid(), [NE1, T], BF16))
    s_wrow = es_moe.enter_context(
        nc.sbuf_tensor("mo_wrow_%d" % ctx.uid(), [1, T], BF16))
    with nc.sbuf_tensor("mo_ro_%d" % ctx.uid(), [128, 16 * NE1], F32) as s_ro, \
         nc.sbuf_tensor("mo_wdb_%d" % ctx.uid(), [128, NE1], BF16) as s_wdb:
        C0 = 0
        def cols(j):
            return slice(j * NE1, (j + 1) * NE1)
        ctx.auto_drain = True
        for tile in range(16):
            tt = tile * 128
            for k in range(NT):
                ctx.sig("tensor", lambda t, k=k, tt=tt: t.matmul(
                    pb[0][:, 0:NE1], s_xn[:, k * T + tt:k * T + tt + 128],
                    s_rwT[:, k * NE1:(k + 1) * NE1], start=(k == 0), stop=(k == NT - 1)))
            tok_lg = ctx.now("tensor")
            ctx.wait("scalar", tok_lg)
            tok_th = ctx.sig("scalar", lambda s: s.activation(
                s_ro[:, cols(0)], pb[0][:, 0:NE1], AF.Tanh, scale=0.5))
            ctx.wait("vector", tok_th)
            ctx.sig("vector", lambda v: v.tensor_scalar(
                s_ro[:, cols(1)], s_ro[:, cols(0)], 1.0, 0.5, ALU.add, ALU.mult))
            tok_sfc = ctx.sig("vector", lambda v: v.tensor_tensor(
                s_ro[:, cols(2)], s_ro[:, cols(1)], s_rbb[:, :], ALU.add))
            ctx.wait("tensor", tok_sfc)
            mcol = 10 * NE1
            for g in range(4):
                gsl = slice(2 * NE1 + 8 * g, 2 * NE1 + 8 * (g + 1))
                ctx.op("vector", lambda v: v.drain())
                ctx.sig("vector", lambda v, gsl=gsl, mcol=mcol: v.tensor_reduce(
                    s_ro[:, mcol:mcol + 1], s_ro[:, gsl], AX.X, ALU.max))
                ctx.op("vector", lambda v: v.drain())
                ctx.sig("vector", lambda v, gsl=gsl, mcol=mcol: v.tensor_scalar(
                    s_ro[:, cols(6)][:, 0:8], s_ro[:, gsl], s_ro[:, mcol:mcol + 1],
                    None, ALU.is_equal))
                ctx.op("vector", lambda v: v.drain())
                ctx.sig("vector", lambda v, gsl=gsl: v.scalar_tensor_tensor(
                    s_ro[:, cols(6)][:, 8:16], s_ro[:, cols(6)][:, 0:8], -1e30,
                    s_ro[:, gsl], ALU.mult, ALU.add))
                ctx.op("vector", lambda v: v.drain())
                ctx.sig("vector", lambda v, mcol=mcol: v.tensor_reduce(
                    s_ro[:, mcol + 1:mcol + 2], s_ro[:, cols(6)][:, 8:16], AX.X, ALU.max))
                ctx.op("vector", lambda v: v.drain())
                ctx.sig("vector", lambda v, g=g, mcol=mcol: v.tensor_tensor(
                    s_ro[:, 3 * NE1 + g:3 * NE1 + g + 1], s_ro[:, mcol:mcol + 1],
                    s_ro[:, mcol + 1:mcol + 2], ALU.add))
            gsl4 = slice(3 * NE1, 3 * NE1 + 4)
            ctx.op("vector", lambda v: v.drain())
            ctx.sig("vector", lambda v: v.tensor_reduce(
                s_ro[:, mcol:mcol + 1], s_ro[:, gsl4], AX.X, ALU.max))
            ctx.op("vector", lambda v: v.drain())
            ctx.sig("vector", lambda v: v.tensor_scalar(
                s_ro[:, cols(6)][:, 0:4], s_ro[:, gsl4], s_ro[:, mcol:mcol + 1],
                None, ALU.is_equal))
            ctx.op("vector", lambda v: v.drain())
            ctx.sig("vector", lambda v: v.scalar_tensor_tensor(
                s_ro[:, cols(6)][:, 8:12], s_ro[:, cols(6)][:, 0:4], -1e30,
                s_ro[:, gsl4], ALU.mult, ALU.add))
            ctx.op("vector", lambda v: v.drain())
            ctx.sig("vector", lambda v: v.tensor_reduce(
                s_ro[:, mcol + 1:mcol + 2], s_ro[:, cols(6)][:, 8:12], AX.X, ALU.max))
            ctx.op("vector", lambda v: v.drain())
            ctx.sig("vector", lambda v: v.tensor_scalar(
                s_ro[:, 3 * NE1 + 8:3 * NE1 + 12], s_ro[:, gsl4],
                s_ro[:, mcol + 1:mcol + 2], None, ALU.is_ge))
            ctx.sig("vector", lambda v: v.tensor_tensor(
                _r(s_ro[:, cols(4)], "p (g j) -> p g j", g=4),
                _r(s_ro[:, cols(2)], "p (g j) -> p g j", g=4),
                s_ro[:, 3 * NE1 + 8:3 * NE1 + 12][:, :, None].broadcast_to([128, 4, 8]),
                ALU.mult))
            ctx.sig("vector", lambda v: v.tensor_copy(s_ro[:, cols(5)], s_ro[:, cols(4)]))
            ctx.op("vector", lambda v: v.drain())
            def _z(v):
                v.memset(s_ro[:, cols(8)], 0.0)
                return v.drain()
            ctx.op("vector", _z)
            for i4 in range(4):
                ctx.op("vector", lambda v: v.drain())
                ctx.sig("vector", lambda v, mcol=mcol: v.tensor_reduce(
                    s_ro[:, mcol + 2:mcol + 3], s_ro[:, cols(5)], AX.X, ALU.max))
                ctx.op("vector", lambda v: v.drain())
                ctx.sig("vector", lambda v, mcol=mcol: v.tensor_scalar(
                    s_ro[:, cols(6)], s_ro[:, cols(5)], s_ro[:, mcol + 2:mcol + 3],
                    None, ALU.is_equal))
                ctx.op("vector", lambda v: v.drain())
                ctx.sig("vector", lambda v: v.tensor_tensor_scan(
                    s_ro[:, cols(7)], s_onesf[:, 0:NE1], s_ro[:, cols(6)],
                    0.0, ALU.mult, ALU.add))
                ctx.op("vector", lambda v: v.drain())
                ctx.sig("vector", lambda v: v.tensor_scalar(
                    s_ro[:, cols(7)], s_ro[:, cols(7)], 1.0, None, ALU.is_equal))
                ctx.op("vector", lambda v: v.drain())
                ctx.sig("vector", lambda v: v.tensor_tensor(
                    s_ro[:, cols(6)], s_ro[:, cols(6)], s_ro[:, cols(7)], ALU.mult))
                ctx.op("vector", lambda v: v.drain())
                ctx.sig("vector", lambda v: v.tensor_tensor(
                    s_ro[:, cols(8)], s_ro[:, cols(8)], s_ro[:, cols(6)], ALU.add))
                ctx.sig("vector", lambda v: v.scalar_tensor_tensor(
                    s_ro[:, cols(5)], s_ro[:, cols(6)], -1e30, s_ro[:, cols(5)],
                    ALU.mult, ALU.add))
                ctx.op("vector", lambda v: v.drain())
            ctx.sig("vector", lambda v: v.tensor_tensor(
                s_ro[:, cols(9)], s_ro[:, cols(8)], s_ro[:, cols(1)], ALU.mult))
            ctx.op("vector", lambda v: v.drain())
            ctx.sig("vector", lambda v, mcol=mcol: v.tensor_reduce(
                s_ro[:, mcol + 3:mcol + 4], s_ro[:, cols(9)], AX.X, ALU.add))
            ctx.op("vector", lambda v: v.drain())
            tok_den = ctx.sig("vector", lambda v, mcol=mcol: v.tensor_scalar(
                s_ro[:, mcol + 3:mcol + 4], s_ro[:, mcol + 3:mcol + 4], 1e-20,
                None, ALU.add))
            ctx.wait("scalar", tok_den)
            ctx.sig("scalar", lambda s, mcol=mcol: s.activation(
                s_ro[:, mcol + 4:mcol + 5], s_ro[:, mcol + 3:mcol + 4], AF.Ln))
            tok_wi = ctx.sig("scalar", lambda s, mcol=mcol: s.activation(
                s_ro[:, mcol + 4:mcol + 5], s_ro[:, mcol + 4:mcol + 5], AF.Exp, scale=-1.0))
            ctx.wait("vector", tok_wi)
            tok_wd = ctx.sig("vector", lambda v, mcol=mcol: v.tensor_scalar(
                s_wdb[:, :], s_ro[:, cols(9)], s_ro[:, mcol + 4:mcol + 5], 2.5,
                ALU.mult, ALU.mult))
            ctx.wait("tensor", tok_wd)
            tok_tr = ctx.sig("tensor", lambda t: t.transpose(
                pbh[0:NE1, 0:128], s_wdb[:, :], s_ident[:, :]))
            ctx.wait("vector", tok_tr)
            tok_wT = ctx.sig("vector", lambda v, tt=tt: v.tensor_copy(
                s_wTb[:, tt:tt + 128], pbh[0:NE1, 0:128]))
            ctx.wait("tensor", tok_wT)
        tok_wTb = ctx.now("vector")
        ctx.auto_drain = False
        if getattr(ctx, "cur_b", 0) == 0:
            S2_mktap[0]("moe_ro", (128, 16 * NE1))
            S2_mktap[0]("rbb", (128, NE1))
            S2_dump[0]("rbb", lambda: s_rbb[:, :])
            S2_mktap[0]("wTb", (NE1, T), BF16)
            S2_dump[0]("moe_ro", lambda: s_ro[:, :])
            S2_dump[0]("wTb", lambda: s_wTb[:, :])

    with nc.sbuf_tensor("sh_gw_%d" % ctx.uid(), [128, NT * SHI], BF16) as s_sgw, \
         nc.sbuf_tensor("sh_uw_%d" % ctx.uid(), [128, NT * SHI], BF16) as s_suw, \
         nc.sbuf_tensor("sh_dw_%d" % ctx.uid(), [128, NT * D], BF16) as s_sdw, \
         nc.sbuf_tensor("sh_hid_%d" % ctx.uid(), [128, NT * TC], BF16) as s_shid, \
         nc.sbuf_tensor("sh_sg_%d" % ctx.uid(), [128, TC], F32) as s_ssg:
        ctx.wait("sync", ctx.now("vector"), ctx.now("tensor"), ctx.now("scalar"))
        tok_g = ctx.sig("sync", lambda y: y.dma_start(
            out=_r(s_sgw[:, :], "p (c m) -> p c m", c=NT),
            in_=_r(i_sgwT, "(c p) m -> p c m", p=128)), dma=True)
        tok_u = ctx.sig("sync", lambda y: y.dma_start(
            out=_r(s_suw[:, :], "p (c m) -> p c m", c=NT),
            in_=_r(i_suwT, "(c p) m -> p c m", p=128)), dma=True)
        tok_d = ctx.sig("sync", lambda y: y.dma_start(
            out=_r(s_sdw[:, :], "p (c m) -> p c m", c=NT),
            in_=_r(i_sdwT, "(c p) m -> p c m", p=128)), dma=True)
        ctx.wait("tensor", tok_g, tok_u, tok_d)
        shev_tok = [None, None]
        for c in range(NTC):
            for it in range(NT):
                for k in range(NT):
                    ctx.sig("tensor", lambda t, k=k, it=it, c=c: t.matmul(
                        pb[0][:, 0:TC], s_sgw[:, k * SHI + it * 128:k * SHI + (it + 1) * 128],
                        s_xn[:, k * T + c * TC:k * T + (c + 1) * TC],
                        start=(k == 0), stop=(k == NT - 1)))
                for k in range(NT):
                    ctx.sig("tensor", lambda t, k=k, it=it, c=c: t.matmul(
                        pb[1][:, 0:TC], s_suw[:, k * SHI + it * 128:k * SHI + (it + 1) * 128],
                        s_xn[:, k * T + c * TC:k * T + (c + 1) * TC],
                        start=(k == 0), stop=(k == NT - 1)))
                tok_gu = ctx.now("tensor")
                ctx.wait("scalar", tok_gu)
                tok_sg = ctx.sig("scalar", lambda s: s.activation(
                    s_ssg[:, :], pb[0][:, 0:TC], AF.Silu))
                ctx.wait("vector", tok_sg)
                tok_hid = ctx.sig("vector", lambda v, it=it: v.tensor_tensor(
                    s_shid[:, it * TC:(it + 1) * TC], s_ssg[:, :], pb[1][:, 0:TC],
                    ALU.mult))
                ctx.wait("tensor", tok_hid)
            for dp in range(NT):
                bnk = 2 + dp % 2
                ctx.wait("tensor", shev_tok[dp % 2])
                for k in range(NT):
                    ctx.sig("tensor", lambda t, k=k, dp=dp, bnk=bnk: t.matmul(
                        pb[bnk][:, 0:TC], s_sdw[:, k * D + dp * 128:k * D + (dp + 1) * 128],
                        s_shid[:, k * TC:(k + 1) * TC], start=(k == 0), stop=(k == NT - 1)))
                tok_o = ctx.now("tensor")
                ctx.wait("vector", tok_o)
                shev_tok[dp % 2] = ctx.sig("vector", lambda v, dp=dp, c=c, bnk=bnk:
                                           v.scalar_tensor_tensor(
                    s_x[:, dp * T + c * TC:dp * T + (c + 1) * TC], pb[bnk][:, 0:TC], 1.0,
                    s_x[:, dp * T + c * TC:dp * T + (c + 1) * TC], ALU.mult, ALU.add))
            ctx.wait("tensor", shev_tok[0], shev_tok[1])

    with nc.sbuf_tensor("mo_gw_%d" % ctx.uid(), [128, 2 * NT * I1], BF16) as s_gw, \
         nc.sbuf_tensor("mo_uw_%d" % ctx.uid(), [128, 2 * NT * I1], BF16) as s_uw, \
         nc.sbuf_tensor("mo_dw_%d" % ctx.uid(), [128, 2 * 4 * D], BF16) as s_dw, \
         nc.sbuf_tensor("mo_hid_%d" % ctx.uid(), [128, 4 * T], BF16) as s_hid, \
         nc.sbuf_tensor("mo_sg_%d" % ctx.uid(), [128, 2 * TC], F32) as s_sg, \
         nc.sbuf_tensor("mo_t1_%d" % ctx.uid(), [128, 2 * TC], F32) as s_t1:
        ld_tok = [None, None]
        used_tok = [None, None]
        hid_done = [None, None]
        sg_free = [None, None]
        tok_wb_last = None
        for e in range(NE1):
            sl = e % 2
            if e == 0:
                ctx.wait("sync", ctx.now("vector"), ctx.now("tensor"), tok_wTb)
            ctx.wait("sync", used_tok[sl])
            ctx.wait("sync", tok_wb_last)
            tok_row = ctx.sig("sync", lambda y, e=e: y.dma_start(
                out=s_wrow[0:1, :], in_=s_wTb[e:e + 1, :]), dma=True)
            ctx.sig("sync", lambda y, e=e, sl=sl: y.dma_start(
                out=_r(s_gw[:, sl * NT * I1:(sl + 1) * NT * I1], "p (c m) -> p c m", c=NT),
                in_=_r(i_gwT[e], "(c p) m -> p c m", p=128)), dma=True)
            ctx.sig("sync", lambda y, e=e, sl=sl: y.dma_start(
                out=_r(s_uw[:, sl * NT * I1:(sl + 1) * NT * I1], "p (c m) -> p c m", c=NT),
                in_=_r(i_uwT[e], "(c p) m -> p c m", p=128)), dma=True)
            ld_tok[sl] = ctx.sig("sync", lambda y, e=e, sl=sl: y.dma_start(
                out=_r(s_dw[:, sl * 4 * D:(sl + 1) * 4 * D], "p (c m) -> p c m", c=4),
                in_=_r(i_dwT[e], "(c p) m -> p c m", p=128)), dma=True)
            ctx.wait("tensor", ld_tok[sl])
            g0 = sl * NT * I1
            d0 = sl * 4 * D
            for c in range(NTC):
                wpar = c % 2
                ctx.wait("tensor", tok_row, hid_done[0], hid_done[1])
                tok_wb = ctx.sig("tensor", lambda t, c=c: t.matmul(
                    pb[2][:, 0:TC], s_onesb[0:1, :],
                    s_wrow[0:1, c * TC:(c + 1) * TC],
                    start=True, stop=True))
                tok_wb_last = tok_wb
                for it in range(4):
                    par = it % 2
                    pg, pu = pb[0 + 5 * par], pb[1 + 5 * par]
                    ctx.wait("tensor", hid_done[par])
                    for k in range(NT):
                        ctx.sig("tensor", lambda t, k=k, it=it, c=c, g0=g0, pg=pg: t.matmul(
                            pg[:, 0:TC],
                            s_gw[:, g0 + k * I1 + it * 128:g0 + k * I1 + (it + 1) * 128],
                            s_xn[:, k * T + c * TC:k * T + (c + 1) * TC],
                            start=(k == 0), stop=(k == NT - 1)))
                    for k in range(NT):
                        ctx.sig("tensor", lambda t, k=k, it=it, c=c, g0=g0, pu=pu: t.matmul(
                            pu[:, 0:TC],
                            s_uw[:, g0 + k * I1 + it * 128:g0 + k * I1 + (it + 1) * 128],
                            s_xn[:, k * T + c * TC:k * T + (c + 1) * TC],
                            start=(k == 0), stop=(k == NT - 1)))
                    tok_gu = ctx.now("tensor")
                    ctx.wait("scalar", tok_gu, sg_free[par])
                    tok_sg = ctx.sig("scalar", lambda s, par=par, pg=pg: s.activation(
                        s_sg[:, par * TC:(par + 1) * TC], pg[:, 0:TC], AF.Silu))
                    ctx.wait("vector", tok_sg, tok_wb)
                    ctx.sig("vector", lambda v, par=par, pu=pu: v.tensor_tensor(
                        s_t1[:, par * TC:(par + 1) * TC],
                        s_sg[:, par * TC:(par + 1) * TC], pu[:, 0:TC], ALU.mult))
                    ctx.op("vector", lambda v: v.drain())
                    tok_hid = ctx.sig("vector", lambda v, it=it, c=c, par=par: v.tensor_tensor(
                        s_hid[:, it * T + c * TC:it * T + (c + 1) * TC],
                        s_t1[:, par * TC:(par + 1) * TC],
                        pb[2][:, 0:TC], ALU.mult))
                    hid_done[par] = tok_hid
                    sg_free[par] = tok_hid
            ctx.wait("tensor", hid_done[0], hid_done[1])
            dev_tok = [None, None]
            for c in range(NTC):
                for dp in range(NT):
                    i2 = c * NT + dp
                    bnk = 3 + (i2 % 2)
                    ctx.wait("tensor", dev_tok[i2 % 2])
                    for k in range(4):
                        ctx.sig("tensor", lambda t, k=k, dp=dp, c=c, d0=d0, bnk=bnk: t.matmul(
                            pb[bnk][:, 0:TC],
                            s_dw[:, d0 + k * D + dp * 128:d0 + k * D + (dp + 1) * 128],
                            s_hid[:, k * T + c * TC:k * T + (c + 1) * TC],
                            start=(k == 0), stop=(k == 3)))
                    tok_o = ctx.now("tensor")
                    ctx.wait("vector", tok_o)
                    dev_tok[i2 % 2] = ctx.sig("vector", lambda v, dp=dp, c=c, bnk=bnk:
                                              v.scalar_tensor_tensor(
                        s_x[:, dp * T + c * TC:dp * T + (c + 1) * TC],
                        pb[bnk][:, 0:TC], 1.0,
                        s_x[:, dp * T + c * TC:dp * T + (c + 1) * TC],
                        ALU.mult, ALU.add))
            ctx.wait("tensor", dev_tok[0], dev_tok[1])
            used_tok[sl] = ctx.now("tensor")
    es_moe.close()


def build_program(debug_taps=()):
    nc = bass.Bass("TRN2", target_bir_lowering=False)
    es = ExitStack()

    def din(name, shape, dt=F32):
        return nc.dram_tensor(name, list(shape), dt, kind="ExternalInput").ap()

    d_x = din("x_in", (BPC, D, T))
    d_wq = din("w_q", (D, H)); d_wk = din("w_k", (D, H)); d_wv = din("w_v", (D, H))
    d_wo = din("w_o", (H, D))
    d_wbase = din("w_base", (D, D))
    d_wd1 = din("wd1", (NE0, D, H0)); d_wd2 = din("wd2", (NE0, H0, D))
    d_wad = din("wa_down", (NE0, D, R0)); d_wau = din("wa_up", (NE0, R0, D))
    d_bbase = din("bb_base", (128, NT))
    d_bd1 = din("bd1_c", (128, NE0, 2))
    d_bd2 = din("bd2_c", (128, NE0, NT))
    d_bad = din("bad_c", (R0, NE0))
    d_bau = din("bau_c", (128, NE0, NT))
    d_winT = din("w_inT", (D, 2 * DI))
    d_wxT = din("w_xT", (DI, 128))
    d_wdtT = din("w_dtT", (DTR, DI))
    d_woutT = din("w_outT", (DI, D))
    d_binc = din("b_in_c", (128, 32))
    d_bxc = din("b_x_c", (128, 1))
    d_bdtc = din("b_dt_c", (128, NDI))
    d_alogc = din("a_log_c", (128, NDI * NS))
    d_dc = din("d_c", (128, NDI))
    d_boutc = din("b_out_c", (128, NT))
    d_rwT = din("rw_T", (D, NE1))
    d_rb = din("r_bias", (1, NE1))
    d_gwT = din("gw_T", (NE1, D, I1))
    d_uwT = din("uw_T", (NE1, D, I1))
    d_dwT = din("dw_T", (NE1, I1, D))
    d_sgwT = din("sgw_T", (D, SHI))
    d_suwT = din("suw_T", (D, SHI))
    d_sdwT = din("sdw_T", (SHI, D))
    d_ident = din("c_ident", (128, 128), BF16)
    d_triu = din("c_triu", (128, 128), BF16)
    d_onesb = din("c_ones_b", (128, 128), BF16)
    d_onesf = din("c_ones_f", (128, 128), F32)

    d_out = nc.dram_tensor("x_out", [BPC, D, T], F32, kind="ExternalOutput").ap()

    def dint(name, shape):
        return nc.dram_tensor(name, list(shape), BF16, kind="Internal").ap()

    i_wq = dint("i_wq", (D, H)); i_wk = dint("i_wk", (D, H)); i_wv = dint("i_wv", (D, H))
    i_wo = dint("i_wo", (H, D))
    i_wbase = dint("i_wbase", (D, D))
    i_wd1 = dint("i_wd1", (NE0, D, H0)); i_wd2 = dint("i_wd2", (NE0, H0, D))
    i_wad = dint("i_wad", (NE0, D, R0)); i_wau = dint("i_wau", (NE0, R0, D))
    i_winT = dint("i_winT", (D, 2 * DI))
    i_wxT = dint("i_wxT", (DI, 128))
    i_wdtT = dint("i_wdtT", (DTR, DI))
    i_woutT = dint("i_woutT", (DI, D))
    i_rwT = dint("i_rwT", (D, NE1))
    i_gwT = dint("i_gwT", (NE1, D, I1))
    i_uwT = dint("i_uwT", (NE1, D, I1))
    i_dwT = dint("i_dwT", (NE1, I1, D))
    i_sgwT = dint("i_sgwT", (D, SHI))
    i_suwT = dint("i_suwT", (D, SHI))
    i_sdwT = dint("i_sdwT", (SHI, D))

    ctx = Ctx(nc, es)
    ctx.debug_taps = tuple(debug_taps)

    taps = {}
    def mktap(name, shape, dt=F32):
        if name in debug_taps and name not in taps:
            taps[name] = nc.dram_tensor("tap_" + name, list(shape), dt,
                                        kind="ExternalOutput").ap()

    for b in range(BPC):
        mktap("xn0_b%d" % b, (128, NT * T), BF16)
        mktap("x_l0_b%d" % b, (128, NT * T))
        mktap("oall_b%d" % b, (H, T), BF16)
        mktap("xn1_b%d" % b, (128, NT * T), BF16)
        mktap("x_ssm_b%d" % b, (128, NT * T))

    def dump(name, ap_fn):
        t = taps.get(name)
        if t is None:
            return
        bar = ctx.barrier()
        ctx.sig("gpsimd", lambda g, t=t: g.dma_start(out=t, in_=ap_fn()), dma=True)
        ctx.barrier()
    S2_dump[0] = dump
    S2_mktap[0] = mktap

    def sbr(name, shape, dt):
        return es.enter_context(nc.sbuf_tensor(name, list(shape), dt))

    S = {}
    S["x"] = sbr("s_x", (128, NT * T), F32)
    S["xn"] = sbr("s_xn", (128, NT * T), BF16)
    S["ident"] = sbr("s_ident", (128, 128), BF16)
    S["triu"] = sbr("s_triu", (128, 128), BF16)
    S["onesb"] = sbr("s_onesb", (128, 128), BF16)
    S["onesf"] = sbr("s_onesf", (128, 128), F32)
    S["wq"] = sbr("s_wq", (128, NT * H), BF16)
    S["wk"] = sbr("s_wk", (128, NT * H), BF16)
    S["wv"] = sbr("s_wv", (128, NT * H), BF16)
    S["wo"] = sbr("s_wo", (H, D), BF16)
    S["rwT"] = sbr("s_rwT", (128, NT * NE1), BF16)
    S["bbase"] = sbr("s_bbase", (128, NT), F32)
    S["bd1"] = sbr("s_bd1", (128, NE0 * 2), F32)
    S["bd2"] = sbr("s_bd2", (128, NE0 * NT), F32)
    S["bad"] = sbr("s_bad", (R0, NE0), F32)
    S["bau"] = sbr("s_bau", (128, NE0 * NT), F32)
    S["binc"] = sbr("s_binc", (128, 32), F32)
    S["bxc"] = sbr("s_bxc", (128, 1), F32)
    S["bdtc"] = sbr("s_bdtc", (128, NDI), F32)
    S["A"] = sbr("s_A", (128, NDI * NS), F32)
    S["dc"] = sbr("s_dc", (128, NDI), F32)
    S["boutc"] = sbr("s_boutc", (128, NT), F32)
    S["rbb"] = sbr("s_rbb", (128, NE1), F32)
    S["kone"] = sbr("s_kone", (128, 2), F32)
    S["hstate"] = sbr("s_hstate", (128, NDI * NS), F32)

    pb = [es.enter_context(nc.psum_tensor("pb%d" % i, [128, 512], F32))
          for i in range(7)]
    pbh = es.enter_context(nc.psum_tensor("pbh", [128, 1024], BF16))
    S["pb"] = pb
    S["pbh"] = pbh

    cast_toks = []
    for (dst, src) in [(i_wq, d_wq), (i_wk, d_wk), (i_wv, d_wv), (i_wo, d_wo),
                       (i_wbase, d_wbase), (i_wd1, d_wd1), (i_wd2, d_wd2),
                       (i_wad, d_wad), (i_wau, d_wau)]:
        cast_toks.append(ctx.sig("gpsimd", lambda g, a=dst, b=src: g.dma_start(out=a, in_=b), dma=True))
    tok_cast_l0 = cast_toks[-1]
    for (dst, src) in [(i_winT, d_winT), (i_wxT, d_wxT), (i_wdtT, d_wdtT),
                       (i_woutT, d_woutT), (i_rwT, d_rwT),
                       (i_sgwT, d_sgwT), (i_suwT, d_suwT), (i_sdwT, d_sdwT)]:
        cast_toks.append(ctx.sig("gpsimd", lambda g, a=dst, b=src: g.dma_start(out=a, in_=b), dma=True))
    tok_cast_l1 = cast_toks[-1]
    for (dst, src) in [(i_gwT, d_gwT), (i_uwT, d_uwT), (i_dwT, d_dwT)]:
        cast_toks.append(ctx.sig("gpsimd", lambda g, a=dst, b=src: g.dma_start(out=a, in_=b), dma=True))
    tok_cast_moe = cast_toks[-1]

    for (dst, src) in [(S["ident"], d_ident), (S["triu"], d_triu), (S["onesb"], d_onesb),
                       (S["onesf"], d_onesf), (S["bbase"], d_bbase),
                       (S["bad"], d_bad), (S["binc"], d_binc), (S["bxc"], d_bxc),
                       (S["bdtc"], d_bdtc), (S["dc"], d_dc), (S["boutc"], d_boutc)]:
        ctx.sig("sync", lambda y, a=dst, b=src: y.dma_start(out=a[:, :], in_=b), dma=True)
    ctx.sig("sync", lambda y: y.dma_start(out=S["bd1"][:, :], in_=_r(d_bd1, "p e c -> p (e c)")), dma=True)
    ctx.sig("sync", lambda y: y.dma_start(out=S["bd2"][:, :], in_=_r(d_bd2, "p e c -> p (e c)")), dma=True)
    ctx.sig("sync", lambda y: y.dma_start(out=S["bau"][:, :], in_=_r(d_bau, "p e c -> p (e c)")), dma=True)
    ctx.sig("sync", lambda y: y.dma_start(out=S["A"][:, :], in_=d_alogc), dma=True)
    ctx.sig("sync", lambda y: y.dma_start(out=S["rbb"][0:1, 0:NE1], in_=d_rb), dma=True)
    ctx.wait("sync", tok_cast_l1)
    for (dst, src) in [(S["wq"], i_wq), (S["wk"], i_wk), (S["wv"], i_wv)]:
        ctx.sig("sync", lambda y, a=dst, b=src: y.dma_start(
            out=_r(a[:, :], "p (c m) -> p c m", m=H),
            in_=_r(b, "(c p) m -> p c m", p=128)), dma=True)
    ctx.sig("sync", lambda y: y.dma_start(out=S["wo"][:, :], in_=i_wo), dma=True)
    ctx.sig("sync", lambda y: y.dma_start(
        out=_r(S["rwT"][:, :], "p (c m) -> p c m", m=NE1),
        in_=_r(i_rwT, "(c p) m -> p c m", p=128)), dma=True)
    tok_params = ctx.now("sync")

    ctx.wait("tensor", tok_params)
    tok_rbmm = ctx.sig("tensor", lambda t: t.matmul(
        pb[0][:, 0:NE1], S["onesf"][0:1, :], S["rbb"][0:1, 0:NE1], start=True, stop=True))
    ctx.wait("vector", tok_rbmm)
    ctx.sig("vector", lambda v: v.tensor_copy(S["rbb"][:, :], pb[0][:, 0:NE1]))
    ctx.wait("scalar", tok_params)
    tok_Ae = ctx.sig("scalar", lambda s: s.activation(S["A"][:, :], S["A"][:, :], AF.Exp))
    ctx.wait("vector", tok_Ae)
    ctx.sig("vector", lambda v: v.tensor_scalar(S["A"][:, :], S["A"][:, :], -1.0, None, ALU.mult))
    def _kone(v):
        v.memset(S["kone"][:, 0:1], 1.0)
        v.memset(S["kone"][:, 1:2], 1e-5)
        return v.drain()
    ctx.op("vector", _kone)
    ctx.barrier()

    for b in range(BPC):
        ctx.cur_b = b
        tok_x = ctx.sig("sync", lambda y, b=b: y.dma_start(
            out=_r(S["x"][:, :], "p (c t) -> p c t", c=NT),
            in_=_r(d_x[b], "(c p) t -> p c t", p=128)), dma=True)
        emit_layernorm(ctx, nc, S, [tok_x])
        ctx.barrier()
        dump("xn0_b%d" % b, lambda: S["xn"][:, :])
        es_l0 = ExitStack()
        S["oall"] = es_l0.enter_context(
            nc.sbuf_tensor("s_oall_%d" % ctx.uid(), [H, T], BF16))
        if PHASES["l0a"]:
            emit_l0_attn(ctx, nc, S)
        dump("oall_b%d" % b, lambda: S["oall"][:, :])
        if PHASES["l0f"]:
            emit_l0_ffn(ctx, nc, S, i_wd1, i_wd2, i_wad, i_wau, i_wbase)
        es_l0.close()
        ctx.barrier()
        dump("x_l0_b%d" % b, lambda: S["x"][:, :])
        emit_layernorm(ctx, nc, S, [])
        ctx.barrier()
        dump("xn1_b%d" % b, lambda: S["xn"][:, :])
        if PHASES["ssm"]:
            emit_l1_ssm(ctx, nc, S, i_winT, i_wxT, i_wdtT, i_woutT)
        ctx.barrier()
        dump("x_ssm_b%d" % b, lambda: S["x"][:, :])
        if PHASES["moe"]:
            ctx.wait("sync", tok_cast_moe)
            emit_l1_moe(ctx, nc, S, i_gwT, i_uwT, i_dwT, i_sgwT, i_suwT, i_sdwT)
        ctx.barrier()
        tok_store = ctx.sig("gpsimd", lambda g, b=b: g.dma_start(
            out=_r(d_out[b], "(c p) t -> p c t", p=128),
            in_=_r(S["x"][:, :], "p (c t) -> p c t", c=NT)), dma=True)
        ctx.wait("sync", tok_store)
        ctx.wait("gpsimd", tok_store)

    ctx.barrier()

    blk = es.enter_context(nc.Block())

    @blk.tensor
    def _(e):
        for f in ctx.q["tensor"]:
            f(e)

    @blk.vector
    def _(e):
        for f in ctx.q["vector"]:
            f(e)

    @blk.scalar
    def _(e):
        for f in ctx.q["scalar"]:
            f(e)

    @blk.gpsimd
    def _(e):
        for f in ctx.q["gpsimd"]:
            f(e)

    @blk.sync
    def _(e):
        for f in ctx.q["sync"]:
            f(e)

    es.close()
    return nc


def host_inputs(inp, core):
    f = np.ascontiguousarray

    def t(a):
        return f(np.asarray(a).transpose())

    x = np.asarray(inp["L0_attn_x"])
    xb = x[core * BPC:(core + 1) * BPC]
    m = {
        "x_in": f(xb.transpose(0, 2, 1)),
        "w_q": f(inp["L0_attn_W_q"]), "w_k": f(inp["L0_attn_W_k"]),
        "w_v": f(inp["L0_attn_W_v"]), "w_o": f(inp["L0_attn_W_o"]),
        "w_base": f(inp["L0_ffn_W_base"]),
        "wd1": f(inp["L0_ffn_Wd1"]), "wd2": f(inp["L0_ffn_Wd2"]),
        "wa_down": f(inp["L0_ffn_Wa_down"]), "wa_up": f(inp["L0_ffn_Wa_up"]),
        "bb_base": f(np.asarray(inp["L0_ffn_b_base"]).reshape(NT, 128).T),
        "bd1_c": f(np.asarray(inp["L0_ffn_bd1"]).reshape(NE0, 2, 128).transpose(2, 0, 1)),
        "bd2_c": f(np.asarray(inp["L0_ffn_bd2"]).reshape(NE0, NT, 128).transpose(2, 0, 1)),
        "bad_c": f(np.asarray(inp["L0_ffn_ba_down"]).T),
        "bau_c": f(np.asarray(inp["L0_ffn_ba_up"]).reshape(NE0, NT, 128).transpose(2, 0, 1)),
        "w_inT": t(inp["L1_attn_W_in"]),
        "w_dtT": t(inp["L1_attn_W_dt"]),
        "w_outT": t(inp["L1_attn_W_out"]),
        "b_in_c": f(np.asarray(inp["L1_attn_b_in"]).reshape(32, 128).T),
        "b_dt_c": f(np.asarray(inp["L1_attn_b_dt"]).reshape(NDI, 128).T),
        "a_log_c": f(np.asarray(inp["L1_attn_A_log"]).reshape(NDI, 128, NS)
                     .transpose(1, 0, 2).reshape(128, NDI * NS)),
        "d_c": f(np.asarray(inp["L1_attn_D"]).reshape(NDI, 128).T),
        "b_out_c": f(np.asarray(inp["L1_attn_b_out"]).reshape(NT, 128).T),
        "rw_T": t(inp["L1_ffn_router_weight"]),
        "r_bias": f(np.asarray(inp["L1_ffn_router_bias"]).reshape(1, NE1)),
        "gw_T": f(np.asarray(inp["L1_ffn_gate_w"]).transpose(0, 2, 1)),
        "uw_T": f(np.asarray(inp["L1_ffn_up_w"]).transpose(0, 2, 1)),
        "dw_T": f(np.asarray(inp["L1_ffn_down_w"]).transpose(0, 2, 1)),
        "sgw_T": t(inp["L1_ffn_shared_gate_w"]),
        "suw_T": t(inp["L1_ffn_shared_up_w"]),
        "sdw_T": t(inp["L1_ffn_shared_down_w"]),
    }
    wx = np.asarray(inp["L1_attn_W_x"])
    wxT = np.zeros((DI, 128), dtype=np.float32)
    wxT[:, 0:64] = wx[0:64].T
    wxT[:, 64:80] = wx[64:80].T
    wxT[:, 96:112] = wx[80:96].T
    m["w_xT"] = f(wxT)
    bx = np.asarray(inp["L1_attn_b_x"])
    bxc = np.zeros((128, 1), dtype=np.float32)
    bxc[0:64, 0] = bx[0:64]
    bxc[64:80, 0] = bx[64:80]
    bxc[96:112, 0] = bx[80:96]
    m["b_x_c"] = bxc
    m["c_ident"] = np.eye(128, dtype=ml_dtypes.bfloat16)
    m["c_triu"] = np.triu(np.ones((128, 128), dtype=np.float32)).astype(ml_dtypes.bfloat16)
    m["c_ones_b"] = np.ones((128, 128), dtype=ml_dtypes.bfloat16)
    m["c_ones_f"] = np.ones((128, 128), dtype=np.float32)
    return m


def kernel(**inp):
    nc = build_program(DEBUG_TAPS)
    in_maps = [host_inputs(inp, core) for core in range(NCORES)]
    res = run_bass_kernel_spmd(nc, in_maps, core_ids=list(range(NCORES)))
    outs = []
    for core in range(NCORES):
        o = res.results[core]["x_out"]
        outs.append(np.ascontiguousarray(o.transpose(0, 2, 1)))
    return np.concatenate(outs, axis=0).astype(np.float32)


if __name__ == "__main__":
    print("building program...")
    import time
    t0 = time.time()
    nc = build_program()
    print("built in %.1fs" % (time.time() - t0))
